# version 08
# baseline (speedup 1.0000x reference)
import contextlib
import numpy as np
import ml_dtypes
import concourse.bass as bass
import concourse.mybir as mybir
from concourse.bass_utils import run_bass_kernel_spmd

F32 = mybir.dt.float32
BF16 = mybir.dt.bfloat16
AF = mybir.ActivationFunctionType
ALU = mybir.AluOpType
AX = mybir.AxisListType

T = 2048
D = 1024
NF = 8
NT = 16
NB = 4
BLK = 512
MAGIC = float(np.float32(1.5 * 2 ** 23))
EPS = 1e-5
SCALE = 1.0 / 32.0
FF = 4096

BUILD_UPTO = 6
DEBUG_DUMPS = []

ENGINE_ATTR = {"g": "gpsimd", "y": "sync", "t": "tensor", "v": "vector", "s": "scalar"}
CHAINED = {"v": "V", "s": "S", "g": "G"}


class Prog:

    def __init__(self):
        self.items = {e: [] for e in ENGINE_ATTR}
        self.counts = {}
        self.labels = {e: [] for e in ENGINE_ATTR}
        self.cur_label = "init"
        self.last_tok = {e: None for e in ENGINE_ATTR}

    def label(self, s):
        self.cur_label = s

    def emit(self, eng, fn, inc=None, inc_n=1, waits=()):
        wl = [(s, v) for (s, v) in waits if v is not None and v > 0]
        if eng in CHAINED:
            sem = CHAINED[eng]
            if inc is None:
                inc, inc_n = sem, 1
            prev = self.last_tok[eng]
            if prev is not None:
                wl.append((sem, prev))
        self.items[eng].append((fn, inc, inc_n, wl))
        self.labels[eng].append(self.cur_label)
        tok = None
        if inc is not None:
            self.counts[inc] = self.counts.get(inc, 0) + inc_n
            tok = self.counts[inc]
        if eng in CHAINED and inc == CHAINED[eng]:
            self.last_tok[eng] = tok
        return tok

    def t(self, fn, inc="T", inc_n=1, waits=()):
        return self.emit("t", fn, inc, inc_n, waits)

    def v(self, fn, waits=()):
        return self.emit("v", fn, None, 1, waits)

    def s(self, fn, waits=()):
        return self.emit("s", fn, None, 1, waits)

    def g(self, fn, waits=()):
        return self.emit("g", fn, None, 1, waits)

    def y(self, fn, inc=None, inc_n=1, waits=()):
        return self.emit("y", fn, inc, inc_n, waits)

    def dma_g(self, fn, waits=()):
        return self.emit("g", fn, "DG", 16, waits)

    def dma_y(self, fn, waits=()):
        return self.emit("y", fn, "DY", 16, waits)

    def check(self):
        pos = {e: 0 for e in self.items}
        done = {}
        progress = True
        while progress:
            progress = False
            for e, lst in self.items.items():
                while pos[e] < len(lst):
                    fn, inc, inc_n, waits = lst[pos[e]]
                    if any(done.get(s, 0) < v for (s, v) in waits):
                        break
                    if inc:
                        done[inc] = done.get(inc, 0) + inc_n
                    pos[e] += 1
                    progress = True
        stuck = {e: pos[e] for e in pos if pos[e] < len(self.items[e])}
        if stuck:
            msg = []
            for e, p in stuck.items():
                fn, inc, inc_n, waits = self.items[e][p]
                lbl = self.labels[e][p]
                missing = [(s, v, done.get(s, 0)) for (s, v) in waits
                           if done.get(s, 0) < v]
                msg.append(f"engine {e} item {p} [{lbl}] waits {missing}")
            raise RuntimeError("Deadlock in recorded program:\n" + "\n".join(msg))

    def replay_into(self, nc, block, sems):
        self.check()
        for e, attr in ENGINE_ATTR.items():
            items = self.items[e]
            if not items:
                continue

            def make(items=items):
                def run(eng):
                    waited = {}
                    for (fn, inc, inc_n, waits) in items:
                        for (s_, v_) in waits:
                            if waited.get(s_, 0) >= v_:
                                continue
                            eng.wait_ge(sems[s_], v_)
                            waited[s_] = v_
                        instr = fn(eng)
                        if inc is not None:
                            instr.then_inc(sems[inc], inc_n)
                return run

            getattr(block, attr)(make())


class MemMap:
    def __init__(self, nc):
        self.nc = nc
        self.base = nc.sbuf_base
        self.top = nc.sbuf_top
        self.cur = int(np.ceil(self.base / 32) * 32)
        self.stack = []
        self.peak = self.cur

    def alloc(self, name, shape, dtype):
        bpp = int(np.prod(shape[1:])) * mybir.dt.size(dtype)
        bpp = int(np.ceil(bpp / 32) * 32)
        off = self.cur
        assert off + bpp <= self.top, (
            f"SBUF overflow: {name} needs {bpp}B at {off}, top={self.top}")
        self.cur = off + bpp
        self.peak = max(self.peak, self.cur)
        return self.nc.alloc_sbuf_tensor_at(name, list(shape), dtype, offset=off)

    def mark(self):
        self.stack.append(self.cur)

    def release(self):
        self.cur = self.stack.pop()


class Banks:

    def __init__(self):
        self.last_read = {}
        self.last_write = {}

    @staticmethod
    def of(tensor_is_b, col0, col1):
        b0 = 4 if tensor_is_b else 0
        return list(range(b0 + col0 // 512, b0 + (col1 + 511) // 512))

    def wr(self, banks):
        w = []
        for b in banks:
            if b in self.last_read:
                w.append(self.last_read[b])
        return w

    def rd(self, banks, sem, tok):
        for b in banks:
            self.last_read[b] = (sem, tok)


def build_program():
    nc = bass.Bass("TRN2", target_bir_lowering=False)

    xT = nc.dram_tensor("xT", [D, T], F32, kind="ExternalInput")
    dram = {}
    for nm, shape in [("wqT", [D, D]), ("wkT", [D, D]), ("wvT", [D, D]),
                      ("qwT", [D, D]), ("kwT", [D, D]), ("vwT", [D, D]),
                      ("owT", [D, D]), ("w1qT", [D, D]), ("w1kT", [D, D]),
                      ("w1vT", [D, D]), ("w1oT", [D, D]),
                      ("w1T", [D, FF]), ("w2T", [FF, D]),
                      ("nw_c", [128, NF]), ("nb_c", [128, NF]), ("qb_c", [128, NF]),
                      ("kb_c", [128, NF]), ("ob_c", [128, NF]), ("b2_c", [128, NF]),
                      ("b1_c", [128, 32]), ("vb_bc", [128, D]),
                      ("qg_bc", [128, 64]), ("qbeta_bc", [128, 64]),
                      ("kg_bc", [128, 64]), ("kbeta_bc", [128, 64])]:
        dram[nm] = nc.dram_tensor(nm, shape, F32, kind="ExternalInput")

    yOut = nc.dram_tensor("y", [128, NF, T], F32, kind="ExternalOutput")
    g_hbm = nc.dram_tensor("g_scr", [128, NF, T], BF16, kind="Internal")
    x1_hbm = nc.dram_tensor("x1_scr", [128, NF, T], F32, kind="Internal")
    xn1_hbm = nc.dram_tensor("xn1_scr", [128, NF, T], BF16, kind="Internal")
    s1_hbm = nc.dram_tensor("s1_scr", [128, NF, T], F32, kind="Internal")

    dbg_t = {}
    for nm in DEBUG_DUMPS:
        shp = [128, NF, T]
        dt_ = BF16 if nm in ("xn", "xn1", "g", "pt0") else F32
        dbg_t[nm] = nc.dram_tensor("dbg_" + nm, shp, dt_, kind="ExternalOutput")

    PSA = nc.alloc_psum_tensor("PSA", [128, 2048], F32)
    PSB = nc.alloc_psum_tensor("PSB", [128, 2048], F32)
    nc.psum_base -= 8
    PSAbf = nc.alloc_psum_tensor("PSAbf", [128, 4096], BF16)
    PSBbf = nc.alloc_psum_tensor("PSBbf", [128, 4096], BF16)

    mm = MemMap(nc)
    ident = mm.alloc("ident", [128, 128], BF16)
    maskA = mm.alloc("maskA", [128, 128], F32)
    maskMT = mm.alloc("maskMT", [128, 128], F32)
    ones_colb = mm.alloc("ones_colb", [128, 1], BF16)
    ones_row1f = mm.alloc("ones_row1f", [1, 128], F32)
    ones_colf = mm.alloc("ones_colf", [128, 128], F32)
    cmagic = mm.alloc("cmagic", [128, 1], F32)
    sv = {}
    for name in ["nw", "nb", "qb", "kb", "ob", "b2"]:
        sv[name] = mm.alloc("sv_" + name, [128, NF], F32)
    sv["b1"] = mm.alloc("sv_b1", [128, 32], F32)
    sv["qb32"] = mm.alloc("sv_qb32", [128, NF], F32)
    sv["vbbc"] = mm.alloc("sv_vbbc", [128, D], F32)
    for nm2 in ["qg", "qbe", "kg", "kbe"]:
        sv[nm2] = mm.alloc("sv_" + nm2, [128, 64], F32)
    w_acc = {"w1": mm.alloc("w1_acc", [128, 1], F32),
             "w2": mm.alloc("w2_acc", [128, 1], F32)}
    w_rs = {"w1": mm.alloc("w1_rs", [128, 1], F32),
            "w2": mm.alloc("w2_rs", [128, 1], F32)}
    tern_tmp = mm.alloc("tern_tmp", [128, 2], F32)
    strow = mm.alloc("strow", [1, 5 * BLK], F32)
    scr = mm.alloc("scr", [128, 512], F32)
    negmax = mm.alloc("negmax", [128, 1], F32)
    sums = mm.alloc("sums", [128, 1], F32)
    recs = mm.alloc("recs", [128, 16], F32)

    mm.mark()
    xn = mm.alloc("xn", [128, NF * T], BF16)

    P = Prog()
    BK = Banks()

    P.label("P0.init")
    P.g(lambda g: g.memset(ident[:, :], 0.0))
    P.g(lambda g: g.affine_select(
        out=ident[:, :], in_=ident[:, :], compare_op=ALU.not_equal, fill=1.0,
        base=0, pattern=[[-1, 128]], channel_multiplier=1))
    P.g(lambda g: g.memset(maskA[:, :], 0.0))
    P.g(lambda g: g.affine_select(
        out=maskA[:, :], in_=maskA[:, :], compare_op=ALU.is_ge, fill=-1e9,
        base=0, pattern=[[-1, 128]], channel_multiplier=1))
    P.g(lambda g: g.memset(maskMT[:, :], 1.0))
    P.g(lambda g: g.affine_select(
        out=maskMT[:, :], in_=maskMT[:, :], compare_op=ALU.is_ge, fill=0.0,
        base=0, pattern=[[1, 128]], channel_multiplier=-1))
    P.g(lambda g: g.memset(ones_colb[:, :], 1.0))
    P.g(lambda g: g.memset(ones_row1f[:, :], 1.0))
    P.g(lambda g: g.memset(ones_colf[:, :], 1.0))
    P.g(lambda g: g.memset(w_acc["w1"][:, :], 0.0))
    P.g(lambda g: g.memset(w_acc["w2"][:, :], 0.0))
    tok_init_g = P.g(lambda g: g.memset(cmagic[:, :], MAGIC))

    tok_sv = None
    for (dstn, srcn) in [("nw", "nw_c"), ("nb", "nb_c"), ("qb", "qb_c"),
                         ("kb", "kb_c"), ("ob", "ob_c"), ("b2", "b2_c"),
                         ("b1", "b1_c"), ("vbbc", "vb_bc"),
                         ("qg", "qg_bc"), ("qbe", "qbeta_bc"),
                         ("kg", "kg_bc"), ("kbe", "kbeta_bc")]:
        tok_sv = P.dma_y(lambda y, d=sv[dstn], s_=dram[srcn]: y.dma_start(d[:, :], s_[:, :]))
    tok_qb32 = P.v(lambda v: v.tensor_scalar_mul(sv["qb32"][:, :], sv["qb"][:, :], SCALE),
                   waits=[("DY", tok_sv)])

    def emit_ln(phase, src_kind, dst_write, src_hbm, xblk, xbf, xsq,
                apply_chunks=None, pre_tok=None, first_load_waits=()):
        last_ap = None
        mmlast_prev = None
        finB_prev = None
        apdone = {}
        xtok = {}
        for b in range(NB):
            t0 = b * BLK
            P.label(f"{phase}.b{b}")
            if src_kind == "hbm_f32":
                w_l = list(first_load_waits) if b <= 1 else []
                if b >= 2 and apdone.get(b - 2):
                    w_l.append(("V", apdone[b - 2]))
                if src_hbm is xT:
                    src_ap = xT[:, t0:t0 + BLK].rearrange("(c p) t -> p c t", p=128)
                else:
                    src_ap = src_hbm[:, :, t0:t0 + BLK]
                xtok[b] = P.dma_y(lambda y, src_ap=src_ap, b=b: y.dma_start(
                    xblk[b % 2][:, :].rearrange("p (c t) -> p c t", c=NF), src_ap),
                    waits=w_l)

            tok_mm_last = None
            nsq = xsq.shape[1] // BLK
            mm2_toks = {}
            for c in range(NF):
                csl = c % nsq
                if src_kind == "hbm_f32":
                    xs_f = xblk[b % 2][:, c * BLK:(c + 1) * BLK]
                    wsrc = [("DY", xtok[b])]
                    wreuse = [("T", mmlast_prev)] if mmlast_prev else []
                    tok_cast = P.v(lambda v, xs_f=xs_f, c=c: v.tensor_copy(
                        xbf[:, c * BLK:(c + 1) * BLK], xs_f), waits=wsrc + wreuse)
                    sx_src = xbf[:, c * BLK:(c + 1) * BLK]
                    sx_wait = [("V", tok_cast)]
                else:
                    xs_f = xn[:, c * T + t0: c * T + t0 + BLK]
                    wsrc = [("V", pre_tok)] if pre_tok else []
                    wreuse = [("T", mmlast_prev)] if mmlast_prev else []
                    sx_src = xs_f
                    sx_wait = list(wsrc)
                wsq = [("T", mm2_toks[c - nsq])] if (c - nsq) in mm2_toks else wreuse
                tok_sq = P.s(lambda s, xs_f=xs_f, csl=csl: s.activation(
                    xsq[:, csl * BLK:(csl + 1) * BLK], xs_f, AF.Square),
                    waits=wsrc + wsq)
                bw = BK.wr([4]) if c == 0 else []
                tok_mm1 = P.t(lambda t, sx_src=sx_src, c=c: t.matmul(
                    PSB[0:1, 0:BLK], ones_colb[:, :], sx_src,
                    start=(c == 0), stop=(c == NF - 1)),
                    waits=sx_wait + bw + ([("G", tok_init_g)] if b == 0 else []))
                bw2 = BK.wr([5]) if c == 0 else []
                tok_mm2 = P.t(lambda t, csl=csl, c=c: t.matmul(
                    PSB[0:1, BLK:2 * BLK], ones_colb[:, :],
                    xsq[:, csl * BLK:(csl + 1) * BLK],
                    start=(c == 0), stop=(c == NF - 1)),
                    waits=[("S", tok_sq)] + bw2)
                mm2_toks[c] = tok_mm2
                tok_mm_last = tok_mm2
            mmlast_prev = tok_mm_last

            mu = strow[0:1, 0:BLK]
            m2e = strow[0:1, BLK:2 * BLK]
            musq = strow[0:1, 2 * BLK:3 * BLK]
            varr = strow[0:1, BLK:2 * BLK]
            stdr = strow[0:1, 2 * BLK:3 * BLK]
            arow = strow[0:1, 3 * BLK:4 * BLK]
            brow = strow[0:1, 4 * BLK:5 * BLK]
            wrow = [("T", finB_prev)] if finB_prev else []
            tk1 = P.v(lambda v: v.tensor_scalar_mul(mu, PSB[0:1, 0:BLK], 1.0 / D),
                      waits=[("T", tok_mm_last)] + wrow)
            tk2 = P.v(lambda v: v.tensor_scalar(m2e, PSB[0:1, BLK:2 * BLK],
                                                1.0 / D, EPS, op0=ALU.mult, op1=ALU.add))
            tk3 = P.v(lambda v: v.tensor_mul(musq, mu, mu))
            tk4 = P.v(lambda v: v.tensor_sub(varr, m2e, musq))
            tk5 = P.s(lambda s: s.activation(stdr, varr, AF.Sqrt), waits=[("V", tk4)])
            tk6 = P.v(lambda v: v.reciprocal(arow, stdr), waits=[("S", tk5)])
            tk7 = P.v(lambda v: v.scalar_tensor_tensor(brow, mu, -1.0, arow,
                                                       op0=ALU.mult, op1=ALU.mult))
            BK.rd([4, 5], "V", tk2)

            tokA = P.t(lambda t: t.matmul(PSB[:, 2 * BLK:3 * BLK], ones_row1f[:, :],
                                          arow, start=True, stop=True),
                       waits=[("V", tk7)] + BK.wr([6]))
            tokB = P.t(lambda t: t.matmul(PSB[:, 3 * BLK:4 * BLK], ones_row1f[:, :],
                                          brow, start=True, stop=True),
                       waits=[("V", tk7)] + BK.wr([7]))
            finB_prev = tokB

            nchunks = NF if apply_chunks is None else apply_chunks[b]
            tok_ap = None
            for c in range(nchunks):
                if src_kind == "hbm_f32":
                    xs_f = xblk[b % 2][:, c * BLK:(c + 1) * BLK]
                    w_ap = [("DY", xtok[b])]
                else:
                    xs_f = xn[:, c * T + t0: c * T + t0 + BLK]
                    w_ap = []
                tka = P.v(lambda v, xs_f=xs_f: v.tensor_mul(
                    scr[:, :], xs_f, PSB[:, 2 * BLK:3 * BLK]),
                    waits=[("T", tokA)] + w_ap)
                tok_ap = dst_write(b, c, tokB)
            BK.rd([6, 7], "V", tok_ap)
            apdone[b] = tok_ap
        return apdone[NB - 1]

    mm.mark()
    xblk = [mm.alloc(f"xblk{i}", [128, NF * BLK], F32) for i in range(2)]
    xbf = mm.alloc("xbf", [128, NF * BLK], BF16)
    xsq = mm.alloc("xsq", [128, NF * BLK], BF16)

    def ln0_write(b, c, tokB):
        t0 = b * BLK
        return P.v(lambda v: v.tensor_add(
            xn[:, c * T + t0: c * T + t0 + BLK], scr[:, :], PSB[:, 3 * BLK:4 * BLK]),
            waits=[("T", tokB)])

    P.label("P1.ln0")
    tok_ln0 = emit_ln("P1", "hbm_f32", ln0_write, xT, xblk, xbf, xsq)

    wstage = [mm.alloc(f"wstage{i}", [128, 4096], F32) for i in range(2)]
    P.label("P1.absmean")
    abs_done = {}
    piece_toks = []
    for (mat, hbm) in [("w1", dram["w1T"]), ("w2", dram["w2T"])]:
        for i in range(8):
            slot = wstage[len(piece_toks) % 2]
            if mat == "w1":
                src = hbm[i * 128:(i + 1) * 128, :]
            else:
                src = hbm[i * 512:(i + 1) * 512, :].rearrange("(c p) f -> p c f", p=128)
            wait_prev = [("V", piece_toks[-2])] if len(piece_toks) >= 2 else []
            dst = slot[:, :] if mat == "w1" else slot[:, :].rearrange(
                "p (c f) -> p c f", c=4)
            tok_l = P.dma_y(lambda y, dst=dst, src=src: y.dma_start(dst, src),
                            waits=wait_prev)
            P.v(lambda v, slot=slot: v.reduce_sum(
                tern_tmp[:, 0:1], slot[:, :], axis=AX.X, apply_absolute_value=True),
                waits=[("DY", tok_l)])
            tok_a = P.v(lambda v, acc=w_acc[mat]: v.tensor_add(
                acc[:, :], acc[:, :], tern_tmp[:, 0:1]))
            piece_toks.append(tok_a)
        abs_done[mat] = piece_toks[-1]
    tok_rs = {}
    for mat, nelem in [("w1", D * FF), ("w2", D * FF)]:
        tok_t = P.t(lambda t, acc=w_acc[mat]: t.matmul(
            PSB[:, 0:1], ones_colf[:, :], acc[:, :], start=True, stop=True),
            waits=[("V", abs_done[mat]), ("V", tok_ln0)] + BK.wr([4]))
        tk1 = P.v(lambda v, rs=w_rs[mat], nelem=nelem: v.tensor_scalar(
            rs[:, :], PSB[:, 0:1], 1.0 / nelem, 1e-8, op0=ALU.mult, op1=ALU.max),
            waits=[("T", tok_t)])
        tok_rs[mat] = P.v(lambda v, rs=w_rs[mat]: v.reciprocal(rs[:, :], rs[:, :]))
        BK.rd([4], "V", tk1)
    mm.release()

    def dump(nm, sb, tok, tok_sem="V"):
        if nm not in dbg_t:
            return
        for c in range(NF):
            P.dma_y(lambda y, c=c: y.dma_start(dbg_t[nm][:, c, :],
                                               sb[:, c * T:(c + 1) * T]),
                    waits=[(tok_sem, tok)])

    dump("xn", xn, tok_ln0)

    XB = [0, 8, 12, 14]
    NCF = [8, 4, 2, 1]

    def phase2():
        mm.mark()
        q2 = mm.alloc("q2", [128, NF * T], BF16)
        k2 = mm.alloc("k2", [128, NF * T], BF16)
        v2 = mm.alloc("v2", [128, NT * 1024], BF16)
        wringA = mm.alloc("wringA", [128, NF * 1024], BF16)
        wringB = mm.alloc("wringB", [128, NF * 1024], BF16)

        mm.mark()
        xin = mm.alloc("xin", [128, 15 * BLK], BF16)
        xsq2 = mm.alloc("xsq2", [128, 2 * BLK], BF16)
        scr2 = mm.alloc("scr2", [128, 512], F32)

        sq_rot = {}

        def lni_write(b, c, tokB):
            xoff = (XB[b] + c) * BLK
            P.v(lambda v: v.tensor_add(scr2[:, :], scr[:, :],
                                       PSB[:, 3 * BLK:4 * BLK]),
                waits=[("T", tokB)])
            return P.v(lambda v, xoff=xoff, c=c: v.tensor_scalar(
                xin[:, xoff:xoff + BLK], scr2[:, :],
                sv["nw"][:, c:c + 1], sv["nb"][:, c:c + 1],
                op0=ALU.mult, op1=ALU.add))

        P.label("P2a.lni")
        tok_xin = emit_ln("P2a", "xn_bf16", lni_write, None, None, None, xsq2,
                          apply_chunks=NCF, pre_tok=tok_ln0)

        def wload(dst, src, waits=()):
            return P.dma_g(lambda g: g.dma_start(
                dst[:, :].rearrange("p (c f) -> p c f", c=NF),
                src[:, :].rearrange("(c p) f -> p c f", p=128)), waits=waits)

        P.label("P2b.wload")
        tok_qw = wload(wringA, dram["qwT"])
        tok_kw = wload(wringB, dram["kwT"])

        P.label("P2b.proj")

        def proj_fm(dst, wring, tok_w, bias_cols, scale, extra_wait):
            last_mm = None
            tok_e = None
            si = 0
            for e in range(4):
                for oc in range(NF):
                    sl = si % 4
                    si += 1
                    ps = PSA[:, sl * 512:(sl + 1) * 512]
                    w0 = [("DG", tok_w), ("V", tok_xin)] + extra_wait + BK.wr([sl])
                    ncf = NCF[e]
                    for ci in range(ncf):
                        last_mm = P.t(lambda t, ci=ci, oc=oc, e=e, ps=ps, wring=wring,
                                      ncf=ncf: t.matmul(
                            ps, wring[:, ci * 1024 + oc * 128: ci * 1024 + oc * 128 + 128],
                            xin[:, (XB[e] + ci) * BLK:(XB[e] + ci + 1) * BLK],
                            start=(ci == 0), stop=(ci == ncf - 1)),
                            waits=w0 if ci == 0 else [])
                    tok_e = P.s(lambda s, e=e, oc=oc, ps=ps: s.activation(
                        dst[:, oc * T + e * BLK: oc * T + (e + 1) * BLK], ps, AF.Identity,
                        bias=bias_cols[:, oc:oc + 1], scale=scale),
                        waits=[("T", last_mm)])
                    BK.rd([sl], "S", tok_e)
            return last_mm, tok_e

        tok_q2mm, tok_q2ev = proj_fm(q2, wringA, tok_qw, sv["qb32"], SCALE, [])
        tok_k2mm, tok_k2ev = proj_fm(k2, wringB, tok_kw, sv["kb"], 1.0, [])

        tok_vw = wload(wringA, dram["vwT"], waits=[("T", tok_q2mm)])
        tok_ow = wload(wringB, dram["owT"], waits=[("T", tok_k2mm)])

        last_vmm = None
        vslot_tok = [None, None]
        for e in range(4):
            ncf = NCF[e]
            for tl in range(4):
                tt = e * 4 + tl
                sl = tt % 2
                ps = PSA[:, sl * 1024:(sl + 1) * 1024]
                w0 = [("DG", tok_vw), ("V", tok_xin)]
                if vslot_tok[sl] is not None:
                    w0 += [("V", vslot_tok[sl])]
                w0 += BK.wr([2 * sl, 2 * sl + 1])
                for ci in range(ncf):
                    for nb2 in range(2):
                        last_vmm = P.t(lambda t, ci=ci, e=e, tl=tl, nb2=nb2, ps=ps,
                                       ncf=ncf: t.matmul(
                            ps[:, nb2 * 512:(nb2 + 1) * 512],
                            xin[:, (XB[e] + ci) * BLK + tl * 128:
                                (XB[e] + ci) * BLK + tl * 128 + 128],
                            wringA[:, ci * 1024 + nb2 * 512: ci * 1024 + (nb2 + 1) * 512],
                            start=(ci == 0), stop=(ci == ncf - 1)),
                            waits=w0 if (ci == 0 and nb2 == 0) else [])
                tok_ev = P.v(lambda v, tt=tt, ps=ps: v.tensor_add(
                    v2[:, tt * 1024:(tt + 1) * 1024], ps, sv["vbbc"][:, :]),
                    waits=[("T", last_vmm)])
                vslot_tok[sl] = tok_ev
                BK.rd([2 * sl, 2 * sl + 1], "V", tok_ev)
        tok_v2mm = last_vmm
        mm.release()

        mm.mark()
        P_sb2 = mm.alloc("P_sb2", [128, 3 * T], BF16)
        aotm = mm.alloc("aotm", [128, 1024], BF16)
        aofm = mm.alloc("aofm", [128, NF * 512], BF16)
        g_blk = mm.alloc("g_blk", [128, 2 * 512], BF16)
        nm4 = mm.alloc("nm4", [128, 3 * 4], F32)
        nmx = mm.alloc("nmx", [128, 3], F32)
        sum2 = mm.alloc("sum2", [128, 3], F32)

        tok_gdma = {}
        tok_aofm = None
        exp_of = {}
        recip_of = {}
        ao_last = {}
        prev_outproj_mm = None
        sil_of = {}

        def stage_scores(gi):
            qt, h = divmod(gi, NF)
            w_sc = BK.wr([0, 1, 2, 3])
            if gi == 0:
                w_sc += [("S", tok_q2ev), ("S", tok_k2ev)]
            tok_mm = None
            for j in range(4):
                tok_mm = P.t(lambda t, h=h, qt=qt, j=j: t.matmul(
                    PSA[:, j * 512:(j + 1) * 512],
                    q2[:, h * T + qt * 128: h * T + qt * 128 + 128],
                    k2[:, h * T + j * 512: h * T + (j + 1) * 512],
                    start=True, stop=True),
                    waits=w_sc if j == 0 else [])
            return tok_mm

        def stage_a(gi, tok_mm):
            qt, h = divmod(gi, NF)
            sl = gi % 3
            rq = (qt % 2) * NF
            w_r = [("T", tok_mm)]
            if exp_of.get(gi - 3) is not None:
                w_r += [("S", exp_of[gi - 3])]
            tok_c = P.v(lambda v, sl=sl: v.reduce_max(
                nmx[:, sl:sl + 1], PSA[:, :], axis=AX.X, negate=True),
                waits=w_r)
            w_exp = [("V", tok_c)]
            if (gi - 3) in ao_last:
                w_exp += [("T", ao_last[gi - 3])]
            if (gi - 3) in recip_of:
                w_exp += [("V", recip_of[gi - 3])]
            if gi == 0:
                w_exp += [("T", tok_v2mm)]
            tok_exp = P.s(lambda s, sl=sl: s.activation(
                P_sb2[:, sl * T:(sl + 1) * T], PSA[:, :], AF.Exp,
                bias=nmx[:, sl:sl + 1], accum_out=sum2[:, sl:sl + 1]),
                waits=w_exp)
            exp_of[gi] = tok_exp
            BK.rd([0, 1, 2, 3], "S", tok_exp)
            recip_of[gi] = P.v(lambda v, h=h, sl=sl, rq=rq: v.reciprocal(
                recs[:, rq + h: rq + h + 1], sum2[:, sl:sl + 1]),
                waits=[("S", tok_exp)])

        def stage_b1(gi):
            qt, h = divmod(gi, NF)
            sl = gi % 3
            base = sl * T
            tok_wave = {}
            for w in range(2):
                last_tr = None
                for j in range(8):
                    kt = w * 8 + j
                    w_tr = (BK.wr([4 + w]) + [("S", exp_of[gi])]) if j == 0 else []
                    last_tr = P.t(lambda t, kt=kt, j=j, w=w, base=base: t.transpose(
                        PSBbf[:, w * 1024 + j * 128: w * 1024 + (j + 1) * 128],
                        P_sb2[:, base + kt * 128: base + (kt + 1) * 128],
                        ident[:, :]),
                        waits=w_tr)
                w_ev = [("T", last_tr)]
                if w == 0:
                    tok_wave[w] = P.s(lambda s, w=w, base=base: s.copy(
                        P_sb2[:, base + w * 1024: base + (w + 1) * 1024],
                        PSBbf[:, w * 1024:(w + 1) * 1024]),
                        waits=w_ev)
                    BK.rd([4 + w], "S", tok_wave[w])
                else:
                    tok_wave[w] = P.v(lambda v, w=w, base=base: v.tensor_copy(
                        P_sb2[:, base + w * 1024: base + (w + 1) * 1024],
                        PSBbf[:, w * 1024:(w + 1) * 1024]),
                        waits=w_ev)
                    BK.rd([4 + w], "V", tok_wave[w])
            return tok_wave

        def stage_b2(gi, tok_wave):
            qt, h = divmod(gi, NF)
            sl = gi % 3
            base = sl * T
            for kt in range(NT):
                w_ao = []
                if kt == 0:
                    w_ao = [("V", tok_wave[1]), ("S", tok_wave[0])] + \
                        (BK.wr([6, 7]) if h == 0 else [])
                ao_last[gi] = P.t(lambda t, kt=kt, h=h, base=base: t.matmul(
                    PSB[:, 1024 + h * 128: 1024 + (h + 1) * 128],
                    P_sb2[:, base + kt * 128: base + (kt + 1) * 128],
                    v2[:, kt * 1024 + h * 128: kt * 1024 + (h + 1) * 128],
                    start=(kt == 0), stop=(kt == NT - 1)),
                    waits=w_ao)

        def epilogue(qt):
            nonlocal tok_aofm, prev_outproj_mm
            P.label(f"P2c.epi{qt}")
            rq = (qt % 2) * NF
            tok_ev = None
            for h in range(NF):
                w_ev = [("T", ao_last[qt * NF + NF - 1])] if h == 0 else []
                tok_ev = P.v(lambda v, h=h, rq=rq: v.tensor_scalar(
                    aotm[:, h * 128:(h + 1) * 128],
                    PSB[:, 1024 + h * 128: 1024 + (h + 1) * 128],
                    recs[:, rq + h: rq + h + 1], None, op0=ALU.mult),
                    waits=w_ev)
            BK.rd([6, 7], "V", tok_ev)
            last_tr = None
            for fc in range(NF):
                w_tr = (BK.wr([4]) + [("V", tok_ev)]) if fc == 0 else []
                last_tr = P.t(lambda t, fc=fc: t.transpose(
                    PSBbf[:, fc * 128:(fc + 1) * 128],
                    aotm[:, fc * 128:(fc + 1) * 128],
                    ident[:, :]), waits=w_tr)
            j = qt % 4
            w_aofm = [("T", last_tr)]
            if j == 0 and prev_outproj_mm is not None:
                w_aofm += [("T", prev_outproj_mm)]
            tok_aofm = P.v(lambda v, j=j: v.tensor_copy(
                bass.AP(aofm, j * 128, [[NF * 512, 128], [512, NF], [1, 128]]),
                bass.AP(PSBbf, 0, [[4096, 128], [128, NF], [1, 128]])),
                waits=w_aofm)
            BK.rd([4], "V", tok_aofm)

        def emit_outproj(eb):
            nonlocal prev_outproj_mm
            mcnt = NCF[eb]
            P.label(f"P2d.blk{eb}")
            tok_sil = None
            for mh in range(NF):
                slg = mh % 2
                gk = (eb, mh - 2) if mh >= 2 else (eb - 1, mh + 6)
                wg = [("DY", tok_gdma[gk])] if gk in tok_gdma else []
                if mh < mcnt:
                    sl = mh % 4
                    ps = PSA[:, sl * 512:(sl + 1) * 512]
                    w0 = [("V", tok_aofm), ("DG", tok_ow)] + BK.wr([sl])
                    last_mm = None
                    for ci in range(mcnt):
                        last_mm = P.t(lambda t, ci=ci, mh=mh, ps=ps,
                                      mcnt=mcnt: t.matmul(
                            ps, wringB[:, ci * 1024 + mh * 128: ci * 1024 + mh * 128 + 128],
                            aofm[:, ci * 512:(ci + 1) * 512],
                            start=(ci == 0), stop=(ci == mcnt - 1)),
                            waits=w0 if ci == 0 else [])
                    prev_outproj_mm = last_mm
                    tok_t = P.v(lambda v, mh=mh, eb=eb, ps=ps: v.tensor_add(
                        scr[:, :], ps,
                        xn[:, mh * T + eb * BLK: mh * T + (eb + 1) * BLK]),
                        waits=[("T", last_mm)] +
                              ([("S", tok_sil)] if tok_sil else []))
                    BK.rd([sl], "V", tok_t)
                    tok_sil = P.s(lambda s, mh=mh, slg=slg: s.activation(
                        g_blk[:, slg * 512:(slg + 1) * 512], scr[:, :], AF.Silu,
                        bias=sv["ob"][:, mh:mh + 1]),
                        waits=[("V", tok_t)] + wg)
                else:
                    tok_sil = P.s(lambda s, mh=mh, eb=eb, slg=slg: s.activation(
                        g_blk[:, slg * 512:(slg + 1) * 512],
                        xn[:, mh * T + eb * BLK: mh * T + (eb + 1) * BLK], AF.Silu),
                        waits=wg)
                tok_gdma[(eb, mh)] = P.dma_y(lambda y, eb=eb, mh=mh, slg=slg:
                    y.dma_start(g_hbm[:, mh, eb * BLK:(eb + 1) * BLK],
                                g_blk[:, slg * 512:(slg + 1) * 512]),
                    waits=[("S", tok_sil)])

        def after_b(gj):
            qtj, hj = divmod(gj, NF)
            if hj == NF - 1:
                epilogue(qtj)
                if qtj % 4 == 3:
                    emit_outproj(qtj // 4)

        for qt in range(NT):
            P.label(f"P2c.qt{qt}")
            for h in range(NF):
                gi = qt * NF + h
                if gi >= 2:
                    tw = stage_b1(gi - 2)
                    tok_mm = stage_scores(gi)
                    stage_b2(gi - 2, tw)
                    after_b(gi - 2)
                    stage_a(gi, tok_mm)
                else:
                    tok_mm = stage_scores(gi)
                    stage_a(gi, tok_mm)
        for gj in (NT * NF - 2, NT * NF - 1):
            tw = stage_b1(gj)
            stage_b2(gj, tw)
            after_b(gj)

        mm.release()
        mm.release()
        return tok_gdma[(NB - 1, NF - 1)]

    tok_p2 = None
    if BUILD_UPTO >= 2:
        tok_p2 = phase2()
        if "g" in dbg_t:
            for c in range(NF):
                P.dma_y(lambda y, c=c: y.dma_start(
                    dbg_t["g"][:, c, :], g_hbm[:, c, :]), waits=[("DY", tok_p2)])

    def phase3(tok_g_last):
        mm.mark()
        Qf = mm.alloc("Qf", [128, NF * T], BF16)
        Kf = mm.alloc("Kf", [128, NF * T], BF16)
        Vt = mm.alloc("Vt", [128, NT * 1024], BF16)

        mm.mark()
        wringA = mm.alloc("wringA3", [128, NF * 1024], BF16)
        wringB = mm.alloc("wringB3", [128, NF * 1024], BF16)

        def wload(dst, src, waits=()):
            return P.dma_g(lambda g: g.dma_start(
                dst[:, :].rearrange("p (c f) -> p c f", c=NF),
                src[:, :].rearrange("(c p) f -> p c f", p=128)), waits=waits)

        P.label("P3a.proj")
        w_alias = [("DY", tok_g_last)] if tok_g_last else []
        tok_wq = wload(wringA, dram["wqT"], waits=w_alias)
        tok_wk = wload(wringB, dram["wkT"], waits=w_alias)

        def proj_full_fm(dst, wring, tok_w, scale):
            last_mm = None
            tok_e = None
            si = 0
            for oc in range(NF):
                for nb in range(NB):
                    sl = si % 4
                    si += 1
                    ps = PSA[:, sl * 512:(sl + 1) * 512]
                    w0 = [("DG", tok_w), ("V", tok_ln0)] + BK.wr([sl])
                    for ci in range(NF):
                        last_mm = P.t(lambda t, ci=ci, oc=oc, nb=nb, ps=ps,
                                      wring=wring: t.matmul(
                            ps, wring[:, ci * 1024 + oc * 128: ci * 1024 + oc * 128 + 128],
                            xn[:, ci * T + nb * BLK: ci * T + (nb + 1) * BLK],
                            start=(ci == 0), stop=(ci == NF - 1)),
                            waits=w0 if ci == 0 else [])
                    tok_e = P.s(lambda s, oc=oc, nb=nb, ps=ps: s.activation(
                        dst[:, oc * T + nb * BLK: oc * T + (nb + 1) * BLK], ps,
                        AF.Identity, scale=scale),
                        waits=[("T", last_mm)])
                    BK.rd([sl], "S", tok_e)
            return last_mm, tok_e

        tok_qmm, tok_qev = proj_full_fm(Qf, wringA, tok_wq, SCALE)
        tok_kmm, tok_kev = proj_full_fm(Kf, wringB, tok_wk, 1.0)

        tok_wv = wload(wringA, dram["wvT"], waits=[("T", tok_qmm)])
        last_vmm = None
        tok_vev = None
        for tt in range(NT):
            sl = tt % 2
            ps = PSA[:, sl * 1024:(sl + 1) * 1024]
            w0 = [("DG", tok_wv), ("V", tok_ln0)] + BK.wr([2 * sl, 2 * sl + 1])
            for ci in range(NF):
                for nb2 in range(2):
                    last_vmm = P.t(lambda t, ci=ci, tt=tt, nb2=nb2, ps=ps: t.matmul(
                        ps[:, nb2 * 512:(nb2 + 1) * 512],
                        xn[:, ci * T + tt * 128: ci * T + tt * 128 + 128],
                        wringA[:, ci * 1024 + nb2 * 512: ci * 1024 + (nb2 + 1) * 512],
                        start=(ci == 0), stop=(ci == NF - 1)),
                        waits=w0 if (ci == 0 and nb2 == 0) else [])
            tok_vev = P.v(lambda v, tt=tt, ps=ps: v.tensor_copy(
                Vt[:, tt * 1024:(tt + 1) * 1024], ps),
                waits=[("T", last_vmm)])
            BK.rd([2 * sl, 2 * sl + 1], "V", tok_vev)
        tok_vtmm = last_vmm
        mm.release()

        mm.mark()
        P_sb3 = mm.alloc("P_sb3", [128, T], BF16)
        PT_blk = mm.alloc("PT_blk", [128, NT * 512], BF16)
        x_blk3 = mm.alloc("x_blk3", [128, NF * 512], F32)
        g_rblk = mm.alloc("g_rblk", [128, NF * 512], BF16)
        x1sl = mm.alloc("x1sl", [128, 2 * 512], F32)
        rec1 = mm.alloc("rec1", [128, 1], F32)

        prev_exp = P.counts.get("S", 0)
        prev_recip = None
        prev_norm = None
        prev_lasttr = None
        prev_av = {}
        tok_x1dma = {}
        for b in range(NB):
            P.label(f"P3b.blk{b}")
            if b == 0:
                w_x = [("T", tok_vtmm)]
                w_g = [("T", tok_vtmm)]
            else:
                w_x = [("V", prev_x1_used)]
                w_g = [("V", prev_g_used)]
            tok_xb = P.dma_y(lambda y, b=b: y.dma_start(
                x_blk3[:, :].rearrange("p (c t) -> p c t", c=NF),
                xT[:, b * BLK:(b + 1) * BLK].rearrange("(c p) t -> p c t", p=128)),
                waits=w_x)
            tok_gb = P.dma_y(lambda y, b=b: y.dma_start(
                g_rblk[:, :].rearrange("p (c t) -> p c t", c=NF),
                g_hbm[:, :, b * BLK:(b + 1) * BLK]),
                waits=([("DY", tok_g_last)] if tok_g_last else []) + w_g)
            tok_z = None
            for kt in range(4 * b, 4 * b + 4):
                w_z = []
                if b > 0 and kt == 4 * b:
                    w_z = [("T", prev_av.get(b - 1))]
                tok_z = P.v(lambda v, kt=kt: v.memset(
                    PT_blk[:, kt * 512:(kt + 1) * 512], 0.0), waits=w_z)

            for j in range(4):
                qt = 4 * b + j
                nkt = qt + 1
                ncols = nkt * 128
                nseg = (ncols + 511) // 512
                for sg in range(nseg):
                    wdt = min(512, ncols - sg * 512)
                    w_sc = BK.wr([sg])
                    if qt == 0:
                        w_sc += [("S", tok_qev), ("S", tok_kev)]
                    for ci in range(NF):
                        last_sc = P.t(lambda t, ci=ci, qt=qt, sg=sg, wdt=wdt: t.matmul(
                            PSA[:, sg * 512: sg * 512 + wdt],
                            Qf[:, ci * T + qt * 128: ci * T + qt * 128 + 128],
                            Kf[:, ci * T + sg * 512: ci * T + sg * 512 + wdt],
                            start=(ci == 0), stop=(ci == NF - 1)),
                            waits=w_sc if ci == 0 else [])
                tok_mask = P.v(lambda v, ncols=ncols: v.tensor_add(
                    PSA[:, ncols - 128:ncols], PSA[:, ncols - 128:ncols], maskA[:, :]),
                    waits=[("T", last_sc)])
                tok_nm = P.v(lambda v, ncols=ncols: v.reduce_max(
                    negmax[:, :], PSA[:, 0:ncols], axis=AX.X, negate=True),
                    waits=([("S", prev_exp)] if prev_exp else []))
                w_exp = [("V", tok_nm)]
                if prev_norm:
                    w_exp += [("V", prev_norm)]
                if prev_lasttr:
                    w_exp += [("T", prev_lasttr)]
                tok_exp = P.s(lambda s, ncols=ncols: s.activation(
                    P_sb3[:, 0:ncols], PSA[:, 0:ncols], AF.Exp, bias=negmax[:, :],
                    accum_out=sums[:, :]), waits=w_exp)
                prev_exp = tok_exp
                BK.rd(list(range(nseg)), "S", tok_exp)
                prev_recip = P.v(lambda v: v.reciprocal(rec1[:, :], sums[:, :]),
                                 waits=[("S", tok_exp)])
                prev_norm = P.v(lambda v, ncols=ncols: v.tensor_scalar(
                    P_sb3[:, 0:ncols], P_sb3[:, 0:ncols], rec1[:, :], None,
                    op0=ALU.mult))
                nw = (nkt + 7) // 8
                for w in range(nw):
                    cnt = min(8, nkt - w * 8)
                    last_tr = None
                    for jj in range(cnt):
                        kt = w * 8 + jj
                        w_tr = (BK.wr([4 + (w % 2)]) + [("V", prev_norm)]) if jj == 0 else []
                        last_tr = P.t(lambda t, kt=kt, jj=jj, w=w: t.transpose(
                            PSBbf[:, (w % 2) * 1024 + jj * 128: (w % 2) * 1024 + (jj + 1) * 128],
                            P_sb3[:, kt * 128:(kt + 1) * 128], ident[:, :]),
                            waits=w_tr)
                    prev_lasttr = last_tr
                    tok_evw = P.v(lambda v, w=w, cnt=cnt, j=j: v.tensor_copy(
                        bass.AP(PT_blk, (w * 8) * 512 + j * 128,
                                [[NT * 512, 128], [512, cnt], [1, 128]]),
                        bass.AP(PSBbf, (w % 2) * 1024,
                                [[4096, 128], [128, cnt], [1, 128]])),
                        waits=[("T", last_tr)])
                    BK.rd([4 + (w % 2)], "V", tok_evw)

            if "pt0" in dbg_t and b == 0:
                P.dma_y(lambda y: y.dma_start(dbg_t["pt0"][:, 0, 0:2048],
                                              PT_blk[:, 0:2048]),
                        waits=[("V", tok_evw)])
                P.dma_y(lambda y: y.dma_start(dbg_t["pt0"][:, 1, 0:512],
                                              P_sb3[:, 0:512]),
                        waits=[("V", tok_evw)])

            P.label(f"P3c.blk{b}")
            nkt_b = 4 * b + 4
            tok_ps_done = {}
            for fc in range(NF):
                bank = fc
                psv = (PSA if fc < 4 else PSB)[:, (fc % 4) * 512:(fc % 4 + 1) * 512]
                w_av = BK.wr([bank]) + ([("V", tok_evw)] if fc == 0 else [])
                last_av = None
                for kt in range(nkt_b):
                    last_av = P.t(lambda t, kt=kt, fc=fc, psv=psv: t.matmul(
                        psv, Vt[:, kt * 1024 + fc * 128: kt * 1024 + (fc + 1) * 128],
                        PT_blk[:, kt * 512:(kt + 1) * 512],
                        start=(kt == 0), stop=(kt == nkt_b - 1)),
                        waits=w_av if kt == 0 else [])
                tok_ps_done[fc] = last_av
            prev_av[b] = tok_ps_done[NF - 1]
            tok_dump = None
            if "s0blk" in dbg_t and b == 0:
                for fc in range(NF):
                    psv = (PSA if fc < 4 else PSB)[:, (fc % 4) * 512:(fc % 4 + 1) * 512]
                    tkd = P.v(lambda v, fc=fc, psv=psv: v.tensor_copy(
                        x1sl[:, 0:512], psv),
                        waits=[("T", tok_ps_done[fc])] +
                              ([("DY", tok_dump)] if tok_dump else []))
                    tok_dump = P.dma_y(lambda y, fc=fc: y.dma_start(
                        dbg_t["s0blk"][:, fc, 0:512], x1sl[:, 0:512]),
                        waits=[("V", tkd)])
                    BK.rd([fc], "V", tkd)
            for fc in range(NF):
                psv = (PSA if fc < 4 else PSB)[:, (fc % 4) * 512:(fc % 4 + 1) * 512]
                ssl = x1sl[:, (fc % 2) * 512:(fc % 2 + 1) * 512]
                w_c = [("T", tok_ps_done[fc]), ("DY", tok_gb)]
                if tok_dump and fc < 2:
                    w_c += [("DY", tok_dump)]
                pk = (b, fc - 2) if fc >= 2 else (b - 1, fc + 6)
                if pk in tok_x1dma:
                    w_c += [("DY", tok_x1dma[pk])]
                tok_t1 = P.v(lambda v, fc=fc, psv=psv: v.tensor_mul(
                    scr[:, :], psv, g_rblk[:, fc * 512:(fc + 1) * 512]), waits=w_c)
                tok_t2 = P.v(lambda v, fc=fc, ssl=ssl: v.tensor_add(
                    ssl, scr[:, :], x_blk3[:, fc * 512:(fc + 1) * 512]),
                    waits=[("DY", tok_xb)])
                BK.rd([fc], "V", tok_t1)
                if "x1d" in dbg_t:
                    P.dma_y(lambda y, fc=fc, b=b, ssl=ssl: y.dma_start(
                        dbg_t["x1d"][:, fc, b * BLK:(b + 1) * BLK], ssl),
                        waits=[("V", tok_t2)])
                tok_x1dma[(b, fc)] = P.dma_y(lambda y, fc=fc, b=b, ssl=ssl: y.dma_start(
                    x1_hbm[:, fc, b * BLK:(b + 1) * BLK], ssl),
                    waits=[("V", tok_t2)])
            prev_x1_used = tok_t2
            prev_g_used = tok_t1
        mm.release()
        mm.release()
        return tok_x1dma[(NB - 1, NF - 1)]

    tok_p3 = None
    if BUILD_UPTO >= 3:
        tok_p3 = phase3(tok_p2)
        if "x1" in dbg_t:
            last = None
            for c in range(NF):
                last = P.dma_y(lambda y, c=c: y.dma_start(
                    dbg_t["x1"][:, c, :], x1_hbm[:, c, :]), waits=[("DY", tok_p3)])

    def phase4(tok_x1_last):
        mm.mark()
        xblk4 = [mm.alloc(f"xblk4_{i}", [128, NF * BLK], F32) for i in range(2)]
        xbf4 = mm.alloc("xbf4", [128, NF * BLK], BF16)
        xsq4 = mm.alloc("xsq4", [128, NF * BLK], BF16)
        last_store = [None]

        def ln1_write(b, c, tokB):
            t0 = b * BLK
            tk = P.v(lambda v: v.tensor_add(
                xn[:, c * T + t0: c * T + t0 + BLK], scr[:, :],
                PSB[:, 3 * BLK:4 * BLK]),
                waits=[("T", tokB)])
            last_store[0] = P.dma_y(lambda y, c=c, t0=t0: y.dma_start(
                xn1_hbm[:, c, t0:t0 + BLK], xn[:, c * T + t0: c * T + t0 + BLK]),
                waits=[("V", tk)])
            return tk

        P.label("P4.ln1")
        tok_ln1 = emit_ln("P4", "hbm_f32", ln1_write, x1_hbm, xblk4, xbf4, xsq4,
                          first_load_waits=[("DY", tok_x1_last)])
        mm.release()
        return tok_ln1, last_store[0]

    tok_p4 = tok_p4_store = None
    if BUILD_UPTO >= 4:
        tok_p4, tok_p4_store = phase4(tok_p3)
        if "xn1" in dbg_t:
            for c in range(NF):
                P.dma_y(lambda y, c=c: y.dma_start(
                    dbg_t["xn1"][:, c, :], xn[:, c * T:(c + 1) * T]),
                    waits=[("V", tok_p4)])

    def phase5(tok_ln1, tok_ln1_store):
        mm.mark()
        q1 = mm.alloc("q1", [128, NT * 1024], BF16)
        k1 = mm.alloc("k1", [128, NT * 1024], BF16)
        v1 = mm.alloc("v1", [128, NT * 1024], BF16)
        woring = mm.alloc("woring", [128, NF * 1024], BF16)
        st_q = mm.alloc("st_q", [128, 2 * 256], F32)
        st_k = mm.alloc("st_k", [128, 2 * 256], F32)
        rstd_q = mm.alloc("rstd_q", [128, 256], F32)
        rstd_k = mm.alloc("rstd_k", [128, 256], F32)
        qsq5 = mm.alloc("qsq5", [128, 1024], BF16)
        sc5 = mm.alloc("sc5", [128, 2 * 1024], F32)
        qf_c = mm.alloc("qf_c", [128, 1024], BF16)
        kf_c = mm.alloc("kf_c", [128, 1024], BF16)
        st_c = mm.alloc("st_c", [128, 2048], BF16)
        S_f32 = mm.alloc("S_f32", [128, NF * 64], F32)
        S_bf = mm.alloc("S_bf", [128, NF * 64], BF16)
        ofm = mm.alloc("ofm", [128, NF * BLK], BF16)
        s1stg = mm.alloc("s1stg", [128, 2 * 512], F32)

        P.label("P5.proj")
        w5 = [("DY", tok_ln1_store)]
        mm2 = mm

        def wload5(src, waits=()):
            return P.dma_g(lambda g: g.dma_start(
                woring[:, :].rearrange("p (c f) -> p c f", c=NF),
                src[:, :].rearrange("(c p) f -> p c f", p=128)), waits=waits)

        def proj_tm(dst, wsrc, wait_w):
            tok_w = wload5(wsrc, waits=wait_w)
            last_mm = None
            tok_ev = None
            for tt in range(NT):
                sl = tt % 2
                ps = PSA[:, sl * 1024:(sl + 1) * 1024]
                w0 = [("DG", tok_w), ("V", tok_ln1)] + BK.wr([2 * sl, 2 * sl + 1])
                for ci in range(NF):
                    for nb2 in range(2):
                        last_mm = P.t(lambda t, ci=ci, tt=tt, nb2=nb2, ps=ps: t.matmul(
                            ps[:, nb2 * 512:(nb2 + 1) * 512],
                            xn[:, ci * T + tt * 128: ci * T + tt * 128 + 128],
                            woring[:, ci * 1024 + nb2 * 512: ci * 1024 + (nb2 + 1) * 512],
                            start=(ci == 0), stop=(ci == NF - 1)),
                            waits=w0 if (ci == 0 and nb2 == 0) else [])
                tok_ev = P.v(lambda v, tt=tt, ps=ps: v.tensor_copy(
                    dst[:, tt * 1024:(tt + 1) * 1024], ps),
                    waits=[("T", last_mm)])
                BK.rd([2 * sl, 2 * sl + 1], "V", tok_ev)
            return last_mm, tok_ev

        tok_q1mm, tok_q1ev = proj_tm(q1, dram["w1qT"], w5)
        tok_k1mm, tok_k1ev = proj_tm(k1, dram["w1kT"], [("T", tok_q1mm)])
        tok_v1mm, tok_v1ev = proj_tm(v1, dram["w1vT"], [("T", tok_k1mm)])
        tok_wo = wload5(dram["w1oT"], waits=[("T", tok_v1mm)])

        P.label("P5.hln")

        def head_ln(src, st, rstd, gamma, beta, pre):
            tok_red = None
            for tt in range(NT):
                tile = src[:, tt * 1024:(tt + 1) * 1024]
                tok_red = P.v(lambda v, tile=tile, tt=tt: v.reduce_sum(
                    st[:, tt * 16:(tt + 1) * 16],
                    tile.rearrange("p (h d) -> p h d", d=64), axis=AX.X),
                    waits=[("V", pre)] if tt == 0 else [])
                tok_sq = P.s(lambda s, tile=tile: s.activation(
                    qsq5[:, :], tile, AF.Square),
                    waits=[("V", tok_red)])
                tok_red = P.v(lambda v, tt=tt: v.reduce_sum(
                    st[:, 256 + tt * 16: 256 + (tt + 1) * 16],
                    qsq5[:, :].rearrange("p (h d) -> p h d", d=64), axis=AX.X),
                    waits=[("S", tok_sq)])
            P.v(lambda v: v.tensor_scalar_mul(st[:, 0:256], st[:, 0:256], 1.0 / 64))
            P.v(lambda v: v.tensor_scalar(rstd[:, :], st[:, 256:512], 1.0 / 64, EPS,
                                          op0=ALU.mult, op1=ALU.add))
            P.v(lambda v: v.scalar_tensor_tensor(sc5[:, 0:256], st[:, 0:256], 1.0,
                                                 st[:, 0:256], op0=ALU.mult,
                                                 op1=ALU.mult))
            tk = P.v(lambda v: v.tensor_sub(rstd[:, :], rstd[:, :], sc5[:, 0:256]))
            tk2 = P.s(lambda s: s.activation(rstd[:, :], rstd[:, :], AF.Sqrt),
                      waits=[("V", tk)])
            tok_fin = P.v(lambda v: v.reciprocal(rstd[:, :], rstd[:, :]),
                          waits=[("S", tk2)])
            tok_ap = None
            for tt in range(NT):
                tile = src[:, tt * 1024:(tt + 1) * 1024]
                muAP = bass.AP(st, tt * 16, [[512, 128], [1, 16], [0, 64]])
                aAP = bass.AP(rstd, tt * 16, [[256, 128], [1, 16], [0, 64]])
                gAP = bass.AP(gamma, 0, [[64, 128], [0, 16], [1, 64]])
                bAP = bass.AP(beta, 0, [[64, 128], [0, 16], [1, 64]])
                t3d = sc5[:, 0:1024].rearrange("p (h d) -> p h d", d=64)
                t3d2 = sc5[:, 1024:2048].rearrange("p (h d) -> p h d", d=64)
                w1 = [("V", tok_fin)] if tt == 0 else []
                if tok_ap is not None:
                    w1 += [("G", tok_ap)]
                P.v(lambda v, tile=tile, muAP=muAP, t3d=t3d: v.tensor_tensor(
                    t3d, tile.rearrange("p (h d) -> p h d", d=64), muAP,
                    op=ALU.subtract),
                    waits=w1)
                tok_v2b = P.v(lambda v, aAP=aAP, t3d=t3d, t3d2=t3d2: v.tensor_tensor(
                    t3d2, t3d, aAP, op=ALU.mult))
                P.g(lambda g, gAP=gAP, t3d=t3d, t3d2=t3d2: g.tensor_tensor(
                    t3d, t3d2, gAP, op=ALU.mult),
                    waits=[("V", tok_v2b)])
                tok_ap = P.g(lambda g, tile=tile, bAP=bAP, t3d=t3d: g.tensor_tensor(
                    tile.rearrange("p (h d) -> p h d", d=64), t3d, bAP, op=ALU.add))
            return tok_ap

        import os
        p5lim = os.environ.get("P5_LIMIT", "full")
        if p5lim == "proj":
            mm.release()
            return tok_v1ev and None or tok_v1ev

        tok_q1n = head_ln(q1, st_q, rstd_q, sv["qg"], sv["qbe"], tok_q1ev)
        tok_k1n = head_ln(k1, st_k, rstd_k, sv["kg"], sv["kbe"], tok_k1ev)
        if p5lim == "hln":
            mm.release()
            return None

        P.v(lambda v: v.memset(S_f32[:, :], 0.0))
        tok_s0 = P.v(lambda v: v.memset(S_bf[:, :], 0.0))

        P.label("P5.attn")
        tok_s1dma = None
        prev_ofm_ev = None
        prev_o_mm = {}
        prev_sb = {}
        s1dma_slot = {}
        tok_wo_used = None
        ncchunks = {"attn1": 1, "attn4": 4}.get(p5lim, NT)
        for c in range(ncchunks):
            toks_f = {}
            for (nm, src, dstb) in [("q", q1, qf_c), ("k", k1, kf_c)]:
                last_tr = None
                bank = 4 if nm == "q" else 5
                for hp in range(NF):
                    w_tr = []
                    if hp == 0:
                        w_tr = BK.wr([bank]) + [("G", tok_k1n), ("V", tok_s0)]
                        if (c - 1) in prev_o_mm:
                            w_tr += [("T", prev_o_mm[c - 1])]
                    last_tr = P.t(lambda t, c=c, hp=hp, src=src, bank=bank: t.transpose(
                        PSBbf[:, (bank - 4) * 1024 + hp * 128: (bank - 4) * 1024 + (hp + 1) * 128],
                        src[:, c * 1024 + hp * 128: c * 1024 + (hp + 1) * 128],
                        ident[:, :]),
                        waits=w_tr)
                toks_f[nm] = P.v(lambda v, dstb=dstb, bank=bank: v.tensor_copy(
                    dstb[:, :],
                    PSBbf[:, (bank - 4) * 1024:(bank - 4) * 1024 + 1024]),
                    waits=[("T", last_tr)])
                BK.rd([bank], "V", toks_f[nm])

            if p5lim == "tr":
                prev_o_mm[c] = P.counts["T"]
                continue
            for hp in range(NF):
                qA = qf_c[0:64, hp * 128:(hp + 1) * 128]
                qB = qf_c[64:128, hp * 128:(hp + 1) * 128]
                kA = kf_c[0:64, hp * 128:(hp + 1) * 128]
                kB = kf_c[64:128, hp * 128:(hp + 1) * 128]
                psst = PSA[:, (hp % 2) * 512:(hp % 2) * 512 + 256]
                w_st = BK.wr([hp % 2]) + ([("V", toks_f["q"]), ("V", toks_f["k"])]
                                          if True else [])
                tok_stA = P.t(lambda t, psst=psst, kA=kA, qA=qA: t.matmul(
                    psst[:, 0:128], kA, qA, start=True, stop=True), waits=w_st)
                tok_stB = P.t(lambda t, psst=psst, kB=kB, qB=qB: t.matmul(
                    psst[:, 128:256], kB, qB, start=True, stop=True,
                    tile_position=(64, 0)), waits=[("T", tok_stA)])
                stc = st_c[:, hp * 256:(hp + 1) * 256]
                mAP = bass.AP(maskMT, 0, [[128, 128], [0, 2], [1, 128]])
                w_msk = [("T", tok_stB)]
                if hp == 0 and (c - 1) in prev_o_mm:
                    w_msk += [("T", prev_o_mm[c - 1])]
                tok_msk = P.v(lambda v, stc=stc, psst=psst, mAP=mAP: v.tensor_tensor(
                    stc.rearrange("p (a t) -> p a t", a=2), psst.rearrange(
                        "p (a t) -> p a t", a=2), mAP, op=ALU.mult),
                    waits=w_msk)
                BK.rd([hp % 2], "V", tok_msk)
                if p5lim == "sc":
                    prev_o_mm[c] = P.counts["T"]
                    continue
                pso = PSB[:, 1024 + hp * 128: 1024 + (hp + 1) * 128]
                w_o = BK.wr([6 + hp // 4]) if hp in (0, 4) else []
                vA = v1[:, c * 1024 + hp * 128: c * 1024 + hp * 128 + 64]
                vB = v1[:, c * 1024 + hp * 128 + 64: c * 1024 + (hp + 1) * 128]
                tok_o1 = P.t(lambda t, pso=pso, vA=vA, stc=stc: t.matmul(
                    pso[0:64, :], vA, stc[:, 0:128], start=True, stop=False),
                    waits=w_o + [("V", tok_msk)])
                tok_o2 = P.t(lambda t, pso=pso, vB=vB, stc=stc: t.matmul(
                    pso[64:128, :], vB, stc[:, 128:256], start=True, stop=False,
                    tile_position=(0, 64)))
                sA = S_bf[0:64, hp * 64:(hp + 1) * 64]
                sB = S_bf[64:128, hp * 64:(hp + 1) * 64]
                w_o3 = [("V", prev_sb[hp])] if prev_sb.get(hp) else []
                tok_o3 = P.t(lambda t, pso=pso, sA=sA, qA=qA: t.matmul(
                    pso[0:64, :], sA, qA, start=False, stop=True), waits=w_o3)
                tok_o4 = P.t(lambda t, pso=pso, sB=sB, qB=qB: t.matmul(
                    pso[64:128, :], sB, qB, start=False, stop=True,
                    tile_position=(64, 64)))
                prev_o_mm[c] = tok_o4
                if p5lim == "o":
                    continue
                pssu = PSA[:, 1024 + (hp % 2) * 512: 1024 + (hp % 2) * 512 + 64]
                kAt = k1[:, c * 1024 + hp * 128: c * 1024 + hp * 128 + 64]
                kBt = k1[:, c * 1024 + hp * 128 + 64: c * 1024 + (hp + 1) * 128]
                w_su = BK.wr([2 + hp % 2])
                tok_su1 = P.t(lambda t, pssu=pssu, kAt=kAt, vA=vA: t.matmul(
                    pssu[0:64, :], kAt, vA, start=True, stop=True),
                    waits=w_su)
                tok_su2 = P.t(lambda t, pssu=pssu, kBt=kBt, vB=vB: t.matmul(
                    pssu[64:128, :], kBt, vB, start=True, stop=True,
                    tile_position=(0, 64)))
                tok_sa = P.v(lambda v, hp=hp, pssu=pssu: v.tensor_add(
                    S_f32[:, hp * 64:(hp + 1) * 64], S_f32[:, hp * 64:(hp + 1) * 64],
                    pssu),
                    waits=[("T", tok_su2), ("T", tok_o4)])
                tok_sb = P.v(lambda v, hp=hp: v.tensor_copy(
                    S_bf[:, hp * 64:(hp + 1) * 64],
                    S_f32[:, hp * 64:(hp + 1) * 64]))
                prev_sb[hp] = tok_sb
                BK.rd([2 + hp % 2], "V", tok_sa)
            w_ev = [("T", prev_o_mm[c])]
            if c % 4 == 0 and tok_wo_used is not None:
                w_ev += [("T", tok_wo_used)]
            tok_oev = P.v(lambda v, c=c: v.tensor_copy(
                bass.AP(ofm, (c % 4) * 128, [[NF * BLK, 128], [BLK, NF], [1, 128]]),
                bass.AP(PSB, 1024, [[2048, 128], [128, NF], [1, 128]])),
                waits=w_ev)
            BK.rd([6, 7], "V", tok_oev)

            if c % 4 == 3:
                bb = c // 4
                P.label(f"P5.wo{bb}")
                for oc in range(NF):
                    sl = oc % 2
                    ps = PSB[:, sl * 512:(sl + 1) * 512]
                    w0 = [("V", tok_oev), ("DG", tok_wo)] + BK.wr([4 + sl])
                    last_mm = None
                    for ci in range(NF):
                        last_mm = P.t(lambda t, ci=ci, oc=oc, ps=ps: t.matmul(
                            ps, woring[:, ci * 1024 + oc * 128: ci * 1024 + oc * 128 + 128],
                            ofm[:, ci * BLK:(ci + 1) * BLK],
                            start=(ci == 0), stop=(ci == NF - 1)),
                            waits=w0 if ci == 0 else [])
                    tok_wo_used = last_mm
                    w_se = [("T", last_mm)]
                    if s1dma_slot.get(oc % 2):
                        w_se += [("DY", s1dma_slot[oc % 2])]
                    tok_sev = P.v(lambda v, oc=oc, ps=ps: v.tensor_copy(
                        s1stg[:, (oc % 2) * 512:(oc % 2 + 1) * 512], ps),
                        waits=w_se)
                    BK.rd([4 + sl], "V", tok_sev)
                    tok_s1dma = P.dma_y(lambda y, oc=oc, bb=bb: y.dma_start(
                        s1_hbm[:, oc, bb * BLK:(bb + 1) * BLK],
                        s1stg[:, (oc % 2) * 512:(oc % 2 + 1) * 512]),
                        waits=[("V", tok_sev)])
                    s1dma_slot[oc % 2] = tok_s1dma
        mm.release()
        return tok_s1dma

    tok_p5 = None
    import os as _os
    if _os.environ.get("P5_LIMIT", "full") != "full" and BUILD_UPTO >= 5:
        phase5(tok_p4, tok_p4_store)
        P.label("final")
        P.y(lambda y: y.nop(), waits=[("DY", P.counts.get("DY", 0)),
                                      ("DG", P.counts.get("DG", 0))])
        with contextlib.ExitStack() as st:
            sems = {}
            for name in ["DG", "DY", "T", "V", "S", "G"]:
                sems[name] = st.enter_context(nc.semaphore(name))
            block = st.enter_context(nc.Block())
            P.replay_into(nc, block, sems)
        return nc
    if BUILD_UPTO >= 5:
        tok_p5 = phase5(tok_p4, tok_p4_store)
        if "s1" in dbg_t:
            for c in range(NF):
                P.dma_y(lambda y, c=c: y.dma_start(
                    dbg_t["s1"][:, c, :], s1_hbm[:, c, :]), waits=[("DY", tok_p5)])

    def phase6(tok_s1_last):
        mm.release()
        mm.mark()
        W1t = mm.alloc("W1t", [128, NF * FF], BF16)
        W2t = mm.alloc("W2t", [128, 32 * 1024], BF16)
        mm.mark()
        stg = [mm.alloc(f"tstg{i}", [128, 1024], F32) for i in range(2)]
        tsc = [mm.alloc(f"tsc{i}", [128, 1024], F32) for i in range(2)]

        P.label("P6.tern")
        tern_done = {}
        tok_ld = {}
        pieces = []
        for fc in range(NF):
            for j in range(4):
                pieces.append(("w1", fc, j))
        for cc in range(32):
            pieces.append(("w2", cc, 0))
        prev_tern = {}
        for idx, (mat, a, jj) in enumerate(pieces):
            sl = idx % 2
            if mat == "w1":
                src = dram["w1T"][a * 128:(a + 1) * 128, jj * 1024:(jj + 1) * 1024]
                dst = W1t[:, a * FF + jj * 1024: a * FF + (jj + 1) * 1024]
                rs = w_rs["w1"]
            else:
                src = dram["w2T"][a * 128:(a + 1) * 128, :]
                dst = W2t[:, a * 1024:(a + 1) * 1024]
                rs = w_rs["w2"]
            w_ld = [("DY", tok_s1_last)] if idx < 2 else [("S", prev_tern[idx - 2])]
            tok_l = P.dma_y(lambda y, sl=sl, src=src: y.dma_start(stg[sl][:, :], src),
                            waits=w_ld)
            tok_t1 = P.s(lambda s, sl=sl, rs=rs: s.activation(
                tsc[sl][:, :], stg[sl][:, :], AF.Identity,
                bias=cmagic[:, :], scale=rs[:, :]),
                waits=[("DY", tok_l)] + ([("V", tok_qb32)] if idx == 0 else []) +
                      ([("V", tern_done[idx - 2])] if (idx - 2) in tern_done else []))
            prev_tern[idx] = tok_t1
            P.v(lambda v, sl=sl: v.tensor_scalar(
                tsc[sl][:, :], tsc[sl][:, :], MAGIC, -1.0,
                op0=ALU.subtract, op1=ALU.max),
                waits=[("S", tok_t1)])
            tern_done[idx] = P.v(lambda v, sl=sl, dst=dst: v.tensor_scalar_min(
                dst, tsc[sl][:, :], 1.0))
        tok_tern = tern_done[len(pieces) - 1]
        mm.release()
        h_blk = mm.alloc("h_blk", [128, 32 * 256], BF16)
        xn1b = mm.alloc("xn1b", [128, NF * 256], BF16)
        x1b = mm.alloc("x1b", [128, NF * 256], F32)
        s1b = mm.alloc("s1b", [128, NF * 256], F32)
        gm = mm.alloc("gm", [128, 2 * 256], F32)
        yb = mm.alloc("yb", [128, 2 * 256], F32)

        NBB = 8
        TB = 256
        tok_ydma = {}
        prev_h_mm = None
        prev_g_mm = None
        tok_xnb = tok_x1b = tok_s1b = None
        for bb in range(NBB):
            P.label(f"P6.blk{bb}")
            t0 = bb * TB
            w_ld = [("V", tok_tern)]
            if bb > 0:
                w_ld = [("T", prev_h_mm)]
            tok_xnb = P.dma_y(lambda y, t0=t0: y.dma_start(
                xn1b[:, :].rearrange("p (c t) -> p c t", c=NF),
                xn1_hbm[:, :, t0:t0 + TB]), waits=w_ld)
            w_l2 = [("V", tok_tern)] if bb == 0 else [("V", prev_comb)]
            tok_x1b = P.dma_y(lambda y, t0=t0: y.dma_start(
                x1b[:, :].rearrange("p (c t) -> p c t", c=NF),
                x1_hbm[:, :, t0:t0 + TB]), waits=w_l2)
            tok_s1b = P.dma_y(lambda y, t0=t0: y.dma_start(
                s1b[:, :].rearrange("p (c t) -> p c t", c=NF),
                s1_hbm[:, :, t0:t0 + TB]),
                waits=[("DY", tok_s1_last), ("V", tok_tern)] if bb == 0
                else [("V", prev_comb)])

            prev_hev = {}
            for cc in range(32):
                bank = 4 + cc % 2
                ps = PSB[:, (cc % 2) * 512:(cc % 2) * 512 + TB]
                w0 = [("DY", tok_xnb), ("V", tok_tern)] + BK.wr([bank])
                last_mm = None
                for ci in range(NF):
                    last_mm = P.t(lambda t, ci=ci, cc=cc, ps=ps: t.matmul(
                        ps, W1t[:, ci * FF + cc * 128: ci * FF + cc * 128 + 128],
                        xn1b[:, ci * TB:(ci + 1) * TB],
                        start=(ci == 0), stop=(ci == NF - 1)),
                        waits=w0 if ci == 0 else [])
                prev_h_mm = last_mm
                tok_h = P.s(lambda s, cc=cc, ps=ps: s.activation(
                    h_blk[:, cc * TB:(cc + 1) * TB], ps, AF.Gelu,
                    bias=sv["b1"][:, cc:cc + 1]),
                    waits=[("T", last_mm)])
                BK.rd([bank], "S", tok_h)
                prev_hev[cc] = tok_h
            for mh in range(NF):
                bank = mh % 4
                ps = PSA[:, bank * 512: bank * 512 + TB]
                w0 = [("S", prev_hev[31])] + BK.wr([bank])
                last_mm = None
                for cc in range(32):
                    last_mm = P.t(lambda t, cc=cc, mh=mh, ps=ps: t.matmul(
                        ps, W2t[:, cc * 1024 + mh * 128: cc * 1024 + mh * 128 + 128],
                        h_blk[:, cc * TB:(cc + 1) * TB],
                        start=(cc == 0), stop=(cc == 31)),
                        waits=w0 if cc == 0 else [])
                prev_g_mm = last_mm
                tok_g = P.s(lambda s, mh=mh, ps=ps: s.activation(
                    gm[:, (mh % 2) * TB:(mh % 2 + 1) * TB], ps, AF.Silu,
                    bias=sv["b2"][:, mh:mh + 1]),
                    waits=[("T", last_mm)] +
                          ([("V", tok_yw2)] if mh >= 2 else []))
                tok_yw1 = P.v(lambda v, mh=mh: v.tensor_mul(
                    scr[:, 0:TB], gm[:, (mh % 2) * TB:(mh % 2 + 1) * TB],
                    s1b[:, mh * TB:(mh + 1) * TB]),
                    waits=[("S", tok_g), ("DY", tok_s1b)])
                BK.rd([bank], "V", tok_yw1)
                w_y2 = [("DY", tok_x1b)]
                yk = (bb, mh - 2) if mh >= 2 else (bb - 1, mh + 6)
                if yk in tok_ydma:
                    w_y2 += [("DY", tok_ydma[yk])]
                tok_yw2 = P.v(lambda v, mh=mh: v.tensor_add(
                    yb[:, (mh % 2) * TB:(mh % 2 + 1) * TB], scr[:, 0:TB],
                    x1b[:, mh * TB:(mh + 1) * TB]),
                    waits=w_y2)
                tok_ydma[(bb, mh)] = P.dma_y(lambda y, mh=mh, t0=t0: y.dma_start(
                    yOut[:, mh, t0:t0 + TB],
                    yb[:, (mh % 2) * TB:(mh % 2 + 1) * TB]),
                    waits=[("V", tok_yw2)])
            prev_comb = tok_yw2
        mm.release()
        return tok_ydma[(NBB - 1, NF - 1)]

    if BUILD_UPTO >= 6:
        phase6(tok_p5)

    P.label("final")
    P.y(lambda y: y.nop(), waits=[("DY", P.counts.get("DY", 0)),
                                  ("DG", P.counts.get("DG", 0))])

    with contextlib.ExitStack() as st:
        sems = {}
        for name in ["DG", "DY", "T", "V", "S", "G"]:
            sems[name] = st.enter_context(nc.semaphore(name))
        block = st.enter_context(nc.Block())
        P.replay_into(nc, block, sems)
    return nc


def build_p2(env):
    pass


def build_p3(env):
    pass


def build_p4(env):
    pass


def build_p5(env):
    pass


def build_p6(env):
    pass


_cached = {}


def kernel(**inputs):
    if "nc" not in _cached:
        _cached["nc"] = build_program()
    nc = _cached["nc"]

    x = np.asarray(inputs["L0_attn_x"], np.float32)

    def tr(a):
        return np.ascontiguousarray(np.asarray(a, np.float32).T)

    def cols(vec, k):
        v = np.ascontiguousarray(np.asarray(vec, np.float32))
        return np.ascontiguousarray(v.reshape(k, 128).T)

    def bc(vec):
        return np.ascontiguousarray(np.tile(
            np.asarray(vec, np.float32)[None, :], (128, 1)))

    shared = {
        "wqT": tr(inputs["L0_attn_WQ"]), "wkT": tr(inputs["L0_attn_WK"]),
        "wvT": tr(inputs["L0_attn_WV"]),
        "qwT": tr(inputs["L0_ffn_q_w"]), "kwT": tr(inputs["L0_ffn_k_w"]),
        "vwT": tr(inputs["L0_ffn_v_w"]), "owT": tr(inputs["L0_ffn_o_w"]),
        "nw_c": cols(inputs["L0_ffn_norm_w"], NF),
        "nb_c": cols(inputs["L0_ffn_norm_b"], NF),
        "qb_c": cols(inputs["L0_ffn_q_b"], NF),
        "kb_c": cols(inputs["L0_ffn_k_b"], NF),
        "ob_c": cols(inputs["L0_ffn_o_b"], NF),
        "vb_bc": bc(inputs["L0_ffn_v_b"]),
        "w1qT": tr(inputs["L1_attn_Wq"]), "w1kT": tr(inputs["L1_attn_Wk"]),
        "w1vT": tr(inputs["L1_attn_Wv"]), "w1oT": tr(inputs["L1_attn_Wo"]),
        "qg_bc": bc(inputs["L1_attn_q_gamma"]),
        "qbeta_bc": bc(inputs["L1_attn_q_beta"]),
        "kg_bc": bc(inputs["L1_attn_k_gamma"]),
        "kbeta_bc": bc(inputs["L1_attn_k_beta"]),
        "w1T": tr(inputs["L1_ffn_W1"]), "w2T": tr(inputs["L1_ffn_W2"]),
        "b1_c": cols(inputs["L1_ffn_b1"], 32),
        "b2_c": cols(inputs["L1_ffn_b2"], NF),
    }

    in_maps = []
    for c in range(8):
        m = dict(shared)
        m["xT"] = np.ascontiguousarray(x[c].T)
        in_maps.append(m)

    res = run_bass_kernel_spmd(nc, in_maps, core_ids=list(range(8)))
    _cached["last_res"] = res
    outs = []
    for c in range(8):
        y = np.asarray(res.results[c]["y"], np.float32)
        outs.append(y.transpose(2, 1, 0).reshape(T, D))
    return np.stack(outs, axis=0)


# version 09
# speedup vs baseline: 1.0569x; 1.0569x over previous
import contextlib
import numpy as np
import ml_dtypes
import concourse.bass as bass
import concourse.mybir as mybir
from concourse.bass_utils import run_bass_kernel_spmd

F32 = mybir.dt.float32
BF16 = mybir.dt.bfloat16
AF = mybir.ActivationFunctionType
ALU = mybir.AluOpType
AX = mybir.AxisListType

T = 2048
D = 1024
NF = 8
NT = 16
NB = 4
BLK = 512
MAGIC = float(np.float32(1.5 * 2 ** 23))
EPS = 1e-5
SCALE = 1.0 / 32.0
FF = 4096

BUILD_UPTO = 6
DEBUG_DUMPS = []

ENGINE_ATTR = {"g": "gpsimd", "y": "sync", "t": "tensor", "v": "vector", "s": "scalar"}
CHAINED = {"v": "V", "s": "S", "g": "G"}


class Prog:

    def __init__(self):
        self.items = {e: [] for e in ENGINE_ATTR}
        self.counts = {}
        self.labels = {e: [] for e in ENGINE_ATTR}
        self.cur_label = "init"
        self.last_tok = {e: None for e in ENGINE_ATTR}

    def label(self, s):
        self.cur_label = s

    def emit(self, eng, fn, inc=None, inc_n=1, waits=()):
        wl = [(s, v) for (s, v) in waits if v is not None and v > 0]
        if eng in CHAINED:
            sem = CHAINED[eng]
            if inc is None:
                inc, inc_n = sem, 1
            prev = self.last_tok[eng]
            if prev is not None:
                wl.append((sem, prev))
        self.items[eng].append((fn, inc, inc_n, wl))
        self.labels[eng].append(self.cur_label)
        tok = None
        if inc is not None:
            self.counts[inc] = self.counts.get(inc, 0) + inc_n
            tok = self.counts[inc]
        if eng in CHAINED and inc == CHAINED[eng]:
            self.last_tok[eng] = tok
        return tok

    def t(self, fn, inc="T", inc_n=1, waits=()):
        return self.emit("t", fn, inc, inc_n, waits)

    def v(self, fn, waits=()):
        return self.emit("v", fn, None, 1, waits)

    def s(self, fn, waits=()):
        return self.emit("s", fn, None, 1, waits)

    def g(self, fn, waits=()):
        return self.emit("g", fn, None, 1, waits)

    def y(self, fn, inc=None, inc_n=1, waits=()):
        return self.emit("y", fn, inc, inc_n, waits)

    def dma_g(self, fn, waits=()):
        return self.emit("g", fn, "DG", 16, waits)

    def dma_y(self, fn, waits=()):
        return self.emit("y", fn, "DY", 16, waits)

    def check(self):
        pos = {e: 0 for e in self.items}
        done = {}
        progress = True
        while progress:
            progress = False
            for e, lst in self.items.items():
                while pos[e] < len(lst):
                    fn, inc, inc_n, waits = lst[pos[e]]
                    if any(done.get(s, 0) < v for (s, v) in waits):
                        break
                    if inc:
                        done[inc] = done.get(inc, 0) + inc_n
                    pos[e] += 1
                    progress = True
        stuck = {e: pos[e] for e in pos if pos[e] < len(self.items[e])}
        if stuck:
            msg = []
            for e, p in stuck.items():
                fn, inc, inc_n, waits = self.items[e][p]
                lbl = self.labels[e][p]
                missing = [(s, v, done.get(s, 0)) for (s, v) in waits
                           if done.get(s, 0) < v]
                msg.append(f"engine {e} item {p} [{lbl}] waits {missing}")
            raise RuntimeError("Deadlock in recorded program:\n" + "\n".join(msg))

    def replay_into(self, nc, block, sems):
        self.check()
        for e, attr in ENGINE_ATTR.items():
            items = self.items[e]
            if not items:
                continue

            def make(items=items):
                def run(eng):
                    waited = {}
                    for (fn, inc, inc_n, waits) in items:
                        for (s_, v_) in waits:
                            if waited.get(s_, 0) >= v_:
                                continue
                            eng.wait_ge(sems[s_], v_)
                            waited[s_] = v_
                        instr = fn(eng)
                        if inc is not None:
                            instr.then_inc(sems[inc], inc_n)
                return run

            getattr(block, attr)(make())


class MemMap:
    def __init__(self, nc):
        self.nc = nc
        self.base = nc.sbuf_base
        self.top = nc.sbuf_top
        self.cur = int(np.ceil(self.base / 32) * 32)
        self.stack = []
        self.peak = self.cur

    def alloc(self, name, shape, dtype):
        bpp = int(np.prod(shape[1:])) * mybir.dt.size(dtype)
        bpp = int(np.ceil(bpp / 32) * 32)
        off = self.cur
        assert off + bpp <= self.top, (
            f"SBUF overflow: {name} needs {bpp}B at {off}, top={self.top}")
        self.cur = off + bpp
        self.peak = max(self.peak, self.cur)
        return self.nc.alloc_sbuf_tensor_at(name, list(shape), dtype, offset=off)

    def mark(self):
        self.stack.append(self.cur)

    def release(self):
        self.cur = self.stack.pop()


class Banks:

    def __init__(self):
        self.last_read = {}
        self.last_write = {}

    @staticmethod
    def of(tensor_is_b, col0, col1):
        b0 = 4 if tensor_is_b else 0
        return list(range(b0 + col0 // 512, b0 + (col1 + 511) // 512))

    def wr(self, banks):
        w = []
        for b in banks:
            if b in self.last_read:
                w.append(self.last_read[b])
        return w

    def rd(self, banks, sem, tok):
        for b in banks:
            self.last_read[b] = (sem, tok)


def build_program():
    nc = bass.Bass("TRN2", target_bir_lowering=False)

    xT = nc.dram_tensor("xT", [D, T], F32, kind="ExternalInput")
    dram = {}
    for nm, shape in [("wqT", [D, D]), ("wkT", [D, D]), ("wvT", [D, D]),
                      ("qwT", [D, D]), ("kwT", [D, D]), ("vwT", [D, D]),
                      ("owT", [D, D]), ("w1qT", [D, D]), ("w1kT", [D, D]),
                      ("w1vT", [D, D]), ("w1oT", [D, D]),
                      ("w1T", [D, FF]), ("w2T", [FF, D]),
                      ("nw_c", [128, NF]), ("nb_c", [128, NF]), ("qb_c", [128, NF]),
                      ("kb_c", [128, NF]), ("ob_c", [128, NF]), ("b2_c", [128, NF]),
                      ("b1_c", [128, 32]), ("vb_bc", [128, D]),
                      ("qg_bc", [128, 64]), ("qbeta_bc", [128, 64]),
                      ("kg_bc", [128, 64]), ("kbeta_bc", [128, 64])]:
        dram[nm] = nc.dram_tensor(nm, shape, F32, kind="ExternalInput")

    yOut = nc.dram_tensor("y", [128, NF, T], F32, kind="ExternalOutput")
    g_hbm = nc.dram_tensor("g_scr", [128, NF, T], BF16, kind="Internal")
    x1_hbm = nc.dram_tensor("x1_scr", [128, NF, T], F32, kind="Internal")
    xn1_hbm = nc.dram_tensor("xn1_scr", [128, NF, T], BF16, kind="Internal")
    s1_hbm = nc.dram_tensor("s1_scr", [128, NF, T], F32, kind="Internal")

    dbg_t = {}
    for nm in DEBUG_DUMPS:
        shp = [128, NF, T]
        dt_ = BF16 if nm in ("xn", "xn1", "g", "pt0") else F32
        dbg_t[nm] = nc.dram_tensor("dbg_" + nm, shp, dt_, kind="ExternalOutput")

    PSA = nc.alloc_psum_tensor("PSA", [128, 2048], F32)
    PSB = nc.alloc_psum_tensor("PSB", [128, 2048], F32)
    nc.psum_base -= 8
    PSAbf = nc.alloc_psum_tensor("PSAbf", [128, 4096], BF16)
    PSBbf = nc.alloc_psum_tensor("PSBbf", [128, 4096], BF16)

    mm = MemMap(nc)
    ident = mm.alloc("ident", [128, 128], BF16)
    maskA = mm.alloc("maskA", [128, 128], F32)
    maskMT = mm.alloc("maskMT", [128, 128], F32)
    ones_colb = mm.alloc("ones_colb", [128, 1], BF16)
    ones_row1f = mm.alloc("ones_row1f", [1, 128], F32)
    ones_colf = mm.alloc("ones_colf", [128, 128], F32)
    cmagic = mm.alloc("cmagic", [128, 1], F32)
    sv = {}
    for name in ["nw", "nb", "qb", "kb", "ob", "b2"]:
        sv[name] = mm.alloc("sv_" + name, [128, NF], F32)
    sv["b1"] = mm.alloc("sv_b1", [128, 32], F32)
    sv["qb32"] = mm.alloc("sv_qb32", [128, NF], F32)
    sv["vbbc"] = mm.alloc("sv_vbbc", [128, D], F32)
    for nm2 in ["qg", "qbe", "kg", "kbe"]:
        sv[nm2] = mm.alloc("sv_" + nm2, [128, 64], F32)
    w_acc = {"w1": mm.alloc("w1_acc", [128, 1], F32),
             "w2": mm.alloc("w2_acc", [128, 1], F32)}
    w_rs = {"w1": mm.alloc("w1_rs", [128, 1], F32),
            "w2": mm.alloc("w2_rs", [128, 1], F32)}
    tern_tmp = mm.alloc("tern_tmp", [128, 2], F32)
    strow = mm.alloc("strow", [1, 5 * BLK], F32)
    scr = mm.alloc("scr", [128, 512], F32)
    negmax = mm.alloc("negmax", [128, 1], F32)
    sums = mm.alloc("sums", [128, 1], F32)
    recs = mm.alloc("recs", [128, 16], F32)

    mm.mark()
    xn = mm.alloc("xn", [128, NF * T], BF16)

    P = Prog()
    BK = Banks()

    P.label("P0.init")
    P.g(lambda g: g.memset(ident[:, :], 0.0))
    P.g(lambda g: g.affine_select(
        out=ident[:, :], in_=ident[:, :], compare_op=ALU.not_equal, fill=1.0,
        base=0, pattern=[[-1, 128]], channel_multiplier=1))
    P.g(lambda g: g.memset(maskA[:, :], 0.0))
    P.g(lambda g: g.affine_select(
        out=maskA[:, :], in_=maskA[:, :], compare_op=ALU.is_ge, fill=-1e9,
        base=0, pattern=[[-1, 128]], channel_multiplier=1))
    P.g(lambda g: g.memset(maskMT[:, :], 1.0))
    P.g(lambda g: g.affine_select(
        out=maskMT[:, :], in_=maskMT[:, :], compare_op=ALU.is_ge, fill=0.0,
        base=0, pattern=[[1, 128]], channel_multiplier=-1))
    P.g(lambda g: g.memset(ones_colb[:, :], 1.0))
    P.g(lambda g: g.memset(ones_row1f[:, :], 1.0))
    P.g(lambda g: g.memset(ones_colf[:, :], 1.0))
    P.g(lambda g: g.memset(w_acc["w1"][:, :], 0.0))
    P.g(lambda g: g.memset(w_acc["w2"][:, :], 0.0))
    tok_init_g = P.g(lambda g: g.memset(cmagic[:, :], MAGIC))

    tok_sv = None
    for (dstn, srcn) in [("nw", "nw_c"), ("nb", "nb_c"), ("qb", "qb_c"),
                         ("kb", "kb_c"), ("ob", "ob_c"), ("b2", "b2_c"),
                         ("b1", "b1_c"), ("vbbc", "vb_bc"),
                         ("qg", "qg_bc"), ("qbe", "qbeta_bc"),
                         ("kg", "kg_bc"), ("kbe", "kbeta_bc")]:
        tok_sv = P.dma_y(lambda y, d=sv[dstn], s_=dram[srcn]: y.dma_start(d[:, :], s_[:, :]))
    tok_qb32 = P.v(lambda v: v.tensor_scalar_mul(sv["qb32"][:, :], sv["qb"][:, :], SCALE),
                   waits=[("DY", tok_sv)])

    def emit_ln(phase, src_kind, dst_write, src_hbm, xblk, xbf, xsq,
                apply_chunks=None, pre_tok=None, first_load_waits=()):
        last_ap = None
        mmlast_prev = None
        finB_prev = None
        apdone = {}
        xtok = {}
        for b in range(NB):
            t0 = b * BLK
            P.label(f"{phase}.b{b}")
            if src_kind == "hbm_f32":
                w_l = list(first_load_waits) if b <= 1 else []
                if b >= 2 and apdone.get(b - 2):
                    w_l.append(("V", apdone[b - 2]))
                if src_hbm is xT:
                    src_ap = xT[:, t0:t0 + BLK].rearrange("(c p) t -> p c t", p=128)
                else:
                    src_ap = src_hbm[:, :, t0:t0 + BLK]
                xtok[b] = P.dma_y(lambda y, src_ap=src_ap, b=b: y.dma_start(
                    xblk[b % 2][:, :].rearrange("p (c t) -> p c t", c=NF), src_ap),
                    waits=w_l)

            tok_mm_last = None
            nsq = xsq.shape[1] // BLK
            mm2_toks = {}
            for c in range(NF):
                csl = c % nsq
                if src_kind == "hbm_f32":
                    xs_f = xblk[b % 2][:, c * BLK:(c + 1) * BLK]
                    wsrc = [("DY", xtok[b])]
                    wreuse = [("T", mmlast_prev)] if mmlast_prev else []
                    tok_cast = P.v(lambda v, xs_f=xs_f, c=c: v.tensor_copy(
                        xbf[:, c * BLK:(c + 1) * BLK], xs_f), waits=wsrc + wreuse)
                    sx_src = xbf[:, c * BLK:(c + 1) * BLK]
                    sx_wait = [("V", tok_cast)]
                else:
                    xs_f = xn[:, c * T + t0: c * T + t0 + BLK]
                    wsrc = [("V", pre_tok)] if pre_tok else []
                    wreuse = [("T", mmlast_prev)] if mmlast_prev else []
                    sx_src = xs_f
                    sx_wait = list(wsrc)
                wsq = [("T", mm2_toks[c - nsq])] if (c - nsq) in mm2_toks else wreuse
                tok_sq = P.s(lambda s, xs_f=xs_f, csl=csl: s.activation(
                    xsq[:, csl * BLK:(csl + 1) * BLK], xs_f, AF.Square),
                    waits=wsrc + wsq)
                bw = BK.wr([4]) if c == 0 else []
                tok_mm1 = P.t(lambda t, sx_src=sx_src, c=c: t.matmul(
                    PSB[0:1, 0:BLK], ones_colb[:, :], sx_src,
                    start=(c == 0), stop=(c == NF - 1)),
                    waits=sx_wait + bw + ([("G", tok_init_g)] if b == 0 else []))
                bw2 = BK.wr([5]) if c == 0 else []
                tok_mm2 = P.t(lambda t, csl=csl, c=c: t.matmul(
                    PSB[0:1, BLK:2 * BLK], ones_colb[:, :],
                    xsq[:, csl * BLK:(csl + 1) * BLK],
                    start=(c == 0), stop=(c == NF - 1)),
                    waits=[("S", tok_sq)] + bw2)
                mm2_toks[c] = tok_mm2
                tok_mm_last = tok_mm2
            mmlast_prev = tok_mm_last

            mu = strow[0:1, 0:BLK]
            m2e = strow[0:1, BLK:2 * BLK]
            musq = strow[0:1, 2 * BLK:3 * BLK]
            varr = strow[0:1, BLK:2 * BLK]
            stdr = strow[0:1, 2 * BLK:3 * BLK]
            arow = strow[0:1, 3 * BLK:4 * BLK]
            brow = strow[0:1, 4 * BLK:5 * BLK]
            wrow = [("T", finB_prev)] if finB_prev else []
            tk1 = P.v(lambda v: v.tensor_scalar_mul(mu, PSB[0:1, 0:BLK], 1.0 / D),
                      waits=[("T", tok_mm_last)] + wrow)
            tk2 = P.v(lambda v: v.tensor_scalar(m2e, PSB[0:1, BLK:2 * BLK],
                                                1.0 / D, EPS, op0=ALU.mult, op1=ALU.add))
            tk3 = P.v(lambda v: v.tensor_mul(musq, mu, mu))
            tk4 = P.v(lambda v: v.tensor_sub(varr, m2e, musq))
            tk5 = P.s(lambda s: s.activation(stdr, varr, AF.Sqrt), waits=[("V", tk4)])
            tk6 = P.v(lambda v: v.reciprocal(arow, stdr), waits=[("S", tk5)])
            tk7 = P.v(lambda v: v.scalar_tensor_tensor(brow, mu, -1.0, arow,
                                                       op0=ALU.mult, op1=ALU.mult))
            BK.rd([4, 5], "V", tk2)

            tokA = P.t(lambda t: t.matmul(PSB[:, 2 * BLK:3 * BLK], ones_row1f[:, :],
                                          arow, start=True, stop=True),
                       waits=[("V", tk7)] + BK.wr([6]))
            tokB = P.t(lambda t: t.matmul(PSB[:, 3 * BLK:4 * BLK], ones_row1f[:, :],
                                          brow, start=True, stop=True),
                       waits=[("V", tk7)] + BK.wr([7]))
            finB_prev = tokB

            nchunks = NF if apply_chunks is None else apply_chunks[b]
            tok_ap = None
            for c in range(nchunks):
                if src_kind == "hbm_f32":
                    xs_f = xblk[b % 2][:, c * BLK:(c + 1) * BLK]
                    w_ap = [("DY", xtok[b])]
                else:
                    xs_f = xn[:, c * T + t0: c * T + t0 + BLK]
                    w_ap = []
                tka = P.v(lambda v, xs_f=xs_f: v.tensor_mul(
                    scr[:, :], xs_f, PSB[:, 2 * BLK:3 * BLK]),
                    waits=[("T", tokA)] + w_ap)
                tok_ap = dst_write(b, c, tokB)
            BK.rd([6, 7], "V", tok_ap)
            apdone[b] = tok_ap
        return apdone[NB - 1]

    mm.mark()
    xblk = [mm.alloc(f"xblk{i}", [128, NF * BLK], F32) for i in range(2)]
    xbf = mm.alloc("xbf", [128, NF * BLK], BF16)
    xsq = mm.alloc("xsq", [128, NF * BLK], BF16)

    def ln0_write(b, c, tokB):
        t0 = b * BLK
        return P.v(lambda v: v.tensor_add(
            xn[:, c * T + t0: c * T + t0 + BLK], scr[:, :], PSB[:, 3 * BLK:4 * BLK]),
            waits=[("T", tokB)])

    P.label("P1.ln0")
    tok_ln0 = emit_ln("P1", "hbm_f32", ln0_write, xT, xblk, xbf, xsq)

    wstage = [mm.alloc(f"wstage{i}", [128, 4096], F32) for i in range(2)]
    P.label("P1.absmean")
    abs_done = {}
    piece_toks = []
    for (mat, hbm) in [("w1", dram["w1T"]), ("w2", dram["w2T"])]:
        for i in range(8):
            slot = wstage[len(piece_toks) % 2]
            if mat == "w1":
                src = hbm[i * 128:(i + 1) * 128, :]
            else:
                src = hbm[i * 512:(i + 1) * 512, :].rearrange("(c p) f -> p c f", p=128)
            wait_prev = [("V", piece_toks[-2])] if len(piece_toks) >= 2 else []
            dst = slot[:, :] if mat == "w1" else slot[:, :].rearrange(
                "p (c f) -> p c f", c=4)
            tok_l = P.dma_y(lambda y, dst=dst, src=src: y.dma_start(dst, src),
                            waits=wait_prev)
            P.v(lambda v, slot=slot: v.reduce_sum(
                tern_tmp[:, 0:1], slot[:, :], axis=AX.X, apply_absolute_value=True),
                waits=[("DY", tok_l)])
            tok_a = P.v(lambda v, acc=w_acc[mat]: v.tensor_add(
                acc[:, :], acc[:, :], tern_tmp[:, 0:1]))
            piece_toks.append(tok_a)
        abs_done[mat] = piece_toks[-1]
    tok_rs = {}
    for mat, nelem in [("w1", D * FF), ("w2", D * FF)]:
        tok_t = P.t(lambda t, acc=w_acc[mat]: t.matmul(
            PSB[:, 0:1], ones_colf[:, :], acc[:, :], start=True, stop=True),
            waits=[("V", abs_done[mat]), ("V", tok_ln0)] + BK.wr([4]))
        tk1 = P.v(lambda v, rs=w_rs[mat], nelem=nelem: v.tensor_scalar(
            rs[:, :], PSB[:, 0:1], 1.0 / nelem, 1e-8, op0=ALU.mult, op1=ALU.max),
            waits=[("T", tok_t)])
        tok_rs[mat] = P.v(lambda v, rs=w_rs[mat]: v.reciprocal(rs[:, :], rs[:, :]))
        BK.rd([4], "V", tk1)
    mm.release()

    def dump(nm, sb, tok, tok_sem="V"):
        if nm not in dbg_t:
            return
        for c in range(NF):
            P.dma_y(lambda y, c=c: y.dma_start(dbg_t[nm][:, c, :],
                                               sb[:, c * T:(c + 1) * T]),
                    waits=[(tok_sem, tok)])

    dump("xn", xn, tok_ln0)

    XB = [0, 8, 12, 14]
    NCF = [8, 4, 2, 1]

    def phase2():
        mm.mark()
        q2 = mm.alloc("q2", [128, NF * T], BF16)
        k2 = mm.alloc("k2", [128, NF * T], BF16)
        v2 = mm.alloc("v2", [128, NT * 1024], BF16)
        wringA = mm.alloc("wringA", [128, NF * 1024], BF16)
        wringB = mm.alloc("wringB", [128, NF * 1024], BF16)

        mm.mark()
        xin = mm.alloc("xin", [128, 15 * BLK], BF16)
        xsq2 = mm.alloc("xsq2", [128, 2 * BLK], BF16)
        scr2 = mm.alloc("scr2", [128, 512], F32)

        sq_rot = {}

        def lni_write(b, c, tokB):
            xoff = (XB[b] + c) * BLK
            P.v(lambda v: v.tensor_add(scr2[:, :], scr[:, :],
                                       PSB[:, 3 * BLK:4 * BLK]),
                waits=[("T", tokB)])
            return P.v(lambda v, xoff=xoff, c=c: v.tensor_scalar(
                xin[:, xoff:xoff + BLK], scr2[:, :],
                sv["nw"][:, c:c + 1], sv["nb"][:, c:c + 1],
                op0=ALU.mult, op1=ALU.add))

        P.label("P2a.lni")
        tok_xin = emit_ln("P2a", "xn_bf16", lni_write, None, None, None, xsq2,
                          apply_chunks=NCF, pre_tok=tok_ln0)

        def wload(dst, src, waits=()):
            return P.dma_g(lambda g: g.dma_start(
                dst[:, :].rearrange("p (c f) -> p c f", c=NF),
                src[:, :].rearrange("(c p) f -> p c f", p=128)), waits=waits)

        P.label("P2b.wload")
        tok_qw = wload(wringA, dram["qwT"])
        tok_kw = wload(wringB, dram["kwT"])

        P.label("P2b.proj")

        def proj_fm(dst, wring, tok_w, bias_cols, scale, extra_wait):
            last_mm = None
            tok_e = None
            si = 0
            for e in range(4):
                for oc in range(NF):
                    sl = si % 4
                    si += 1
                    ps = PSA[:, sl * 512:(sl + 1) * 512]
                    w0 = [("DG", tok_w), ("V", tok_xin)] + extra_wait + BK.wr([sl])
                    ncf = NCF[e]
                    for ci in range(ncf):
                        last_mm = P.t(lambda t, ci=ci, oc=oc, e=e, ps=ps, wring=wring,
                                      ncf=ncf: t.matmul(
                            ps, wring[:, ci * 1024 + oc * 128: ci * 1024 + oc * 128 + 128],
                            xin[:, (XB[e] + ci) * BLK:(XB[e] + ci + 1) * BLK],
                            start=(ci == 0), stop=(ci == ncf - 1)),
                            waits=w0 if ci == 0 else [])
                    tok_e = P.s(lambda s, e=e, oc=oc, ps=ps: s.activation(
                        dst[:, oc * T + e * BLK: oc * T + (e + 1) * BLK], ps, AF.Identity,
                        bias=bias_cols[:, oc:oc + 1], scale=scale),
                        waits=[("T", last_mm)])
                    BK.rd([sl], "S", tok_e)
            return last_mm, tok_e

        tok_q2mm, tok_q2ev = proj_fm(q2, wringA, tok_qw, sv["qb32"], SCALE, [])
        tok_k2mm, tok_k2ev = proj_fm(k2, wringB, tok_kw, sv["kb"], 1.0, [])

        tok_vw = wload(wringA, dram["vwT"], waits=[("T", tok_q2mm)])
        tok_ow = wload(wringB, dram["owT"], waits=[("T", tok_k2mm)])

        last_vmm = None
        vslot_tok = [None, None]
        for e in range(4):
            ncf = NCF[e]
            for tl in range(4):
                tt = e * 4 + tl
                sl = tt % 2
                ps = PSA[:, sl * 1024:(sl + 1) * 1024]
                w0 = [("DG", tok_vw), ("V", tok_xin)]
                if vslot_tok[sl] is not None:
                    w0 += [("V", vslot_tok[sl])]
                w0 += BK.wr([2 * sl, 2 * sl + 1])
                for ci in range(ncf):
                    for nb2 in range(2):
                        last_vmm = P.t(lambda t, ci=ci, e=e, tl=tl, nb2=nb2, ps=ps,
                                       ncf=ncf: t.matmul(
                            ps[:, nb2 * 512:(nb2 + 1) * 512],
                            xin[:, (XB[e] + ci) * BLK + tl * 128:
                                (XB[e] + ci) * BLK + tl * 128 + 128],
                            wringA[:, ci * 1024 + nb2 * 512: ci * 1024 + (nb2 + 1) * 512],
                            start=(ci == 0), stop=(ci == ncf - 1)),
                            waits=w0 if (ci == 0 and nb2 == 0) else [])
                tok_ev = P.v(lambda v, tt=tt, ps=ps: v.tensor_add(
                    v2[:, tt * 1024:(tt + 1) * 1024], ps, sv["vbbc"][:, :]),
                    waits=[("T", last_vmm)])
                vslot_tok[sl] = tok_ev
                BK.rd([2 * sl, 2 * sl + 1], "V", tok_ev)
        tok_v2mm = last_vmm
        mm.release()

        mm.mark()
        P_sb2 = mm.alloc("P_sb2", [128, 3 * T], BF16)
        aotm = mm.alloc("aotm", [128, 1024], BF16)
        aofm = mm.alloc("aofm", [128, NF * 512], BF16)
        g_blk = mm.alloc("g_blk", [128, 2 * 512], BF16)
        nm4 = mm.alloc("nm4", [128, 3 * 4], F32)
        nmx = mm.alloc("nmx", [128, 3], F32)
        sum2 = mm.alloc("sum2", [128, 3], F32)

        tok_gdma = {}
        tok_aofm = None
        exp_of = {}
        recip_of = {}
        ao_last = {}
        prev_outproj_mm = None
        sil_of = {}

        def stage_scores(gi):
            qt, h = divmod(gi, NF)
            w_sc = BK.wr([0, 1, 2, 3])
            if gi == 0:
                w_sc += [("S", tok_q2ev), ("S", tok_k2ev)]
            tok_mm = None
            for j in range(4):
                tok_mm = P.t(lambda t, h=h, qt=qt, j=j: t.matmul(
                    PSA[:, j * 512:(j + 1) * 512],
                    q2[:, h * T + qt * 128: h * T + qt * 128 + 128],
                    k2[:, h * T + j * 512: h * T + (j + 1) * 512],
                    start=True, stop=True),
                    waits=w_sc if j == 0 else [])
            return tok_mm

        def stage_a(gi, tok_mm):
            qt, h = divmod(gi, NF)
            sl = gi % 3
            rq = (qt % 2) * NF
            w_r = [("T", tok_mm)]
            if exp_of.get(gi - 3) is not None:
                w_r += [("S", exp_of[gi - 3])]
            tok_c = P.v(lambda v, sl=sl: v.reduce_max(
                nmx[:, sl:sl + 1], PSA[:, :], axis=AX.X, negate=True),
                waits=w_r)
            w_exp = [("V", tok_c)]
            if (gi - 3) in ao_last:
                w_exp += [("T", ao_last[gi - 3])]
            if (gi - 3) in recip_of:
                w_exp += [("V", recip_of[gi - 3])]
            if gi == 0:
                w_exp += [("T", tok_v2mm)]
            tok_exp = P.s(lambda s, sl=sl: s.activation(
                P_sb2[:, sl * T:(sl + 1) * T], PSA[:, :], AF.Exp,
                bias=nmx[:, sl:sl + 1], accum_out=sum2[:, sl:sl + 1]),
                waits=w_exp)
            exp_of[gi] = tok_exp
            BK.rd([0, 1, 2, 3], "S", tok_exp)
            recip_of[gi] = P.v(lambda v, h=h, sl=sl, rq=rq: v.reciprocal(
                recs[:, rq + h: rq + h + 1], sum2[:, sl:sl + 1]),
                waits=[("S", tok_exp)])

        def stage_b1(gi):
            qt, h = divmod(gi, NF)
            sl = gi % 3
            base = sl * T
            tok_wave = {}
            for w in range(2):
                last_tr = None
                for j in range(8):
                    kt = w * 8 + j
                    w_tr = (BK.wr([4 + w]) + [("S", exp_of[gi])]) if j == 0 else []
                    last_tr = P.t(lambda t, kt=kt, j=j, w=w, base=base: t.transpose(
                        PSBbf[:, w * 1024 + j * 128: w * 1024 + (j + 1) * 128],
                        P_sb2[:, base + kt * 128: base + (kt + 1) * 128],
                        ident[:, :]),
                        waits=w_tr)
                w_ev = [("T", last_tr)]
                if w == 0:
                    tok_wave[w] = P.s(lambda s, w=w, base=base: s.copy(
                        P_sb2[:, base + w * 1024: base + (w + 1) * 1024],
                        PSBbf[:, w * 1024:(w + 1) * 1024]),
                        waits=w_ev)
                    BK.rd([4 + w], "S", tok_wave[w])
                else:
                    tok_wave[w] = P.v(lambda v, w=w, base=base: v.tensor_copy(
                        P_sb2[:, base + w * 1024: base + (w + 1) * 1024],
                        PSBbf[:, w * 1024:(w + 1) * 1024]),
                        waits=w_ev)
                    BK.rd([4 + w], "V", tok_wave[w])
            return tok_wave

        def stage_b2(gi, tok_wave):
            qt, h = divmod(gi, NF)
            sl = gi % 3
            base = sl * T
            for kt in range(NT):
                w_ao = []
                if kt == 0:
                    w_ao = [("V", tok_wave[1]), ("S", tok_wave[0])] + \
                        (BK.wr([6, 7]) if h == 0 else [])
                ao_last[gi] = P.t(lambda t, kt=kt, h=h, base=base: t.matmul(
                    PSB[:, 1024 + h * 128: 1024 + (h + 1) * 128],
                    P_sb2[:, base + kt * 128: base + (kt + 1) * 128],
                    v2[:, kt * 1024 + h * 128: kt * 1024 + (h + 1) * 128],
                    start=(kt == 0), stop=(kt == NT - 1)),
                    waits=w_ao)

        def epilogue(qt):
            nonlocal tok_aofm, prev_outproj_mm
            P.label(f"P2c.epi{qt}")
            rq = (qt % 2) * NF
            tok_ev = None
            for h in range(NF):
                w_ev = [("T", ao_last[qt * NF + NF - 1])] if h == 0 else []
                tok_ev = P.v(lambda v, h=h, rq=rq: v.tensor_scalar(
                    aotm[:, h * 128:(h + 1) * 128],
                    PSB[:, 1024 + h * 128: 1024 + (h + 1) * 128],
                    recs[:, rq + h: rq + h + 1], None, op0=ALU.mult),
                    waits=w_ev)
            BK.rd([6, 7], "V", tok_ev)
            last_tr = None
            for fc in range(NF):
                w_tr = (BK.wr([4]) + [("V", tok_ev)]) if fc == 0 else []
                last_tr = P.t(lambda t, fc=fc: t.transpose(
                    PSBbf[:, fc * 128:(fc + 1) * 128],
                    aotm[:, fc * 128:(fc + 1) * 128],
                    ident[:, :]), waits=w_tr)
            j = qt % 4
            w_aofm = [("T", last_tr)]
            if j == 0 and prev_outproj_mm is not None:
                w_aofm += [("T", prev_outproj_mm)]
            tok_aofm = P.v(lambda v, j=j: v.tensor_copy(
                bass.AP(aofm, j * 128, [[NF * 512, 128], [512, NF], [1, 128]]),
                bass.AP(PSBbf, 0, [[4096, 128], [128, NF], [1, 128]])),
                waits=w_aofm)
            BK.rd([4], "V", tok_aofm)

        def emit_outproj(eb):
            nonlocal prev_outproj_mm
            mcnt = NCF[eb]
            P.label(f"P2d.blk{eb}")
            tok_sil = None
            for mh in range(NF):
                slg = mh % 2
                gk = (eb, mh - 2) if mh >= 2 else (eb - 1, mh + 6)
                wg = [("DY", tok_gdma[gk])] if gk in tok_gdma else []
                if mh < mcnt:
                    sl = mh % 4
                    ps = PSA[:, sl * 512:(sl + 1) * 512]
                    w0 = [("V", tok_aofm), ("DG", tok_ow)] + BK.wr([sl])
                    last_mm = None
                    for ci in range(mcnt):
                        last_mm = P.t(lambda t, ci=ci, mh=mh, ps=ps,
                                      mcnt=mcnt: t.matmul(
                            ps, wringB[:, ci * 1024 + mh * 128: ci * 1024 + mh * 128 + 128],
                            aofm[:, ci * 512:(ci + 1) * 512],
                            start=(ci == 0), stop=(ci == mcnt - 1)),
                            waits=w0 if ci == 0 else [])
                    prev_outproj_mm = last_mm
                    tok_t = P.v(lambda v, mh=mh, eb=eb, ps=ps: v.tensor_add(
                        scr[:, :], ps,
                        xn[:, mh * T + eb * BLK: mh * T + (eb + 1) * BLK]),
                        waits=[("T", last_mm)] +
                              ([("S", tok_sil)] if tok_sil else []))
                    BK.rd([sl], "V", tok_t)
                    tok_sil = P.s(lambda s, mh=mh, slg=slg: s.activation(
                        g_blk[:, slg * 512:(slg + 1) * 512], scr[:, :], AF.Silu,
                        bias=sv["ob"][:, mh:mh + 1]),
                        waits=[("V", tok_t)] + wg)
                else:
                    tok_sil = P.s(lambda s, mh=mh, eb=eb, slg=slg: s.activation(
                        g_blk[:, slg * 512:(slg + 1) * 512],
                        xn[:, mh * T + eb * BLK: mh * T + (eb + 1) * BLK], AF.Silu),
                        waits=wg)
                tok_gdma[(eb, mh)] = P.dma_y(lambda y, eb=eb, mh=mh, slg=slg:
                    y.dma_start(g_hbm[:, mh, eb * BLK:(eb + 1) * BLK],
                                g_blk[:, slg * 512:(slg + 1) * 512]),
                    waits=[("S", tok_sil)])

        def after_b(gj):
            qtj, hj = divmod(gj, NF)
            if hj == NF - 1:
                epilogue(qtj)
                if qtj % 4 == 3:
                    emit_outproj(qtj // 4)

        for qt in range(NT):
            P.label(f"P2c.qt{qt}")
            for h in range(NF):
                gi = qt * NF + h
                if gi >= 2:
                    tw = stage_b1(gi - 2)
                    tok_mm = stage_scores(gi)
                    stage_b2(gi - 2, tw)
                    after_b(gi - 2)
                    stage_a(gi, tok_mm)
                else:
                    tok_mm = stage_scores(gi)
                    stage_a(gi, tok_mm)
        for gj in (NT * NF - 2, NT * NF - 1):
            tw = stage_b1(gj)
            stage_b2(gj, tw)
            after_b(gj)

        mm.release()
        mm.release()
        return tok_gdma[(NB - 1, NF - 1)]

    tok_p2 = None
    if BUILD_UPTO >= 2:
        tok_p2 = phase2()
        if "g" in dbg_t:
            for c in range(NF):
                P.dma_y(lambda y, c=c: y.dma_start(
                    dbg_t["g"][:, c, :], g_hbm[:, c, :]), waits=[("DY", tok_p2)])

    def phase3(tok_g_last):
        mm.mark()
        Qf = mm.alloc("Qf", [128, NF * T], BF16)
        Kf = mm.alloc("Kf", [128, NF * T], BF16)
        Vt = mm.alloc("Vt", [128, NT * 1024], BF16)

        mm.mark()
        wringA = mm.alloc("wringA3", [128, NF * 1024], BF16)
        wringB = mm.alloc("wringB3", [128, NF * 1024], BF16)

        def wload(dst, src, waits=()):
            return P.dma_g(lambda g: g.dma_start(
                dst[:, :].rearrange("p (c f) -> p c f", c=NF),
                src[:, :].rearrange("(c p) f -> p c f", p=128)), waits=waits)

        P.label("P3a.proj")
        w_alias = [("DY", tok_g_last)] if tok_g_last else []
        tok_wq = wload(wringA, dram["wqT"], waits=w_alias)
        tok_wk = wload(wringB, dram["wkT"], waits=w_alias)

        def proj_full_fm(dst, wring, tok_w, scale):
            last_mm = None
            tok_e = None
            si = 0
            for oc in range(NF):
                for nb in range(NB):
                    sl = si % 4
                    si += 1
                    ps = PSA[:, sl * 512:(sl + 1) * 512]
                    w0 = [("DG", tok_w), ("V", tok_ln0)] + BK.wr([sl])
                    for ci in range(NF):
                        last_mm = P.t(lambda t, ci=ci, oc=oc, nb=nb, ps=ps,
                                      wring=wring: t.matmul(
                            ps, wring[:, ci * 1024 + oc * 128: ci * 1024 + oc * 128 + 128],
                            xn[:, ci * T + nb * BLK: ci * T + (nb + 1) * BLK],
                            start=(ci == 0), stop=(ci == NF - 1)),
                            waits=w0 if ci == 0 else [])
                    tok_e = P.s(lambda s, oc=oc, nb=nb, ps=ps: s.activation(
                        dst[:, oc * T + nb * BLK: oc * T + (nb + 1) * BLK], ps,
                        AF.Identity, scale=scale),
                        waits=[("T", last_mm)])
                    BK.rd([sl], "S", tok_e)
            return last_mm, tok_e

        tok_qmm, tok_qev = proj_full_fm(Qf, wringA, tok_wq, SCALE)
        tok_kmm, tok_kev = proj_full_fm(Kf, wringB, tok_wk, 1.0)

        tok_wv = wload(wringA, dram["wvT"], waits=[("T", tok_qmm)])
        last_vmm = None
        tok_vev = None
        for tt in range(NT):
            sl = tt % 2
            ps = PSA[:, sl * 1024:(sl + 1) * 1024]
            w0 = [("DG", tok_wv), ("V", tok_ln0)] + BK.wr([2 * sl, 2 * sl + 1])
            for ci in range(NF):
                for nb2 in range(2):
                    last_vmm = P.t(lambda t, ci=ci, tt=tt, nb2=nb2, ps=ps: t.matmul(
                        ps[:, nb2 * 512:(nb2 + 1) * 512],
                        xn[:, ci * T + tt * 128: ci * T + tt * 128 + 128],
                        wringA[:, ci * 1024 + nb2 * 512: ci * 1024 + (nb2 + 1) * 512],
                        start=(ci == 0), stop=(ci == NF - 1)),
                        waits=w0 if (ci == 0 and nb2 == 0) else [])
            tok_vev = P.v(lambda v, tt=tt, ps=ps: v.tensor_copy(
                Vt[:, tt * 1024:(tt + 1) * 1024], ps),
                waits=[("T", last_vmm)])
            BK.rd([2 * sl, 2 * sl + 1], "V", tok_vev)
        tok_vtmm = last_vmm
        mm.release()

        mm.mark()
        P_sb3 = mm.alloc("P_sb3", [128, T], BF16)
        PT_blk = mm.alloc("PT_blk", [128, NT * 512], BF16)
        x_blk3 = mm.alloc("x_blk3", [128, NF * 512], F32)
        g_rblk = mm.alloc("g_rblk", [128, NF * 512], BF16)
        x1sl = mm.alloc("x1sl", [128, 2 * 512], F32)
        rec1 = mm.alloc("rec1", [128, 1], F32)

        prev_exp = P.counts.get("S", 0)
        prev_recip = None
        prev_norm = None
        prev_lasttr = None
        prev_av = {}
        tok_x1dma = {}
        for b in range(NB):
            P.label(f"P3b.blk{b}")
            if b == 0:
                w_x = [("T", tok_vtmm)]
                w_g = [("T", tok_vtmm)]
            else:
                w_x = [("V", prev_x1_used)]
                w_g = [("V", prev_g_used)]
            tok_xb = P.dma_y(lambda y, b=b: y.dma_start(
                x_blk3[:, :].rearrange("p (c t) -> p c t", c=NF),
                xT[:, b * BLK:(b + 1) * BLK].rearrange("(c p) t -> p c t", p=128)),
                waits=w_x)
            tok_gb = P.dma_y(lambda y, b=b: y.dma_start(
                g_rblk[:, :].rearrange("p (c t) -> p c t", c=NF),
                g_hbm[:, :, b * BLK:(b + 1) * BLK]),
                waits=([("DY", tok_g_last)] if tok_g_last else []) + w_g)
            tok_z = None
            for kt in range(4 * b, 4 * b + 4):
                w_z = []
                if b > 0 and kt == 4 * b:
                    w_z = [("T", prev_av.get(b - 1))]
                tok_z = P.v(lambda v, kt=kt: v.memset(
                    PT_blk[:, kt * 512:(kt + 1) * 512], 0.0), waits=w_z)

            for j in range(4):
                qt = 4 * b + j
                nkt = qt + 1
                ncols = nkt * 128
                nseg = (ncols + 511) // 512
                for sg in range(nseg):
                    wdt = min(512, ncols - sg * 512)
                    w_sc = BK.wr([sg])
                    if qt == 0:
                        w_sc += [("S", tok_qev), ("S", tok_kev)]
                    for ci in range(NF):
                        last_sc = P.t(lambda t, ci=ci, qt=qt, sg=sg, wdt=wdt: t.matmul(
                            PSA[:, sg * 512: sg * 512 + wdt],
                            Qf[:, ci * T + qt * 128: ci * T + qt * 128 + 128],
                            Kf[:, ci * T + sg * 512: ci * T + sg * 512 + wdt],
                            start=(ci == 0), stop=(ci == NF - 1)),
                            waits=w_sc if ci == 0 else [])
                tok_mask = P.v(lambda v, ncols=ncols: v.tensor_add(
                    PSA[:, ncols - 128:ncols], PSA[:, ncols - 128:ncols], maskA[:, :]),
                    waits=[("T", last_sc)])
                tok_nm = P.v(lambda v, ncols=ncols: v.reduce_max(
                    negmax[:, :], PSA[:, 0:ncols], axis=AX.X, negate=True),
                    waits=([("S", prev_exp)] if prev_exp else []))
                w_exp = [("V", tok_nm)]
                if prev_norm:
                    w_exp += [("V", prev_norm)]
                if prev_lasttr:
                    w_exp += [("T", prev_lasttr)]
                tok_exp = P.s(lambda s, ncols=ncols: s.activation(
                    P_sb3[:, 0:ncols], PSA[:, 0:ncols], AF.Exp, bias=negmax[:, :],
                    accum_out=sums[:, :]), waits=w_exp)
                prev_exp = tok_exp
                BK.rd(list(range(nseg)), "S", tok_exp)
                prev_recip = P.v(lambda v: v.reciprocal(rec1[:, :], sums[:, :]),
                                 waits=[("S", tok_exp)])
                prev_norm = P.v(lambda v, ncols=ncols: v.tensor_scalar(
                    P_sb3[:, 0:ncols], P_sb3[:, 0:ncols], rec1[:, :], None,
                    op0=ALU.mult))
                nw = (nkt + 7) // 8
                for w in range(nw):
                    cnt = min(8, nkt - w * 8)
                    last_tr = None
                    for jj in range(cnt):
                        kt = w * 8 + jj
                        w_tr = (BK.wr([4 + (w % 2)]) + [("V", prev_norm)]) if jj == 0 else []
                        last_tr = P.t(lambda t, kt=kt, jj=jj, w=w: t.transpose(
                            PSBbf[:, (w % 2) * 1024 + jj * 128: (w % 2) * 1024 + (jj + 1) * 128],
                            P_sb3[:, kt * 128:(kt + 1) * 128], ident[:, :]),
                            waits=w_tr)
                    prev_lasttr = last_tr
                    tok_evw = P.v(lambda v, w=w, cnt=cnt, j=j: v.tensor_copy(
                        bass.AP(PT_blk, (w * 8) * 512 + j * 128,
                                [[NT * 512, 128], [512, cnt], [1, 128]]),
                        bass.AP(PSBbf, (w % 2) * 1024,
                                [[4096, 128], [128, cnt], [1, 128]])),
                        waits=[("T", last_tr)])
                    BK.rd([4 + (w % 2)], "V", tok_evw)

            if "pt0" in dbg_t and b == 0:
                P.dma_y(lambda y: y.dma_start(dbg_t["pt0"][:, 0, 0:2048],
                                              PT_blk[:, 0:2048]),
                        waits=[("V", tok_evw)])
                P.dma_y(lambda y: y.dma_start(dbg_t["pt0"][:, 1, 0:512],
                                              P_sb3[:, 0:512]),
                        waits=[("V", tok_evw)])

            P.label(f"P3c.blk{b}")
            nkt_b = 4 * b + 4
            tok_ps_done = {}
            for fc in range(NF):
                bank = fc
                psv = (PSA if fc < 4 else PSB)[:, (fc % 4) * 512:(fc % 4 + 1) * 512]
                w_av = BK.wr([bank]) + ([("V", tok_evw)] if fc == 0 else [])
                last_av = None
                for kt in range(nkt_b):
                    last_av = P.t(lambda t, kt=kt, fc=fc, psv=psv: t.matmul(
                        psv, Vt[:, kt * 1024 + fc * 128: kt * 1024 + (fc + 1) * 128],
                        PT_blk[:, kt * 512:(kt + 1) * 512],
                        start=(kt == 0), stop=(kt == nkt_b - 1)),
                        waits=w_av if kt == 0 else [])
                tok_ps_done[fc] = last_av
            prev_av[b] = tok_ps_done[NF - 1]
            tok_dump = None
            if "s0blk" in dbg_t and b == 0:
                for fc in range(NF):
                    psv = (PSA if fc < 4 else PSB)[:, (fc % 4) * 512:(fc % 4 + 1) * 512]
                    tkd = P.v(lambda v, fc=fc, psv=psv: v.tensor_copy(
                        x1sl[:, 0:512], psv),
                        waits=[("T", tok_ps_done[fc])] +
                              ([("DY", tok_dump)] if tok_dump else []))
                    tok_dump = P.dma_y(lambda y, fc=fc: y.dma_start(
                        dbg_t["s0blk"][:, fc, 0:512], x1sl[:, 0:512]),
                        waits=[("V", tkd)])
                    BK.rd([fc], "V", tkd)
            for fc in range(NF):
                psv = (PSA if fc < 4 else PSB)[:, (fc % 4) * 512:(fc % 4 + 1) * 512]
                ssl = x1sl[:, (fc % 2) * 512:(fc % 2 + 1) * 512]
                w_c = [("T", tok_ps_done[fc]), ("DY", tok_gb)]
                if tok_dump and fc < 2:
                    w_c += [("DY", tok_dump)]
                pk = (b, fc - 2) if fc >= 2 else (b - 1, fc + 6)
                if pk in tok_x1dma:
                    w_c += [("DY", tok_x1dma[pk])]
                tok_t1 = P.v(lambda v, fc=fc, psv=psv: v.tensor_mul(
                    scr[:, :], psv, g_rblk[:, fc * 512:(fc + 1) * 512]), waits=w_c)
                tok_t2 = P.v(lambda v, fc=fc, ssl=ssl: v.tensor_add(
                    ssl, scr[:, :], x_blk3[:, fc * 512:(fc + 1) * 512]),
                    waits=[("DY", tok_xb)])
                BK.rd([fc], "V", tok_t1)
                if "x1d" in dbg_t:
                    P.dma_y(lambda y, fc=fc, b=b, ssl=ssl: y.dma_start(
                        dbg_t["x1d"][:, fc, b * BLK:(b + 1) * BLK], ssl),
                        waits=[("V", tok_t2)])
                tok_x1dma[(b, fc)] = P.dma_y(lambda y, fc=fc, b=b, ssl=ssl: y.dma_start(
                    x1_hbm[:, fc, b * BLK:(b + 1) * BLK], ssl),
                    waits=[("V", tok_t2)])
            prev_x1_used = tok_t2
            prev_g_used = tok_t1
        mm.release()
        mm.release()
        return tok_x1dma[(NB - 1, NF - 1)]

    tok_p3 = None
    if BUILD_UPTO >= 3:
        tok_p3 = phase3(tok_p2)
        if "x1" in dbg_t:
            last = None
            for c in range(NF):
                last = P.dma_y(lambda y, c=c: y.dma_start(
                    dbg_t["x1"][:, c, :], x1_hbm[:, c, :]), waits=[("DY", tok_p3)])

    def phase4(tok_x1_last):
        mm.mark()
        xblk4 = [mm.alloc(f"xblk4_{i}", [128, NF * BLK], F32) for i in range(2)]
        xbf4 = mm.alloc("xbf4", [128, NF * BLK], BF16)
        xsq4 = mm.alloc("xsq4", [128, NF * BLK], BF16)
        last_store = [None]

        def ln1_write(b, c, tokB):
            t0 = b * BLK
            tk = P.v(lambda v: v.tensor_add(
                xn[:, c * T + t0: c * T + t0 + BLK], scr[:, :],
                PSB[:, 3 * BLK:4 * BLK]),
                waits=[("T", tokB)])
            last_store[0] = P.dma_y(lambda y, c=c, t0=t0: y.dma_start(
                xn1_hbm[:, c, t0:t0 + BLK], xn[:, c * T + t0: c * T + t0 + BLK]),
                waits=[("V", tk)])
            return tk

        P.label("P4.ln1")
        tok_ln1 = emit_ln("P4", "hbm_f32", ln1_write, x1_hbm, xblk4, xbf4, xsq4,
                          first_load_waits=[("DY", tok_x1_last)])
        mm.release()
        return tok_ln1, last_store[0]

    tok_p4 = tok_p4_store = None
    if BUILD_UPTO >= 4:
        tok_p4, tok_p4_store = phase4(tok_p3)
        if "xn1" in dbg_t:
            for c in range(NF):
                P.dma_y(lambda y, c=c: y.dma_start(
                    dbg_t["xn1"][:, c, :], xn[:, c * T:(c + 1) * T]),
                    waits=[("V", tok_p4)])

    def phase5(tok_ln1, tok_ln1_store):
        mm.mark()
        q1 = mm.alloc("q1", [128, NT * 1024], BF16)
        k1 = mm.alloc("k1", [128, NT * 1024], BF16)
        v1 = mm.alloc("v1", [128, NT * 1024], BF16)
        woring = mm.alloc("woring", [128, NF * 1024], BF16)
        st_q = mm.alloc("st_q", [128, 2 * 256], F32)
        st_k = mm.alloc("st_k", [128, 2 * 256], F32)
        rstd_q = mm.alloc("rstd_q", [128, 256], F32)
        rstd_k = mm.alloc("rstd_k", [128, 256], F32)
        qsq5 = mm.alloc("qsq5", [128, 1024], BF16)
        sc5 = mm.alloc("sc5", [128, 2 * 1024], F32)
        qf_c = mm.alloc("qf_c", [128, 1024], BF16)
        kf_c = mm.alloc("kf_c", [128, 1024], BF16)
        st_c = mm.alloc("st_c", [128, 2048], BF16)
        S_f32 = mm.alloc("S_f32", [128, NF * 64], F32)
        S_bf = mm.alloc("S_bf", [128, NF * 64], BF16)
        ofm = mm.alloc("ofm", [128, NF * BLK], BF16)
        s1stg = mm.alloc("s1stg", [128, 2 * 512], F32)

        P.label("P5.proj")
        w5 = [("DY", tok_ln1_store)]
        mm2 = mm

        def wload5(src, waits=()):
            return P.dma_g(lambda g: g.dma_start(
                woring[:, :].rearrange("p (c f) -> p c f", c=NF),
                src[:, :].rearrange("(c p) f -> p c f", p=128)), waits=waits)

        def proj_tm(dst, wsrc, wait_w):
            tok_w = wload5(wsrc, waits=wait_w)
            last_mm = None
            tok_ev = None
            for tt in range(NT):
                sl = tt % 2
                ps = PSA[:, sl * 1024:(sl + 1) * 1024]
                w0 = [("DG", tok_w), ("V", tok_ln1)] + BK.wr([2 * sl, 2 * sl + 1])
                for ci in range(NF):
                    for nb2 in range(2):
                        last_mm = P.t(lambda t, ci=ci, tt=tt, nb2=nb2, ps=ps: t.matmul(
                            ps[:, nb2 * 512:(nb2 + 1) * 512],
                            xn[:, ci * T + tt * 128: ci * T + tt * 128 + 128],
                            woring[:, ci * 1024 + nb2 * 512: ci * 1024 + (nb2 + 1) * 512],
                            start=(ci == 0), stop=(ci == NF - 1)),
                            waits=w0 if (ci == 0 and nb2 == 0) else [])
                tok_ev = P.v(lambda v, tt=tt, ps=ps: v.tensor_copy(
                    dst[:, tt * 1024:(tt + 1) * 1024], ps),
                    waits=[("T", last_mm)])
                BK.rd([2 * sl, 2 * sl + 1], "V", tok_ev)
            return last_mm, tok_ev

        tok_q1mm, tok_q1ev = proj_tm(q1, dram["w1qT"], w5)
        tok_k1mm, tok_k1ev = proj_tm(k1, dram["w1kT"], [("T", tok_q1mm)])
        tok_v1mm, tok_v1ev = proj_tm(v1, dram["w1vT"], [("T", tok_k1mm)])
        tok_wo = wload5(dram["w1oT"], waits=[("T", tok_v1mm)])

        P.label("P5.hln")

        def head_ln(src, st, rstd, gamma, beta, pre):
            tok_red = None
            for tt in range(NT):
                tile = src[:, tt * 1024:(tt + 1) * 1024]
                tok_red = P.v(lambda v, tile=tile, tt=tt: v.reduce_sum(
                    st[:, tt * 16:(tt + 1) * 16],
                    tile.rearrange("p (h d) -> p h d", d=64), axis=AX.X),
                    waits=[("V", pre)] if tt == 0 else [])
                tok_sq = P.s(lambda s, tile=tile: s.activation(
                    qsq5[:, :], tile, AF.Square),
                    waits=[("V", tok_red)])
                tok_red = P.v(lambda v, tt=tt: v.reduce_sum(
                    st[:, 256 + tt * 16: 256 + (tt + 1) * 16],
                    qsq5[:, :].rearrange("p (h d) -> p h d", d=64), axis=AX.X),
                    waits=[("S", tok_sq)])
            P.v(lambda v: v.tensor_scalar_mul(st[:, 0:256], st[:, 0:256], 1.0 / 64))
            P.v(lambda v: v.tensor_scalar(rstd[:, :], st[:, 256:512], 1.0 / 64, EPS,
                                          op0=ALU.mult, op1=ALU.add))
            P.v(lambda v: v.scalar_tensor_tensor(sc5[:, 0:256], st[:, 0:256], 1.0,
                                                 st[:, 0:256], op0=ALU.mult,
                                                 op1=ALU.mult))
            tk = P.v(lambda v: v.tensor_sub(rstd[:, :], rstd[:, :], sc5[:, 0:256]))
            tk2 = P.s(lambda s: s.activation(rstd[:, :], rstd[:, :], AF.Sqrt),
                      waits=[("V", tk)])
            tok_fin = P.v(lambda v: v.reciprocal(rstd[:, :], rstd[:, :]),
                          waits=[("S", tk2)])
            tok_ap = None
            for tt in range(NT):
                tile = src[:, tt * 1024:(tt + 1) * 1024]
                muAP = bass.AP(st, tt * 16, [[512, 128], [1, 16], [0, 64]])
                aAP = bass.AP(rstd, tt * 16, [[256, 128], [1, 16], [0, 64]])
                gAP = bass.AP(gamma, 0, [[64, 128], [0, 16], [1, 64]])
                bAP = bass.AP(beta, 0, [[64, 128], [0, 16], [1, 64]])
                t3d = sc5[:, 0:1024].rearrange("p (h d) -> p h d", d=64)
                t3d2 = sc5[:, 1024:2048].rearrange("p (h d) -> p h d", d=64)
                P.v(lambda v, tile=tile, muAP=muAP, t3d=t3d: v.tensor_tensor(
                    t3d, tile.rearrange("p (h d) -> p h d", d=64), muAP,
                    op=ALU.subtract),
                    waits=[("V", tok_fin)] if tt == 0 else [])
                P.v(lambda v, aAP=aAP, t3d=t3d, t3d2=t3d2: v.tensor_tensor(
                    t3d2, t3d, aAP, op=ALU.mult))
                P.v(lambda v, gAP=gAP, t3d=t3d, t3d2=t3d2: v.tensor_tensor(
                    t3d, t3d2, gAP, op=ALU.mult))
                tok_ap = P.v(lambda v, tile=tile, bAP=bAP, t3d=t3d: v.tensor_tensor(
                    tile.rearrange("p (h d) -> p h d", d=64), t3d, bAP, op=ALU.add))
            return tok_ap

        import os
        p5lim = os.environ.get("P5_LIMIT", "full")
        if p5lim == "proj":
            mm.release()
            return tok_v1ev and None or tok_v1ev

        tok_q1n = head_ln(q1, st_q, rstd_q, sv["qg"], sv["qbe"], tok_q1ev)
        tok_k1n = head_ln(k1, st_k, rstd_k, sv["kg"], sv["kbe"], tok_k1ev)
        if p5lim == "hln":
            mm.release()
            return None

        P.v(lambda v: v.memset(S_f32[:, :], 0.0))
        tok_s0 = P.v(lambda v: v.memset(S_bf[:, :], 0.0))

        P.label("P5.attn")
        tok_s1dma = None
        prev_ofm_ev = None
        prev_o_mm = {}
        prev_sb_all = None
        s1dma_slot = {}
        tok_wo_used = None
        ncchunks = {"attn1": 1, "attn4": 4}.get(p5lim, NT)
        for c in range(ncchunks):
            toks_f = {}
            for (nm, src, dstb) in [("q", q1, qf_c), ("k", k1, kf_c)]:
                last_tr = None
                bank = 4 if nm == "q" else 5
                for hp in range(NF):
                    w_tr = []
                    if hp == 0:
                        w_tr = BK.wr([bank]) + [("V", tok_k1n), ("V", tok_s0)]
                        if (c - 1) in prev_o_mm:
                            w_tr += [("T", prev_o_mm[c - 1])]
                    last_tr = P.t(lambda t, c=c, hp=hp, src=src, bank=bank: t.transpose(
                        PSBbf[:, (bank - 4) * 1024 + hp * 128: (bank - 4) * 1024 + (hp + 1) * 128],
                        src[:, c * 1024 + hp * 128: c * 1024 + (hp + 1) * 128],
                        ident[:, :]),
                        waits=w_tr)
                toks_f[nm] = P.v(lambda v, dstb=dstb, bank=bank: v.tensor_copy(
                    dstb[:, :],
                    PSBbf[:, (bank - 4) * 1024:(bank - 4) * 1024 + 1024]),
                    waits=[("T", last_tr)])
                BK.rd([bank], "V", toks_f[nm])

            if p5lim == "tr":
                prev_o_mm[c] = P.counts["T"]
                continue
            w_st0 = BK.wr([0, 1, 2, 3]) + [("V", toks_f["q"]), ("V", toks_f["k"])]
            for hp in range(NF):
                kA = kf_c[0:64, hp * 128:(hp + 1) * 128]
                qA = qf_c[0:64, hp * 128:(hp + 1) * 128]
                P.t(lambda t, hp=hp, kA=kA, qA=qA: t.matmul(
                    PSA[:, hp * 256: hp * 256 + 128], kA, qA,
                    start=True, stop=True), waits=w_st0 if hp == 0 else [])
            last_stB = None
            for hp in range(NF):
                kB = kf_c[64:128, hp * 128:(hp + 1) * 128]
                qB = qf_c[64:128, hp * 128:(hp + 1) * 128]
                last_stB = P.t(lambda t, hp=hp, kB=kB, qB=qB: t.matmul(
                    PSA[:, hp * 256 + 128: hp * 256 + 256], kB, qB,
                    start=True, stop=True, tile_position=(64, 0)))
            mAP = bass.AP(maskMT, 0, [[128, 128], [0, 16], [1, 128]])
            w_msk = [("T", last_stB)]
            if (c - 1) in prev_o_mm:
                w_msk += [("T", prev_o_mm[c - 1])]
            tok_msk = P.v(lambda v, mAP=mAP: v.tensor_tensor(
                st_c[:, :].rearrange("p (a t) -> p a t", t=128),
                PSA[:, :].rearrange("p (a t) -> p a t", t=128),
                mAP, op=ALU.mult), waits=w_msk)
            BK.rd([0, 1, 2, 3], "V", tok_msk)
            for hp in range(NF):
                qA = qf_c[0:64, hp * 128:(hp + 1) * 128]
                qB = qf_c[64:128, hp * 128:(hp + 1) * 128]
                stc = st_c[:, hp * 256:(hp + 1) * 256]
                pso = PSB[:, 1024 + hp * 128: 1024 + (hp + 1) * 128]
                w_o = BK.wr([6 + hp // 4]) if hp in (0, 4) else []
                vA = v1[:, c * 1024 + hp * 128: c * 1024 + hp * 128 + 64]
                vB = v1[:, c * 1024 + hp * 128 + 64: c * 1024 + (hp + 1) * 128]
                tok_o1 = P.t(lambda t, pso=pso, vA=vA, stc=stc: t.matmul(
                    pso[0:64, :], vA, stc[:, 0:128], start=True, stop=False),
                    waits=w_o + ([("V", tok_msk)] if hp == 0 else []))
                tok_o2 = P.t(lambda t, pso=pso, vB=vB, stc=stc: t.matmul(
                    pso[64:128, :], vB, stc[:, 128:256], start=True, stop=False,
                    tile_position=(0, 64)))
                sA = S_bf[0:64, hp * 64:(hp + 1) * 64]
                sB = S_bf[64:128, hp * 64:(hp + 1) * 64]
                w_o3 = [("V", prev_sb_all)] if (prev_sb_all and hp == 0) else []
                tok_o3 = P.t(lambda t, pso=pso, sA=sA, qA=qA: t.matmul(
                    pso[0:64, :], sA, qA, start=False, stop=True), waits=w_o3)
                tok_o4 = P.t(lambda t, pso=pso, sB=sB, qB=qB: t.matmul(
                    pso[64:128, :], sB, qB, start=False, stop=True,
                    tile_position=(64, 64)))
                prev_o_mm[c] = tok_o4
            w_su0 = BK.wr([5])
            last_su = None
            for hp in range(NF):
                vA = v1[:, c * 1024 + hp * 128: c * 1024 + hp * 128 + 64]
                vB = v1[:, c * 1024 + hp * 128 + 64: c * 1024 + (hp + 1) * 128]
                kAt = k1[:, c * 1024 + hp * 128: c * 1024 + hp * 128 + 64]
                kBt = k1[:, c * 1024 + hp * 128 + 64: c * 1024 + (hp + 1) * 128]
                pssu = PSB[:, 512 + hp * 64: 512 + (hp + 1) * 64]
                P.t(lambda t, pssu=pssu, kAt=kAt, vA=vA: t.matmul(
                    pssu[0:64, :], kAt, vA, start=True, stop=True),
                    waits=w_su0 if hp == 0 else [])
                last_su = P.t(lambda t, pssu=pssu, kBt=kBt, vB=vB: t.matmul(
                    pssu[64:128, :], kBt, vB, start=True, stop=True,
                    tile_position=(0, 64)))
            tok_sa = P.v(lambda v: v.tensor_add(
                S_f32[:, :], S_f32[:, :], PSB[:, 512:1024]),
                waits=[("T", last_su), ("T", prev_o_mm[c])])
            prev_sb_all = P.v(lambda v: v.tensor_copy(S_bf[:, :], S_f32[:, :]))
            BK.rd([5], "V", tok_sa)
            w_ev = [("T", prev_o_mm[c])]
            if c % 4 == 0 and tok_wo_used is not None:
                w_ev += [("T", tok_wo_used)]
            tok_oev = P.v(lambda v, c=c: v.tensor_copy(
                bass.AP(ofm, (c % 4) * 128, [[NF * BLK, 128], [BLK, NF], [1, 128]]),
                bass.AP(PSB, 1024, [[2048, 128], [128, NF], [1, 128]])),
                waits=w_ev)
            BK.rd([6, 7], "V", tok_oev)

            if c % 4 == 3:
                bb = c // 4
                P.label(f"P5.wo{bb}")
                for oc in range(NF):
                    sl = oc % 2
                    ps = PSB[:, sl * 512:(sl + 1) * 512]
                    w0 = [("V", tok_oev), ("DG", tok_wo)] + BK.wr([4 + sl])
                    last_mm = None
                    for ci in range(NF):
                        last_mm = P.t(lambda t, ci=ci, oc=oc, ps=ps: t.matmul(
                            ps, woring[:, ci * 1024 + oc * 128: ci * 1024 + oc * 128 + 128],
                            ofm[:, ci * BLK:(ci + 1) * BLK],
                            start=(ci == 0), stop=(ci == NF - 1)),
                            waits=w0 if ci == 0 else [])
                    tok_wo_used = last_mm
                    w_se = [("T", last_mm)]
                    if s1dma_slot.get(oc % 2):
                        w_se += [("DY", s1dma_slot[oc % 2])]
                    tok_sev = P.v(lambda v, oc=oc, ps=ps: v.tensor_copy(
                        s1stg[:, (oc % 2) * 512:(oc % 2 + 1) * 512], ps),
                        waits=w_se)
                    BK.rd([4 + sl], "V", tok_sev)
                    tok_s1dma = P.dma_y(lambda y, oc=oc, bb=bb: y.dma_start(
                        s1_hbm[:, oc, bb * BLK:(bb + 1) * BLK],
                        s1stg[:, (oc % 2) * 512:(oc % 2 + 1) * 512]),
                        waits=[("V", tok_sev)])
                    s1dma_slot[oc % 2] = tok_s1dma
        mm.release()
        return tok_s1dma

    tok_p5 = None
    import os as _os
    if _os.environ.get("P5_LIMIT", "full") != "full" and BUILD_UPTO >= 5:
        phase5(tok_p4, tok_p4_store)
        P.label("final")
        P.y(lambda y: y.nop(), waits=[("DY", P.counts.get("DY", 0)),
                                      ("DG", P.counts.get("DG", 0))])
        with contextlib.ExitStack() as st:
            sems = {}
            for name in ["DG", "DY", "T", "V", "S", "G"]:
                sems[name] = st.enter_context(nc.semaphore(name))
            block = st.enter_context(nc.Block())
            P.replay_into(nc, block, sems)
        return nc
    if BUILD_UPTO >= 5:
        tok_p5 = phase5(tok_p4, tok_p4_store)
        if "s1" in dbg_t:
            for c in range(NF):
                P.dma_y(lambda y, c=c: y.dma_start(
                    dbg_t["s1"][:, c, :], s1_hbm[:, c, :]), waits=[("DY", tok_p5)])

    def phase6(tok_s1_last):
        mm.release()
        mm.mark()
        W1t = mm.alloc("W1t", [128, NF * FF], BF16)
        W2t = mm.alloc("W2t", [128, 32 * 1024], BF16)
        mm.mark()
        stg = [mm.alloc(f"tstg{i}", [128, 1024], F32) for i in range(2)]
        tsc = [mm.alloc(f"tsc{i}", [128, 1024], F32) for i in range(2)]

        P.label("P6.tern")
        tern_done = {}
        tok_ld = {}
        pieces = []
        for fc in range(NF):
            for j in range(4):
                pieces.append(("w1", fc, j))
        for cc in range(32):
            pieces.append(("w2", cc, 0))
        prev_tern = {}
        for idx, (mat, a, jj) in enumerate(pieces):
            sl = idx % 2
            if mat == "w1":
                src = dram["w1T"][a * 128:(a + 1) * 128, jj * 1024:(jj + 1) * 1024]
                dst = W1t[:, a * FF + jj * 1024: a * FF + (jj + 1) * 1024]
                rs = w_rs["w1"]
            else:
                src = dram["w2T"][a * 128:(a + 1) * 128, :]
                dst = W2t[:, a * 1024:(a + 1) * 1024]
                rs = w_rs["w2"]
            w_ld = [("DY", tok_s1_last)] if idx < 2 else [("S", prev_tern[idx - 2])]
            tok_l = P.dma_y(lambda y, sl=sl, src=src: y.dma_start(stg[sl][:, :], src),
                            waits=w_ld)
            tok_t1 = P.s(lambda s, sl=sl, rs=rs: s.activation(
                tsc[sl][:, :], stg[sl][:, :], AF.Identity,
                bias=cmagic[:, :], scale=rs[:, :]),
                waits=[("DY", tok_l)] + ([("V", tok_qb32)] if idx == 0 else []) +
                      ([("V", tern_done[idx - 2])] if (idx - 2) in tern_done else []))
            prev_tern[idx] = tok_t1
            P.v(lambda v, sl=sl: v.tensor_scalar(
                tsc[sl][:, :], tsc[sl][:, :], MAGIC, -1.0,
                op0=ALU.subtract, op1=ALU.max),
                waits=[("S", tok_t1)])
            tern_done[idx] = P.v(lambda v, sl=sl, dst=dst: v.tensor_scalar_min(
                dst, tsc[sl][:, :], 1.0))
        tok_tern = tern_done[len(pieces) - 1]
        mm.release()
        h_blk = mm.alloc("h_blk", [128, 32 * 256], BF16)
        xn1b = mm.alloc("xn1b", [128, NF * 256], BF16)
        x1b = mm.alloc("x1b", [128, NF * 256], F32)
        s1b = mm.alloc("s1b", [128, NF * 256], F32)
        gm = mm.alloc("gm", [128, 2 * 256], F32)
        yb = mm.alloc("yb", [128, 2 * 256], F32)

        NBB = 8
        TB = 256
        tok_ydma = {}
        prev_h_mm = None
        prev_g_mm = None
        tok_xnb = tok_x1b = tok_s1b = None
        for bb in range(NBB):
            P.label(f"P6.blk{bb}")
            t0 = bb * TB
            w_ld = [("V", tok_tern)]
            if bb > 0:
                w_ld = [("T", prev_h_mm)]
            tok_xnb = P.dma_y(lambda y, t0=t0: y.dma_start(
                xn1b[:, :].rearrange("p (c t) -> p c t", c=NF),
                xn1_hbm[:, :, t0:t0 + TB]), waits=w_ld)
            w_l2 = [("V", tok_tern)] if bb == 0 else [("V", prev_comb)]
            tok_x1b = P.dma_y(lambda y, t0=t0: y.dma_start(
                x1b[:, :].rearrange("p (c t) -> p c t", c=NF),
                x1_hbm[:, :, t0:t0 + TB]), waits=w_l2)
            tok_s1b = P.dma_y(lambda y, t0=t0: y.dma_start(
                s1b[:, :].rearrange("p (c t) -> p c t", c=NF),
                s1_hbm[:, :, t0:t0 + TB]),
                waits=[("DY", tok_s1_last), ("V", tok_tern)] if bb == 0
                else [("V", prev_comb)])

            prev_hev = {}
            for cc in range(32):
                bank = 4 + cc % 2
                ps = PSB[:, (cc % 2) * 512:(cc % 2) * 512 + TB]
                w0 = [("DY", tok_xnb), ("V", tok_tern)] + BK.wr([bank])
                last_mm = None
                for ci in range(NF):
                    last_mm = P.t(lambda t, ci=ci, cc=cc, ps=ps: t.matmul(
                        ps, W1t[:, ci * FF + cc * 128: ci * FF + cc * 128 + 128],
                        xn1b[:, ci * TB:(ci + 1) * TB],
                        start=(ci == 0), stop=(ci == NF - 1)),
                        waits=w0 if ci == 0 else [])
                prev_h_mm = last_mm
                tok_h = P.s(lambda s, cc=cc, ps=ps: s.activation(
                    h_blk[:, cc * TB:(cc + 1) * TB], ps, AF.Gelu,
                    bias=sv["b1"][:, cc:cc + 1]),
                    waits=[("T", last_mm)])
                BK.rd([bank], "S", tok_h)
                prev_hev[cc] = tok_h
            for mh in range(NF):
                bank = mh % 4
                ps = PSA[:, bank * 512: bank * 512 + TB]
                w0 = [("S", prev_hev[31])] + BK.wr([bank])
                last_mm = None
                for cc in range(32):
                    last_mm = P.t(lambda t, cc=cc, mh=mh, ps=ps: t.matmul(
                        ps, W2t[:, cc * 1024 + mh * 128: cc * 1024 + mh * 128 + 128],
                        h_blk[:, cc * TB:(cc + 1) * TB],
                        start=(cc == 0), stop=(cc == 31)),
                        waits=w0 if cc == 0 else [])
                prev_g_mm = last_mm
                tok_g = P.s(lambda s, mh=mh, ps=ps: s.activation(
                    gm[:, (mh % 2) * TB:(mh % 2 + 1) * TB], ps, AF.Silu,
                    bias=sv["b2"][:, mh:mh + 1]),
                    waits=[("T", last_mm)] +
                          ([("V", tok_yw2)] if mh >= 2 else []))
                tok_yw1 = P.v(lambda v, mh=mh: v.tensor_mul(
                    scr[:, 0:TB], gm[:, (mh % 2) * TB:(mh % 2 + 1) * TB],
                    s1b[:, mh * TB:(mh + 1) * TB]),
                    waits=[("S", tok_g), ("DY", tok_s1b)])
                BK.rd([bank], "V", tok_yw1)
                w_y2 = [("DY", tok_x1b)]
                yk = (bb, mh - 2) if mh >= 2 else (bb - 1, mh + 6)
                if yk in tok_ydma:
                    w_y2 += [("DY", tok_ydma[yk])]
                tok_yw2 = P.v(lambda v, mh=mh: v.tensor_add(
                    yb[:, (mh % 2) * TB:(mh % 2 + 1) * TB], scr[:, 0:TB],
                    x1b[:, mh * TB:(mh + 1) * TB]),
                    waits=w_y2)
                tok_ydma[(bb, mh)] = P.dma_y(lambda y, mh=mh, t0=t0: y.dma_start(
                    yOut[:, mh, t0:t0 + TB],
                    yb[:, (mh % 2) * TB:(mh % 2 + 1) * TB]),
                    waits=[("V", tok_yw2)])
            prev_comb = tok_yw2
        mm.release()
        return tok_ydma[(NBB - 1, NF - 1)]

    if BUILD_UPTO >= 6:
        phase6(tok_p5)

    P.label("final")
    P.y(lambda y: y.nop(), waits=[("DY", P.counts.get("DY", 0)),
                                  ("DG", P.counts.get("DG", 0))])

    with contextlib.ExitStack() as st:
        sems = {}
        for name in ["DG", "DY", "T", "V", "S", "G"]:
            sems[name] = st.enter_context(nc.semaphore(name))
        block = st.enter_context(nc.Block())
        P.replay_into(nc, block, sems)
    return nc


def build_p2(env):
    pass


def build_p3(env):
    pass


def build_p4(env):
    pass


def build_p5(env):
    pass


def build_p6(env):
    pass


_cached = {}


def kernel(**inputs):
    if "nc" not in _cached:
        _cached["nc"] = build_program()
    nc = _cached["nc"]

    x = np.asarray(inputs["L0_attn_x"], np.float32)

    def tr(a):
        return np.ascontiguousarray(np.asarray(a, np.float32).T)

    def cols(vec, k):
        v = np.ascontiguousarray(np.asarray(vec, np.float32))
        return np.ascontiguousarray(v.reshape(k, 128).T)

    def bc(vec):
        return np.ascontiguousarray(np.tile(
            np.asarray(vec, np.float32)[None, :], (128, 1)))

    shared = {
        "wqT": tr(inputs["L0_attn_WQ"]), "wkT": tr(inputs["L0_attn_WK"]),
        "wvT": tr(inputs["L0_attn_WV"]),
        "qwT": tr(inputs["L0_ffn_q_w"]), "kwT": tr(inputs["L0_ffn_k_w"]),
        "vwT": tr(inputs["L0_ffn_v_w"]), "owT": tr(inputs["L0_ffn_o_w"]),
        "nw_c": cols(inputs["L0_ffn_norm_w"], NF),
        "nb_c": cols(inputs["L0_ffn_norm_b"], NF),
        "qb_c": cols(inputs["L0_ffn_q_b"], NF),
        "kb_c": cols(inputs["L0_ffn_k_b"], NF),
        "ob_c": cols(inputs["L0_ffn_o_b"], NF),
        "vb_bc": bc(inputs["L0_ffn_v_b"]),
        "w1qT": tr(inputs["L1_attn_Wq"]), "w1kT": tr(inputs["L1_attn_Wk"]),
        "w1vT": tr(inputs["L1_attn_Wv"]), "w1oT": tr(inputs["L1_attn_Wo"]),
        "qg_bc": bc(inputs["L1_attn_q_gamma"]),
        "qbeta_bc": bc(inputs["L1_attn_q_beta"]),
        "kg_bc": bc(inputs["L1_attn_k_gamma"]),
        "kbeta_bc": bc(inputs["L1_attn_k_beta"]),
        "w1T": tr(inputs["L1_ffn_W1"]), "w2T": tr(inputs["L1_ffn_W2"]),
        "b1_c": cols(inputs["L1_ffn_b1"], 32),
        "b2_c": cols(inputs["L1_ffn_b2"], NF),
    }

    in_maps = []
    for c in range(8):
        m = dict(shared)
        m["xT"] = np.ascontiguousarray(x[c].T)
        in_maps.append(m)

    res = run_bass_kernel_spmd(nc, in_maps, core_ids=list(range(8)))
    _cached["last_res"] = res
    outs = []
    for c in range(8):
        y = np.asarray(res.results[c]["y"], np.float32)
        outs.append(y.transpose(2, 1, 0).reshape(T, D))
    return np.stack(outs, axis=0)


# version 11
# speedup vs baseline: 1.0587x; 1.0016x over previous
import contextlib
import numpy as np
import ml_dtypes
import concourse.bass as bass
import concourse.mybir as mybir
from concourse.bass_utils import run_bass_kernel_spmd

F32 = mybir.dt.float32
BF16 = mybir.dt.bfloat16
AF = mybir.ActivationFunctionType
ALU = mybir.AluOpType
AX = mybir.AxisListType

T = 2048
D = 1024
NF = 8
NT = 16
NB = 4
BLK = 512
MAGIC = float(np.float32(1.5 * 2 ** 23))
EPS = 1e-5
SCALE = 1.0 / 32.0
FF = 4096

BUILD_UPTO = 6
DEBUG_DUMPS = []

ENGINE_ATTR = {"g": "gpsimd", "y": "sync", "t": "tensor", "v": "vector", "s": "scalar"}
CHAINED = {"v": "V", "s": "S", "g": "G"}


class Prog:

    def __init__(self):
        self.items = {e: [] for e in ENGINE_ATTR}
        self.counts = {}
        self.labels = {e: [] for e in ENGINE_ATTR}
        self.cur_label = "init"
        self.last_tok = {e: None for e in ENGINE_ATTR}

    def label(self, s):
        self.cur_label = s

    def emit(self, eng, fn, inc=None, inc_n=1, waits=()):
        wl = [(s, v) for (s, v) in waits if v is not None and v > 0]
        if eng in CHAINED:
            sem = CHAINED[eng]
            if inc is None:
                inc, inc_n = sem, 1
            prev = self.last_tok[eng]
            if prev is not None:
                wl.append((sem, prev))
        self.items[eng].append((fn, inc, inc_n, wl))
        self.labels[eng].append(self.cur_label)
        tok = None
        if inc is not None:
            self.counts[inc] = self.counts.get(inc, 0) + inc_n
            tok = self.counts[inc]
        if eng in CHAINED and inc == CHAINED[eng]:
            self.last_tok[eng] = tok
        return tok

    def t(self, fn, inc="T", inc_n=1, waits=()):
        return self.emit("t", fn, inc, inc_n, waits)

    def v(self, fn, waits=()):
        return self.emit("v", fn, None, 1, waits)

    def s(self, fn, waits=()):
        return self.emit("s", fn, None, 1, waits)

    def g(self, fn, waits=()):
        return self.emit("g", fn, None, 1, waits)

    def y(self, fn, inc=None, inc_n=1, waits=()):
        return self.emit("y", fn, inc, inc_n, waits)

    def dma_g(self, fn, waits=()):
        return self.emit("g", fn, "DG", 16, waits)

    def dma_y(self, fn, waits=()):
        return self.emit("y", fn, "DY", 16, waits)

    def check(self):
        pos = {e: 0 for e in self.items}
        done = {}
        progress = True
        while progress:
            progress = False
            for e, lst in self.items.items():
                while pos[e] < len(lst):
                    fn, inc, inc_n, waits = lst[pos[e]]
                    if any(done.get(s, 0) < v for (s, v) in waits):
                        break
                    if inc:
                        done[inc] = done.get(inc, 0) + inc_n
                    pos[e] += 1
                    progress = True
        stuck = {e: pos[e] for e in pos if pos[e] < len(self.items[e])}
        if stuck:
            msg = []
            for e, p in stuck.items():
                fn, inc, inc_n, waits = self.items[e][p]
                lbl = self.labels[e][p]
                missing = [(s, v, done.get(s, 0)) for (s, v) in waits
                           if done.get(s, 0) < v]
                msg.append(f"engine {e} item {p} [{lbl}] waits {missing}")
            raise RuntimeError("Deadlock in recorded program:\n" + "\n".join(msg))

    def replay_into(self, nc, block, sems):
        self.check()
        for e, attr in ENGINE_ATTR.items():
            items = self.items[e]
            if not items:
                continue

            def make(items=items):
                def run(eng):
                    waited = {}
                    for (fn, inc, inc_n, waits) in items:
                        for (s_, v_) in waits:
                            if waited.get(s_, 0) >= v_:
                                continue
                            eng.wait_ge(sems[s_], v_)
                            waited[s_] = v_
                        instr = fn(eng)
                        if inc is not None:
                            instr.then_inc(sems[inc], inc_n)
                return run

            getattr(block, attr)(make())


class MemMap:
    def __init__(self, nc):
        self.nc = nc
        self.base = nc.sbuf_base
        self.top = nc.sbuf_top
        self.cur = int(np.ceil(self.base / 32) * 32)
        self.stack = []
        self.peak = self.cur

    def alloc(self, name, shape, dtype):
        bpp = int(np.prod(shape[1:])) * mybir.dt.size(dtype)
        bpp = int(np.ceil(bpp / 32) * 32)
        off = self.cur
        assert off + bpp <= self.top, (
            f"SBUF overflow: {name} needs {bpp}B at {off}, top={self.top}")
        self.cur = off + bpp
        self.peak = max(self.peak, self.cur)
        return self.nc.alloc_sbuf_tensor_at(name, list(shape), dtype, offset=off)

    def mark(self):
        self.stack.append(self.cur)

    def release(self):
        self.cur = self.stack.pop()


class Banks:

    def __init__(self):
        self.last_read = {}
        self.last_write = {}

    @staticmethod
    def of(tensor_is_b, col0, col1):
        b0 = 4 if tensor_is_b else 0
        return list(range(b0 + col0 // 512, b0 + (col1 + 511) // 512))

    def wr(self, banks):
        w = []
        for b in banks:
            if b in self.last_read:
                w.append(self.last_read[b])
        return w

    def rd(self, banks, sem, tok):
        for b in banks:
            self.last_read[b] = (sem, tok)


def build_program():
    nc = bass.Bass("TRN2", target_bir_lowering=False)

    xT = nc.dram_tensor("xT", [D, T], F32, kind="ExternalInput")
    dram = {}
    for nm, shape in [("wqT", [D, D]), ("wkT", [D, D]), ("wvT", [D, D]),
                      ("qwT", [D, D]), ("kwT", [D, D]), ("vwT", [D, D]),
                      ("owT", [D, D]), ("w1qT", [D, D]), ("w1kT", [D, D]),
                      ("w1vT", [D, D]), ("w1oT", [D, D]),
                      ("w1T", [D, FF]), ("w2T", [FF, D]),
                      ("nw_c", [128, NF]), ("nb_c", [128, NF]), ("qb_c", [128, NF]),
                      ("kb_c", [128, NF]), ("ob_c", [128, NF]), ("b2_c", [128, NF]),
                      ("b1_c", [128, 32]), ("vb_bc", [128, D]),
                      ("qg_bc", [128, 64]), ("qbeta_bc", [128, 64]),
                      ("kg_bc", [128, 64]), ("kbeta_bc", [128, 64])]:
        dram[nm] = nc.dram_tensor(nm, shape, F32, kind="ExternalInput")

    yOut = nc.dram_tensor("y", [128, NF, T], F32, kind="ExternalOutput")
    g_hbm = nc.dram_tensor("g_scr", [128, NF, T], BF16, kind="Internal")
    x1_hbm = nc.dram_tensor("x1_scr", [128, NF, T], F32, kind="Internal")
    xn1_hbm = nc.dram_tensor("xn1_scr", [128, NF, T], BF16, kind="Internal")
    s1_hbm = nc.dram_tensor("s1_scr", [128, NF, T], F32, kind="Internal")

    dbg_t = {}
    for nm in DEBUG_DUMPS:
        shp = [128, NF, T]
        dt_ = BF16 if nm in ("xn", "xn1", "g", "pt0") else F32
        dbg_t[nm] = nc.dram_tensor("dbg_" + nm, shp, dt_, kind="ExternalOutput")

    PSA = nc.alloc_psum_tensor("PSA", [128, 2048], F32)
    PSB = nc.alloc_psum_tensor("PSB", [128, 2048], F32)
    nc.psum_base -= 8
    PSAbf = nc.alloc_psum_tensor("PSAbf", [128, 4096], BF16)
    PSBbf = nc.alloc_psum_tensor("PSBbf", [128, 4096], BF16)

    mm = MemMap(nc)
    ident = mm.alloc("ident", [128, 128], BF16)
    maskA = mm.alloc("maskA", [128, 128], F32)
    maskMT = mm.alloc("maskMT", [128, 128], F32)
    ones_colb = mm.alloc("ones_colb", [128, 1], BF16)
    ones_row1f = mm.alloc("ones_row1f", [1, 128], F32)
    ones_colf = mm.alloc("ones_colf", [128, 128], F32)
    cmagic = mm.alloc("cmagic", [128, 1], F32)
    sv = {}
    for name in ["nw", "nb", "qb", "kb", "ob", "b2"]:
        sv[name] = mm.alloc("sv_" + name, [128, NF], F32)
    sv["b1"] = mm.alloc("sv_b1", [128, 32], F32)
    sv["qb32"] = mm.alloc("sv_qb32", [128, NF], F32)
    sv["vbbc"] = mm.alloc("sv_vbbc", [128, D], F32)
    for nm2 in ["qg", "qbe", "kg", "kbe"]:
        sv[nm2] = mm.alloc("sv_" + nm2, [128, 64], F32)
    w_acc = {"w1": mm.alloc("w1_acc", [128, 1], F32),
             "w2": mm.alloc("w2_acc", [128, 1], F32)}
    w_rs = {"w1": mm.alloc("w1_rs", [128, 1], F32),
            "w2": mm.alloc("w2_rs", [128, 1], F32)}
    tern_tmp = mm.alloc("tern_tmp", [128, 2], F32)
    strow = mm.alloc("strow", [1, 5 * BLK], F32)
    scr = mm.alloc("scr", [128, 512], F32)
    negmax = mm.alloc("negmax", [128, 1], F32)
    sums = mm.alloc("sums", [128, 1], F32)
    recs = mm.alloc("recs", [128, 16], F32)

    mm.mark()
    xn = mm.alloc("xn", [128, NF * T], BF16)

    P = Prog()
    BK = Banks()

    P.label("P0.init")
    P.g(lambda g: g.memset(ident[:, :], 0.0))
    P.g(lambda g: g.affine_select(
        out=ident[:, :], in_=ident[:, :], compare_op=ALU.not_equal, fill=1.0,
        base=0, pattern=[[-1, 128]], channel_multiplier=1))
    P.g(lambda g: g.memset(maskA[:, :], 0.0))
    P.g(lambda g: g.affine_select(
        out=maskA[:, :], in_=maskA[:, :], compare_op=ALU.is_ge, fill=-1e9,
        base=0, pattern=[[-1, 128]], channel_multiplier=1))
    P.g(lambda g: g.memset(maskMT[:, :], 1.0))
    P.g(lambda g: g.affine_select(
        out=maskMT[:, :], in_=maskMT[:, :], compare_op=ALU.is_ge, fill=0.0,
        base=0, pattern=[[1, 128]], channel_multiplier=-1))
    P.g(lambda g: g.memset(ones_colb[:, :], 1.0))
    P.g(lambda g: g.memset(ones_row1f[:, :], 1.0))
    P.g(lambda g: g.memset(ones_colf[:, :], 1.0))
    P.g(lambda g: g.memset(w_acc["w1"][:, :], 0.0))
    P.g(lambda g: g.memset(w_acc["w2"][:, :], 0.0))
    tok_init_g = P.g(lambda g: g.memset(cmagic[:, :], MAGIC))

    tok_sv = None
    for (dstn, srcn) in [("nw", "nw_c"), ("nb", "nb_c"), ("qb", "qb_c"),
                         ("kb", "kb_c"), ("ob", "ob_c"), ("b2", "b2_c"),
                         ("b1", "b1_c"), ("vbbc", "vb_bc"),
                         ("qg", "qg_bc"), ("qbe", "qbeta_bc"),
                         ("kg", "kg_bc"), ("kbe", "kbeta_bc")]:
        tok_sv = P.dma_y(lambda y, d=sv[dstn], s_=dram[srcn]: y.dma_start(d[:, :], s_[:, :]))
    tok_qb32 = P.v(lambda v: v.tensor_scalar_mul(sv["qb32"][:, :], sv["qb"][:, :], SCALE),
                   waits=[("DY", tok_sv)])

    def emit_ln(phase, src_kind, dst_write, src_hbm, xblk, xbf, xsq,
                apply_chunks=None, pre_tok=None, first_load_waits=()):
        last_ap = None
        mmlast_prev = None
        finB_prev = None
        apdone = {}
        xtok = {}
        for b in range(NB):
            t0 = b * BLK
            P.label(f"{phase}.b{b}")
            if src_kind == "hbm_f32":
                w_l = list(first_load_waits) if b <= 1 else []
                if b >= 2 and apdone.get(b - 2):
                    w_l.append(("V", apdone[b - 2]))
                if src_hbm is xT:
                    src_ap = xT[:, t0:t0 + BLK].rearrange("(c p) t -> p c t", p=128)
                else:
                    src_ap = src_hbm[:, :, t0:t0 + BLK]
                xtok[b] = P.dma_y(lambda y, src_ap=src_ap, b=b: y.dma_start(
                    xblk[b % 2][:, :].rearrange("p (c t) -> p c t", c=NF), src_ap),
                    waits=w_l)

            tok_mm_last = None
            nsq = xsq.shape[1] // BLK
            mm2_toks = {}
            for c in range(NF):
                csl = c % nsq
                if src_kind == "hbm_f32":
                    xs_f = xblk[b % 2][:, c * BLK:(c + 1) * BLK]
                    wsrc = [("DY", xtok[b])]
                    wreuse = [("T", mmlast_prev)] if mmlast_prev else []
                    tok_cast = P.v(lambda v, xs_f=xs_f, c=c: v.tensor_copy(
                        xbf[:, c * BLK:(c + 1) * BLK], xs_f), waits=wsrc + wreuse)
                    sx_src = xbf[:, c * BLK:(c + 1) * BLK]
                    sx_wait = [("V", tok_cast)]
                else:
                    xs_f = xn[:, c * T + t0: c * T + t0 + BLK]
                    wsrc = [("V", pre_tok)] if pre_tok else []
                    wreuse = [("T", mmlast_prev)] if mmlast_prev else []
                    sx_src = xs_f
                    sx_wait = list(wsrc)
                wsq = [("T", mm2_toks[c - nsq])] if (c - nsq) in mm2_toks else wreuse
                tok_sq = P.s(lambda s, xs_f=xs_f, csl=csl: s.activation(
                    xsq[:, csl * BLK:(csl + 1) * BLK], xs_f, AF.Square),
                    waits=wsrc + wsq)
                bw = BK.wr([4]) if c == 0 else []
                tok_mm1 = P.t(lambda t, sx_src=sx_src, c=c: t.matmul(
                    PSB[0:1, 0:BLK], ones_colb[:, :], sx_src,
                    start=(c == 0), stop=(c == NF - 1)),
                    waits=sx_wait + bw + ([("G", tok_init_g)] if b == 0 else []))
                bw2 = BK.wr([5]) if c == 0 else []
                tok_mm2 = P.t(lambda t, csl=csl, c=c: t.matmul(
                    PSB[0:1, BLK:2 * BLK], ones_colb[:, :],
                    xsq[:, csl * BLK:(csl + 1) * BLK],
                    start=(c == 0), stop=(c == NF - 1)),
                    waits=[("S", tok_sq)] + bw2)
                mm2_toks[c] = tok_mm2
                tok_mm_last = tok_mm2
            mmlast_prev = tok_mm_last

            mu = strow[0:1, 0:BLK]
            m2e = strow[0:1, BLK:2 * BLK]
            musq = strow[0:1, 2 * BLK:3 * BLK]
            varr = strow[0:1, BLK:2 * BLK]
            stdr = strow[0:1, 2 * BLK:3 * BLK]
            arow = strow[0:1, 3 * BLK:4 * BLK]
            brow = strow[0:1, 4 * BLK:5 * BLK]
            wrow = [("T", finB_prev)] if finB_prev else []
            tk1 = P.v(lambda v: v.tensor_scalar_mul(mu, PSB[0:1, 0:BLK], 1.0 / D),
                      waits=[("T", tok_mm_last)] + wrow)
            tk2 = P.v(lambda v: v.tensor_scalar(m2e, PSB[0:1, BLK:2 * BLK],
                                                1.0 / D, EPS, op0=ALU.mult, op1=ALU.add))
            tk3 = P.v(lambda v: v.tensor_mul(musq, mu, mu))
            tk4 = P.v(lambda v: v.tensor_sub(varr, m2e, musq))
            tk5 = P.s(lambda s: s.activation(stdr, varr, AF.Sqrt), waits=[("V", tk4)])
            tk6 = P.v(lambda v: v.reciprocal(arow, stdr), waits=[("S", tk5)])
            tk7 = P.v(lambda v: v.scalar_tensor_tensor(brow, mu, -1.0, arow,
                                                       op0=ALU.mult, op1=ALU.mult))
            BK.rd([4, 5], "V", tk2)

            tokA = P.t(lambda t: t.matmul(PSB[:, 2 * BLK:3 * BLK], ones_row1f[:, :],
                                          arow, start=True, stop=True),
                       waits=[("V", tk7)] + BK.wr([6]))
            tokB = P.t(lambda t: t.matmul(PSB[:, 3 * BLK:4 * BLK], ones_row1f[:, :],
                                          brow, start=True, stop=True),
                       waits=[("V", tk7)] + BK.wr([7]))
            finB_prev = tokB

            nchunks = NF if apply_chunks is None else apply_chunks[b]
            tok_ap = None
            for c in range(nchunks):
                if src_kind == "hbm_f32":
                    xs_f = xblk[b % 2][:, c * BLK:(c + 1) * BLK]
                    w_ap = [("DY", xtok[b])]
                else:
                    xs_f = xn[:, c * T + t0: c * T + t0 + BLK]
                    w_ap = []
                tka = P.v(lambda v, xs_f=xs_f: v.tensor_mul(
                    scr[:, :], xs_f, PSB[:, 2 * BLK:3 * BLK]),
                    waits=[("T", tokA)] + w_ap)
                tok_ap = dst_write(b, c, tokB)
            BK.rd([6, 7], "V", tok_ap)
            apdone[b] = tok_ap
        return apdone[NB - 1]

    mm.mark()
    xblk = [mm.alloc(f"xblk{i}", [128, NF * BLK], F32) for i in range(2)]
    xbf = mm.alloc("xbf", [128, NF * BLK], BF16)
    xsq = mm.alloc("xsq", [128, NF * BLK], BF16)

    def ln0_write(b, c, tokB):
        t0 = b * BLK
        return P.v(lambda v: v.tensor_add(
            xn[:, c * T + t0: c * T + t0 + BLK], scr[:, :], PSB[:, 3 * BLK:4 * BLK]),
            waits=[("T", tokB)])

    P.label("P1.ln0")
    tok_ln0 = emit_ln("P1", "hbm_f32", ln0_write, xT, xblk, xbf, xsq)

    wstage = [mm.alloc(f"wstage{i}", [128, 4096], F32) for i in range(2)]
    P.label("P1.absmean")
    abs_done = {}
    piece_toks = []
    for (mat, hbm) in [("w1", dram["w1T"]), ("w2", dram["w2T"])]:
        for i in range(8):
            slot = wstage[len(piece_toks) % 2]
            if mat == "w1":
                src = hbm[i * 128:(i + 1) * 128, :]
            else:
                src = hbm[i * 512:(i + 1) * 512, :].rearrange("(c p) f -> p c f", p=128)
            wait_prev = [("V", piece_toks[-2])] if len(piece_toks) >= 2 else []
            dst = slot[:, :] if mat == "w1" else slot[:, :].rearrange(
                "p (c f) -> p c f", c=4)
            tok_l = P.dma_y(lambda y, dst=dst, src=src: y.dma_start(dst, src),
                            waits=wait_prev)
            P.v(lambda v, slot=slot: v.reduce_sum(
                tern_tmp[:, 0:1], slot[:, :], axis=AX.X, apply_absolute_value=True),
                waits=[("DY", tok_l)])
            tok_a = P.v(lambda v, acc=w_acc[mat]: v.tensor_add(
                acc[:, :], acc[:, :], tern_tmp[:, 0:1]))
            piece_toks.append(tok_a)
        abs_done[mat] = piece_toks[-1]
    tok_rs = {}
    for mat, nelem in [("w1", D * FF), ("w2", D * FF)]:
        tok_t = P.t(lambda t, acc=w_acc[mat]: t.matmul(
            PSB[:, 0:1], ones_colf[:, :], acc[:, :], start=True, stop=True),
            waits=[("V", abs_done[mat]), ("V", tok_ln0)] + BK.wr([4]))
        tk1 = P.v(lambda v, rs=w_rs[mat], nelem=nelem: v.tensor_scalar(
            rs[:, :], PSB[:, 0:1], 1.0 / nelem, 1e-8, op0=ALU.mult, op1=ALU.max),
            waits=[("T", tok_t)])
        tok_rs[mat] = P.v(lambda v, rs=w_rs[mat]: v.reciprocal(rs[:, :], rs[:, :]))
        BK.rd([4], "V", tk1)
    mm.release()

    def dump(nm, sb, tok, tok_sem="V"):
        if nm not in dbg_t:
            return
        for c in range(NF):
            P.dma_y(lambda y, c=c: y.dma_start(dbg_t[nm][:, c, :],
                                               sb[:, c * T:(c + 1) * T]),
                    waits=[(tok_sem, tok)])

    dump("xn", xn, tok_ln0)

    XB = [0, 8, 12, 14]
    NCF = [8, 4, 2, 1]

    def phase2():
        mm.mark()
        q2 = mm.alloc("q2", [128, NF * T], BF16)
        k2 = mm.alloc("k2", [128, NF * T], BF16)
        v2 = mm.alloc("v2", [128, NT * 1024], BF16)
        wringA = mm.alloc("wringA", [128, NF * 1024], BF16)
        wringB = mm.alloc("wringB", [128, NF * 1024], BF16)

        mm.mark()
        xin = mm.alloc("xin", [128, 15 * BLK], BF16)
        xsq2 = mm.alloc("xsq2", [128, 2 * BLK], BF16)
        scr2 = mm.alloc("scr2", [128, 512], F32)

        sq_rot = {}

        def lni_write(b, c, tokB):
            xoff = (XB[b] + c) * BLK
            P.v(lambda v: v.tensor_add(scr2[:, :], scr[:, :],
                                       PSB[:, 3 * BLK:4 * BLK]),
                waits=[("T", tokB)])
            return P.v(lambda v, xoff=xoff, c=c: v.tensor_scalar(
                xin[:, xoff:xoff + BLK], scr2[:, :],
                sv["nw"][:, c:c + 1], sv["nb"][:, c:c + 1],
                op0=ALU.mult, op1=ALU.add))

        P.label("P2a.lni")
        tok_xin = emit_ln("P2a", "xn_bf16", lni_write, None, None, None, xsq2,
                          apply_chunks=NCF, pre_tok=tok_ln0)

        def wload(dst, src, waits=()):
            return P.dma_g(lambda g: g.dma_start(
                dst[:, :].rearrange("p (c f) -> p c f", c=NF),
                src[:, :].rearrange("(c p) f -> p c f", p=128)), waits=waits)

        P.label("P2b.wload")
        tok_qw = wload(wringA, dram["qwT"])
        tok_kw = wload(wringB, dram["kwT"])

        P.label("P2b.proj")

        def proj_fm(dst, wring, tok_w, bias_cols, scale, extra_wait):
            last_mm = None
            tok_e = None
            si = 0
            for e in range(4):
                for oc in range(NF):
                    sl = si % 4
                    si += 1
                    ps = PSA[:, sl * 512:(sl + 1) * 512]
                    w0 = [("DG", tok_w), ("V", tok_xin)] + extra_wait + BK.wr([sl])
                    ncf = NCF[e]
                    for ci in range(ncf):
                        last_mm = P.t(lambda t, ci=ci, oc=oc, e=e, ps=ps, wring=wring,
                                      ncf=ncf: t.matmul(
                            ps, wring[:, ci * 1024 + oc * 128: ci * 1024 + oc * 128 + 128],
                            xin[:, (XB[e] + ci) * BLK:(XB[e] + ci + 1) * BLK],
                            start=(ci == 0), stop=(ci == ncf - 1)),
                            waits=w0 if ci == 0 else [])
                    tok_e = P.s(lambda s, e=e, oc=oc, ps=ps: s.activation(
                        dst[:, oc * T + e * BLK: oc * T + (e + 1) * BLK], ps, AF.Identity,
                        bias=bias_cols[:, oc:oc + 1], scale=scale),
                        waits=[("T", last_mm)])
                    BK.rd([sl], "S", tok_e)
            return last_mm, tok_e

        tok_q2mm, tok_q2ev = proj_fm(q2, wringA, tok_qw, sv["qb32"], SCALE, [])
        tok_k2mm, tok_k2ev = proj_fm(k2, wringB, tok_kw, sv["kb"], 1.0, [])

        tok_vw = wload(wringA, dram["vwT"], waits=[("T", tok_q2mm)])
        tok_ow = wload(wringB, dram["owT"], waits=[("T", tok_k2mm)])

        last_vmm = None
        vslot_tok = [None, None]
        for e in range(4):
            ncf = NCF[e]
            for tl in range(4):
                tt = e * 4 + tl
                sl = tt % 2
                ps = PSA[:, sl * 1024:(sl + 1) * 1024]
                w0 = [("DG", tok_vw), ("V", tok_xin)]
                if vslot_tok[sl] is not None:
                    w0 += [("V", vslot_tok[sl])]
                w0 += BK.wr([2 * sl, 2 * sl + 1])
                for ci in range(ncf):
                    for nb2 in range(2):
                        last_vmm = P.t(lambda t, ci=ci, e=e, tl=tl, nb2=nb2, ps=ps,
                                       ncf=ncf: t.matmul(
                            ps[:, nb2 * 512:(nb2 + 1) * 512],
                            xin[:, (XB[e] + ci) * BLK + tl * 128:
                                (XB[e] + ci) * BLK + tl * 128 + 128],
                            wringA[:, ci * 1024 + nb2 * 512: ci * 1024 + (nb2 + 1) * 512],
                            start=(ci == 0), stop=(ci == ncf - 1)),
                            waits=w0 if (ci == 0 and nb2 == 0) else [])
                tok_ev = P.v(lambda v, tt=tt, ps=ps: v.tensor_add(
                    v2[:, tt * 1024:(tt + 1) * 1024], ps, sv["vbbc"][:, :]),
                    waits=[("T", last_vmm)])
                vslot_tok[sl] = tok_ev
                BK.rd([2 * sl, 2 * sl + 1], "V", tok_ev)
        tok_v2mm = last_vmm
        mm.release()

        mm.mark()
        P_sb2 = mm.alloc("P_sb2", [128, 3 * T], BF16)
        aotm = mm.alloc("aotm", [128, 1024], BF16)
        aofm = mm.alloc("aofm", [128, NF * 512], BF16)
        g_blk = mm.alloc("g_blk", [128, 2 * 512], BF16)
        nm4 = mm.alloc("nm4", [128, 3 * 4], F32)
        nmx = mm.alloc("nmx", [128, 3], F32)
        sum2 = mm.alloc("sum2", [128, 3], F32)

        tok_gdma = {}
        tok_aofm = None
        exp_of = {}
        recip_of = {}
        ao_last = {}
        prev_outproj_mm = None
        sil_of = {}

        def stage_scores(gi):
            qt, h = divmod(gi, NF)
            w_sc = BK.wr([0, 1, 2, 3])
            if gi == 0:
                w_sc += [("S", tok_q2ev), ("S", tok_k2ev)]
            tok_mm = None
            for j in range(4):
                tok_mm = P.t(lambda t, h=h, qt=qt, j=j: t.matmul(
                    PSA[:, j * 512:(j + 1) * 512],
                    q2[:, h * T + qt * 128: h * T + qt * 128 + 128],
                    k2[:, h * T + j * 512: h * T + (j + 1) * 512],
                    start=True, stop=True),
                    waits=w_sc if j == 0 else [])
            return tok_mm

        def stage_a(gi, tok_mm):
            qt, h = divmod(gi, NF)
            sl = gi % 3
            rq = (qt % 2) * NF
            w_r = [("T", tok_mm)]
            if exp_of.get(gi - 3) is not None:
                w_r += [("S", exp_of[gi - 3])]
            tok_c = P.v(lambda v, sl=sl: v.reduce_max(
                nmx[:, sl:sl + 1], PSA[:, :], axis=AX.X, negate=True),
                waits=w_r)
            w_exp = [("V", tok_c)]
            if (gi - 3) in ao_last:
                w_exp += [("T", ao_last[gi - 3])]
            if (gi - 3) in recip_of:
                w_exp += [("V", recip_of[gi - 3])]
            if gi == 0:
                w_exp += [("T", tok_v2mm)]
            tok_exp = P.s(lambda s, sl=sl: s.activation(
                P_sb2[:, sl * T:(sl + 1) * T], PSA[:, :], AF.Exp,
                bias=nmx[:, sl:sl + 1], accum_out=sum2[:, sl:sl + 1]),
                waits=w_exp)
            exp_of[gi] = tok_exp
            BK.rd([0, 1, 2, 3], "S", tok_exp)
            recip_of[gi] = P.v(lambda v, h=h, sl=sl, rq=rq: v.reciprocal(
                recs[:, rq + h: rq + h + 1], sum2[:, sl:sl + 1]),
                waits=[("S", tok_exp)])

        def stage_b1(gi):
            qt, h = divmod(gi, NF)
            sl = gi % 3
            base = sl * T
            tok_wave = {}
            for w in range(2):
                last_tr = None
                for j in range(8):
                    kt = w * 8 + j
                    w_tr = (BK.wr([4 + w]) + [("S", exp_of[gi])]) if j == 0 else []
                    last_tr = P.t(lambda t, kt=kt, j=j, w=w, base=base: t.transpose(
                        PSBbf[:, w * 1024 + j * 128: w * 1024 + (j + 1) * 128],
                        P_sb2[:, base + kt * 128: base + (kt + 1) * 128],
                        ident[:, :]),
                        waits=w_tr)
                w_ev = [("T", last_tr)]
                if w == 0:
                    tok_wave[w] = P.s(lambda s, w=w, base=base: s.copy(
                        P_sb2[:, base + w * 1024: base + (w + 1) * 1024],
                        PSBbf[:, w * 1024:(w + 1) * 1024]),
                        waits=w_ev)
                    BK.rd([4 + w], "S", tok_wave[w])
                else:
                    tok_wave[w] = P.v(lambda v, w=w, base=base: v.tensor_copy(
                        P_sb2[:, base + w * 1024: base + (w + 1) * 1024],
                        PSBbf[:, w * 1024:(w + 1) * 1024]),
                        waits=w_ev)
                    BK.rd([4 + w], "V", tok_wave[w])
            return tok_wave

        def stage_b2(gi, tok_wave):
            qt, h = divmod(gi, NF)
            sl = gi % 3
            base = sl * T
            for kt in range(NT):
                w_ao = []
                if kt == 0:
                    w_ao = [("V", tok_wave[1]), ("S", tok_wave[0])] + \
                        (BK.wr([6, 7]) if h == 0 else [])
                ao_last[gi] = P.t(lambda t, kt=kt, h=h, base=base: t.matmul(
                    PSB[:, 1024 + h * 128: 1024 + (h + 1) * 128],
                    P_sb2[:, base + kt * 128: base + (kt + 1) * 128],
                    v2[:, kt * 1024 + h * 128: kt * 1024 + (h + 1) * 128],
                    start=(kt == 0), stop=(kt == NT - 1)),
                    waits=w_ao)

        def epilogue(qt):
            nonlocal tok_aofm, prev_outproj_mm
            P.label(f"P2c.epi{qt}")
            rq = (qt % 2) * NF
            tok_ev = None
            for h in range(NF):
                w_ev = [("T", ao_last[qt * NF + NF - 1])] if h == 0 else []
                tok_ev = P.v(lambda v, h=h, rq=rq: v.tensor_scalar(
                    aotm[:, h * 128:(h + 1) * 128],
                    PSB[:, 1024 + h * 128: 1024 + (h + 1) * 128],
                    recs[:, rq + h: rq + h + 1], None, op0=ALU.mult),
                    waits=w_ev)
            BK.rd([6, 7], "V", tok_ev)
            last_tr = None
            for fc in range(NF):
                w_tr = (BK.wr([4]) + [("V", tok_ev)]) if fc == 0 else []
                last_tr = P.t(lambda t, fc=fc: t.transpose(
                    PSBbf[:, fc * 128:(fc + 1) * 128],
                    aotm[:, fc * 128:(fc + 1) * 128],
                    ident[:, :]), waits=w_tr)
            j = qt % 4
            w_aofm = [("T", last_tr)]
            if j == 0 and prev_outproj_mm is not None:
                w_aofm += [("T", prev_outproj_mm)]
            tok_aofm = P.v(lambda v, j=j: v.tensor_copy(
                bass.AP(aofm, j * 128, [[NF * 512, 128], [512, NF], [1, 128]]),
                bass.AP(PSBbf, 0, [[4096, 128], [128, NF], [1, 128]])),
                waits=w_aofm)
            BK.rd([4], "V", tok_aofm)

        def emit_outproj(eb):
            nonlocal prev_outproj_mm
            mcnt = NCF[eb]
            P.label(f"P2d.blk{eb}")
            tok_sil = None
            for mh in range(NF):
                slg = mh % 2
                gk = (eb, mh - 2) if mh >= 2 else (eb - 1, mh + 6)
                wg = [("DY", tok_gdma[gk])] if gk in tok_gdma else []
                if mh < mcnt:
                    sl = mh % 4
                    ps = PSA[:, sl * 512:(sl + 1) * 512]
                    w0 = [("V", tok_aofm), ("DG", tok_ow)] + BK.wr([sl])
                    last_mm = None
                    for ci in range(mcnt):
                        last_mm = P.t(lambda t, ci=ci, mh=mh, ps=ps,
                                      mcnt=mcnt: t.matmul(
                            ps, wringB[:, ci * 1024 + mh * 128: ci * 1024 + mh * 128 + 128],
                            aofm[:, ci * 512:(ci + 1) * 512],
                            start=(ci == 0), stop=(ci == mcnt - 1)),
                            waits=w0 if ci == 0 else [])
                    prev_outproj_mm = last_mm
                    tok_t = P.v(lambda v, mh=mh, eb=eb, ps=ps: v.tensor_add(
                        scr[:, :], ps,
                        xn[:, mh * T + eb * BLK: mh * T + (eb + 1) * BLK]),
                        waits=[("T", last_mm)] +
                              ([("S", tok_sil)] if tok_sil else []))
                    BK.rd([sl], "V", tok_t)
                    tok_sil = P.s(lambda s, mh=mh, slg=slg: s.activation(
                        g_blk[:, slg * 512:(slg + 1) * 512], scr[:, :], AF.Silu,
                        bias=sv["ob"][:, mh:mh + 1]),
                        waits=[("V", tok_t)] + wg)
                else:
                    tok_sil = P.s(lambda s, mh=mh, eb=eb, slg=slg: s.activation(
                        g_blk[:, slg * 512:(slg + 1) * 512],
                        xn[:, mh * T + eb * BLK: mh * T + (eb + 1) * BLK], AF.Silu),
                        waits=wg)
                tok_gdma[(eb, mh)] = P.dma_y(lambda y, eb=eb, mh=mh, slg=slg:
                    y.dma_start(g_hbm[:, mh, eb * BLK:(eb + 1) * BLK],
                                g_blk[:, slg * 512:(slg + 1) * 512]),
                    waits=[("S", tok_sil)])

        def after_b(gj):
            qtj, hj = divmod(gj, NF)
            if hj == NF - 1:
                epilogue(qtj)
                if qtj % 4 == 3:
                    emit_outproj(qtj // 4)

        for qt in range(NT):
            P.label(f"P2c.qt{qt}")
            for h in range(NF):
                gi = qt * NF + h
                if gi >= 2:
                    tw = stage_b1(gi - 2)
                    tok_mm = stage_scores(gi)
                    stage_b2(gi - 2, tw)
                    after_b(gi - 2)
                    stage_a(gi, tok_mm)
                else:
                    tok_mm = stage_scores(gi)
                    stage_a(gi, tok_mm)
        for gj in (NT * NF - 2, NT * NF - 1):
            tw = stage_b1(gj)
            stage_b2(gj, tw)
            after_b(gj)

        mm.release()
        mm.release()
        return tok_gdma[(NB - 1, NF - 1)]

    tok_p2 = None
    if BUILD_UPTO >= 2:
        tok_p2 = phase2()
        if "g" in dbg_t:
            for c in range(NF):
                P.dma_y(lambda y, c=c: y.dma_start(
                    dbg_t["g"][:, c, :], g_hbm[:, c, :]), waits=[("DY", tok_p2)])

    def phase3(tok_g_last):
        mm.mark()
        Qf = mm.alloc("Qf", [128, NF * T], BF16)
        Kf = mm.alloc("Kf", [128, NF * T], BF16)
        Vt = mm.alloc("Vt", [128, NT * 1024], BF16)

        mm.mark()
        wringA = mm.alloc("wringA3", [128, NF * 1024], BF16)
        wringB = mm.alloc("wringB3", [128, NF * 1024], BF16)

        def wload(dst, src, waits=()):
            return P.dma_g(lambda g: g.dma_start(
                dst[:, :].rearrange("p (c f) -> p c f", c=NF),
                src[:, :].rearrange("(c p) f -> p c f", p=128)), waits=waits)

        P.label("P3a.proj")
        w_alias = [("DY", tok_g_last)] if tok_g_last else []
        tok_wq = wload(wringA, dram["wqT"], waits=w_alias)
        tok_wk = wload(wringB, dram["wkT"], waits=w_alias)

        def proj_full_fm(dst, wring, tok_w, scale):
            last_mm = None
            tok_e = None
            si = 0
            for oc in range(NF):
                for nb in range(NB):
                    sl = si % 4
                    si += 1
                    ps = PSA[:, sl * 512:(sl + 1) * 512]
                    w0 = [("DG", tok_w), ("V", tok_ln0)] + BK.wr([sl])
                    for ci in range(NF):
                        last_mm = P.t(lambda t, ci=ci, oc=oc, nb=nb, ps=ps,
                                      wring=wring: t.matmul(
                            ps, wring[:, ci * 1024 + oc * 128: ci * 1024 + oc * 128 + 128],
                            xn[:, ci * T + nb * BLK: ci * T + (nb + 1) * BLK],
                            start=(ci == 0), stop=(ci == NF - 1)),
                            waits=w0 if ci == 0 else [])
                    tok_e = P.s(lambda s, oc=oc, nb=nb, ps=ps: s.activation(
                        dst[:, oc * T + nb * BLK: oc * T + (nb + 1) * BLK], ps,
                        AF.Identity, scale=scale),
                        waits=[("T", last_mm)])
                    BK.rd([sl], "S", tok_e)
            return last_mm, tok_e

        tok_qmm, tok_qev = proj_full_fm(Qf, wringA, tok_wq, SCALE)
        tok_kmm, tok_kev = proj_full_fm(Kf, wringB, tok_wk, 1.0)

        tok_wv = wload(wringA, dram["wvT"], waits=[("T", tok_qmm)])
        last_vmm = None
        tok_vev = None
        for tt in range(NT):
            sl = tt % 2
            ps = PSA[:, sl * 1024:(sl + 1) * 1024]
            w0 = [("DG", tok_wv), ("V", tok_ln0)] + BK.wr([2 * sl, 2 * sl + 1])
            for ci in range(NF):
                for nb2 in range(2):
                    last_vmm = P.t(lambda t, ci=ci, tt=tt, nb2=nb2, ps=ps: t.matmul(
                        ps[:, nb2 * 512:(nb2 + 1) * 512],
                        xn[:, ci * T + tt * 128: ci * T + tt * 128 + 128],
                        wringA[:, ci * 1024 + nb2 * 512: ci * 1024 + (nb2 + 1) * 512],
                        start=(ci == 0), stop=(ci == NF - 1)),
                        waits=w0 if (ci == 0 and nb2 == 0) else [])
            tok_vev = P.v(lambda v, tt=tt, ps=ps: v.tensor_copy(
                Vt[:, tt * 1024:(tt + 1) * 1024], ps),
                waits=[("T", last_vmm)])
            BK.rd([2 * sl, 2 * sl + 1], "V", tok_vev)
        tok_vtmm = last_vmm
        mm.release()

        mm.mark()
        P_sb3 = mm.alloc("P_sb3", [128, T], BF16)
        PT_blk = mm.alloc("PT_blk", [128, NT * 512], BF16)
        x_blk3 = mm.alloc("x_blk3", [128, NF * 512], F32)
        g_rblk = mm.alloc("g_rblk", [128, NF * 512], BF16)
        x1sl = mm.alloc("x1sl", [128, 2 * 512], F32)
        rec1 = mm.alloc("rec1", [128, 1], F32)

        prev_exp = P.counts.get("S", 0)
        prev_recip = None
        prev_norm = None
        prev_lasttr = None
        prev_av = {}
        tok_x1dma = {}
        for b in range(NB):
            P.label(f"P3b.blk{b}")
            if b == 0:
                w_x = [("T", tok_vtmm)]
                w_g = [("T", tok_vtmm)]
            else:
                w_x = [("V", prev_x1_used)]
                w_g = [("V", prev_g_used)]
            tok_xb = P.dma_y(lambda y, b=b: y.dma_start(
                x_blk3[:, :].rearrange("p (c t) -> p c t", c=NF),
                xT[:, b * BLK:(b + 1) * BLK].rearrange("(c p) t -> p c t", p=128)),
                waits=w_x)
            tok_gb = P.dma_y(lambda y, b=b: y.dma_start(
                g_rblk[:, :].rearrange("p (c t) -> p c t", c=NF),
                g_hbm[:, :, b * BLK:(b + 1) * BLK]),
                waits=([("DY", tok_g_last)] if tok_g_last else []) + w_g)
            tok_z = None
            for kt in range(4 * b, 4 * b + 4):
                w_z = []
                if b > 0 and kt == 4 * b:
                    w_z = [("T", prev_av.get(b - 1))]
                tok_z = P.v(lambda v, kt=kt: v.memset(
                    PT_blk[:, kt * 512:(kt + 1) * 512], 0.0), waits=w_z)

            for j in range(4):
                qt = 4 * b + j
                nkt = qt + 1
                ncols = nkt * 128
                nseg = (ncols + 511) // 512
                for sg in range(nseg):
                    wdt = min(512, ncols - sg * 512)
                    w_sc = BK.wr([sg])
                    if qt == 0:
                        w_sc += [("S", tok_qev), ("S", tok_kev)]
                    for ci in range(NF):
                        last_sc = P.t(lambda t, ci=ci, qt=qt, sg=sg, wdt=wdt: t.matmul(
                            PSA[:, sg * 512: sg * 512 + wdt],
                            Qf[:, ci * T + qt * 128: ci * T + qt * 128 + 128],
                            Kf[:, ci * T + sg * 512: ci * T + sg * 512 + wdt],
                            start=(ci == 0), stop=(ci == NF - 1)),
                            waits=w_sc if ci == 0 else [])
                tok_mask = P.v(lambda v, ncols=ncols: v.tensor_add(
                    PSA[:, ncols - 128:ncols], PSA[:, ncols - 128:ncols], maskA[:, :]),
                    waits=[("T", last_sc)])
                tok_nm = P.v(lambda v, ncols=ncols: v.reduce_max(
                    negmax[:, :], PSA[:, 0:ncols], axis=AX.X, negate=True),
                    waits=([("S", prev_exp)] if prev_exp else []))
                w_exp = [("V", tok_nm)]
                if prev_norm:
                    w_exp += [("V", prev_norm)]
                if prev_lasttr:
                    w_exp += [("T", prev_lasttr)]
                tok_exp = P.s(lambda s, ncols=ncols: s.activation(
                    P_sb3[:, 0:ncols], PSA[:, 0:ncols], AF.Exp, bias=negmax[:, :],
                    accum_out=sums[:, :]), waits=w_exp)
                prev_exp = tok_exp
                BK.rd(list(range(nseg)), "S", tok_exp)
                prev_recip = P.v(lambda v: v.reciprocal(rec1[:, :], sums[:, :]),
                                 waits=[("S", tok_exp)])
                prev_norm = P.v(lambda v, ncols=ncols: v.tensor_scalar(
                    P_sb3[:, 0:ncols], P_sb3[:, 0:ncols], rec1[:, :], None,
                    op0=ALU.mult))
                nw = (nkt + 7) // 8
                for w in range(nw):
                    cnt = min(8, nkt - w * 8)
                    last_tr = None
                    for jj in range(cnt):
                        kt = w * 8 + jj
                        w_tr = (BK.wr([4 + (w % 2)]) + [("V", prev_norm)]) if jj == 0 else []
                        last_tr = P.t(lambda t, kt=kt, jj=jj, w=w: t.transpose(
                            PSBbf[:, (w % 2) * 1024 + jj * 128: (w % 2) * 1024 + (jj + 1) * 128],
                            P_sb3[:, kt * 128:(kt + 1) * 128], ident[:, :]),
                            waits=w_tr)
                    prev_lasttr = last_tr
                    tok_evw = P.v(lambda v, w=w, cnt=cnt, j=j: v.tensor_copy(
                        bass.AP(PT_blk, (w * 8) * 512 + j * 128,
                                [[NT * 512, 128], [512, cnt], [1, 128]]),
                        bass.AP(PSBbf, (w % 2) * 1024,
                                [[4096, 128], [128, cnt], [1, 128]])),
                        waits=[("T", last_tr)])
                    BK.rd([4 + (w % 2)], "V", tok_evw)

            if "pt0" in dbg_t and b == 0:
                P.dma_y(lambda y: y.dma_start(dbg_t["pt0"][:, 0, 0:2048],
                                              PT_blk[:, 0:2048]),
                        waits=[("V", tok_evw)])
                P.dma_y(lambda y: y.dma_start(dbg_t["pt0"][:, 1, 0:512],
                                              P_sb3[:, 0:512]),
                        waits=[("V", tok_evw)])

            P.label(f"P3c.blk{b}")
            nkt_b = 4 * b + 4
            tok_ps_done = {}
            for fc in range(NF):
                bank = fc
                psv = (PSA if fc < 4 else PSB)[:, (fc % 4) * 512:(fc % 4 + 1) * 512]
                w_av = BK.wr([bank]) + ([("V", tok_evw)] if fc == 0 else [])
                last_av = None
                for kt in range(nkt_b):
                    last_av = P.t(lambda t, kt=kt, fc=fc, psv=psv: t.matmul(
                        psv, Vt[:, kt * 1024 + fc * 128: kt * 1024 + (fc + 1) * 128],
                        PT_blk[:, kt * 512:(kt + 1) * 512],
                        start=(kt == 0), stop=(kt == nkt_b - 1)),
                        waits=w_av if kt == 0 else [])
                tok_ps_done[fc] = last_av
            prev_av[b] = tok_ps_done[NF - 1]
            tok_dump = None
            if "s0blk" in dbg_t and b == 0:
                for fc in range(NF):
                    psv = (PSA if fc < 4 else PSB)[:, (fc % 4) * 512:(fc % 4 + 1) * 512]
                    tkd = P.v(lambda v, fc=fc, psv=psv: v.tensor_copy(
                        x1sl[:, 0:512], psv),
                        waits=[("T", tok_ps_done[fc])] +
                              ([("DY", tok_dump)] if tok_dump else []))
                    tok_dump = P.dma_y(lambda y, fc=fc: y.dma_start(
                        dbg_t["s0blk"][:, fc, 0:512], x1sl[:, 0:512]),
                        waits=[("V", tkd)])
                    BK.rd([fc], "V", tkd)
            for fc in range(NF):
                psv = (PSA if fc < 4 else PSB)[:, (fc % 4) * 512:(fc % 4 + 1) * 512]
                ssl = x1sl[:, (fc % 2) * 512:(fc % 2 + 1) * 512]
                w_c = [("T", tok_ps_done[fc]), ("DY", tok_gb)]
                if tok_dump and fc < 2:
                    w_c += [("DY", tok_dump)]
                pk = (b, fc - 2) if fc >= 2 else (b - 1, fc + 6)
                if pk in tok_x1dma:
                    w_c += [("DY", tok_x1dma[pk])]
                tok_t1 = P.v(lambda v, fc=fc, psv=psv: v.tensor_mul(
                    scr[:, :], psv, g_rblk[:, fc * 512:(fc + 1) * 512]), waits=w_c)
                tok_t2 = P.v(lambda v, fc=fc, ssl=ssl: v.tensor_add(
                    ssl, scr[:, :], x_blk3[:, fc * 512:(fc + 1) * 512]),
                    waits=[("DY", tok_xb)])
                BK.rd([fc], "V", tok_t1)
                if "x1d" in dbg_t:
                    P.dma_y(lambda y, fc=fc, b=b, ssl=ssl: y.dma_start(
                        dbg_t["x1d"][:, fc, b * BLK:(b + 1) * BLK], ssl),
                        waits=[("V", tok_t2)])
                tok_x1dma[(b, fc)] = P.dma_y(lambda y, fc=fc, b=b, ssl=ssl: y.dma_start(
                    x1_hbm[:, fc, b * BLK:(b + 1) * BLK], ssl),
                    waits=[("V", tok_t2)])
            prev_x1_used = tok_t2
            prev_g_used = tok_t1
        mm.release()
        mm.release()
        return tok_x1dma[(NB - 1, NF - 1)]

    tok_p3 = None
    if BUILD_UPTO >= 3:
        tok_p3 = phase3(tok_p2)
        if "x1" in dbg_t:
            last = None
            for c in range(NF):
                last = P.dma_y(lambda y, c=c: y.dma_start(
                    dbg_t["x1"][:, c, :], x1_hbm[:, c, :]), waits=[("DY", tok_p3)])

    def phase4(tok_x1_last):
        mm.mark()
        xblk4 = [mm.alloc(f"xblk4_{i}", [128, NF * BLK], F32) for i in range(2)]
        xbf4 = mm.alloc("xbf4", [128, NF * BLK], BF16)
        xsq4 = mm.alloc("xsq4", [128, NF * BLK], BF16)
        last_store = [None]

        def ln1_write(b, c, tokB):
            t0 = b * BLK
            tk = P.v(lambda v: v.tensor_add(
                xn[:, c * T + t0: c * T + t0 + BLK], scr[:, :],
                PSB[:, 3 * BLK:4 * BLK]),
                waits=[("T", tokB)])
            last_store[0] = P.dma_y(lambda y, c=c, t0=t0: y.dma_start(
                xn1_hbm[:, c, t0:t0 + BLK], xn[:, c * T + t0: c * T + t0 + BLK]),
                waits=[("V", tk)])
            return tk

        P.label("P4.ln1")
        tok_ln1 = emit_ln("P4", "hbm_f32", ln1_write, x1_hbm, xblk4, xbf4, xsq4,
                          first_load_waits=[("DY", tok_x1_last)])
        mm.release()
        return tok_ln1, last_store[0]

    tok_p4 = tok_p4_store = None
    if BUILD_UPTO >= 4:
        tok_p4, tok_p4_store = phase4(tok_p3)
        if "xn1" in dbg_t:
            for c in range(NF):
                P.dma_y(lambda y, c=c: y.dma_start(
                    dbg_t["xn1"][:, c, :], xn[:, c * T:(c + 1) * T]),
                    waits=[("V", tok_p4)])

    def phase5(tok_ln1, tok_ln1_store):
        mm.mark()
        q1 = mm.alloc("q1", [128, NT * 1024], BF16)
        k1 = mm.alloc("k1", [128, NT * 1024], BF16)
        v1 = mm.alloc("v1", [128, NT * 1024], BF16)
        woring = mm.alloc("woring", [128, NF * 1024], BF16)
        st_q = mm.alloc("st_q", [128, 2 * 256], F32)
        st_k = mm.alloc("st_k", [128, 2 * 256], F32)
        rstd_q = mm.alloc("rstd_q", [128, 256], F32)
        rstd_k = mm.alloc("rstd_k", [128, 256], F32)
        qsq5 = mm.alloc("qsq5", [128, 1024], BF16)
        sc5 = mm.alloc("sc5", [128, 2 * 1024], F32)
        qf_c = mm.alloc("qf_c", [128, 1024], BF16)
        kf_c = mm.alloc("kf_c", [128, 1024], BF16)
        st_c = mm.alloc("st_c", [128, 2048], BF16)
        S_f32 = mm.alloc("S_f32", [128, NF * 64], F32)
        S_bf = mm.alloc("S_bf", [128, NF * 64], BF16)
        ofm = mm.alloc("ofm", [128, NF * BLK], BF16)
        s1stg = mm.alloc("s1stg", [128, 2 * 512], F32)

        P.label("P5.proj")
        w5 = [("DY", tok_ln1_store)]
        mm2 = mm

        def wload5(src, waits=()):
            return P.dma_g(lambda g: g.dma_start(
                woring[:, :].rearrange("p (c f) -> p c f", c=NF),
                src[:, :].rearrange("(c p) f -> p c f", p=128)), waits=waits)

        def proj_tm(dst, wsrc, wait_w):
            tok_w = wload5(wsrc, waits=wait_w)
            last_mm = None
            tok_ev = None
            for tt in range(NT):
                sl = tt % 2
                ps = PSA[:, sl * 1024:(sl + 1) * 1024]
                w0 = [("DG", tok_w), ("V", tok_ln1)] + BK.wr([2 * sl, 2 * sl + 1])
                for ci in range(NF):
                    for nb2 in range(2):
                        last_mm = P.t(lambda t, ci=ci, tt=tt, nb2=nb2, ps=ps: t.matmul(
                            ps[:, nb2 * 512:(nb2 + 1) * 512],
                            xn[:, ci * T + tt * 128: ci * T + tt * 128 + 128],
                            woring[:, ci * 1024 + nb2 * 512: ci * 1024 + (nb2 + 1) * 512],
                            start=(ci == 0), stop=(ci == NF - 1)),
                            waits=w0 if (ci == 0 and nb2 == 0) else [])
                tok_ev = P.v(lambda v, tt=tt, ps=ps: v.tensor_copy(
                    dst[:, tt * 1024:(tt + 1) * 1024], ps),
                    waits=[("T", last_mm)])
                BK.rd([2 * sl, 2 * sl + 1], "V", tok_ev)
            return last_mm, tok_ev

        tok_q1mm, tok_q1ev = proj_tm(q1, dram["w1qT"], w5)
        tok_k1mm, tok_k1ev = proj_tm(k1, dram["w1kT"], [("T", tok_q1mm)])
        tok_v1mm, tok_v1ev = proj_tm(v1, dram["w1vT"], [("T", tok_k1mm)])
        tok_wo = wload5(dram["w1oT"], waits=[("T", tok_v1mm)])

        P.label("P5.hln")

        def head_ln(src, st, rstd, gamma, beta, pre):
            tok_red = None
            for tt in range(NT):
                tile = src[:, tt * 1024:(tt + 1) * 1024]
                tok_red = P.v(lambda v, tile=tile, tt=tt: v.reduce_sum(
                    st[:, tt * 16:(tt + 1) * 16],
                    tile.rearrange("p (h d) -> p h d", d=64), axis=AX.X),
                    waits=[("V", pre)] if tt == 0 else [])
                tok_sq = P.s(lambda s, tile=tile: s.activation(
                    qsq5[:, :], tile, AF.Square),
                    waits=[("V", tok_red)])
                tok_red = P.v(lambda v, tt=tt: v.reduce_sum(
                    st[:, 256 + tt * 16: 256 + (tt + 1) * 16],
                    qsq5[:, :].rearrange("p (h d) -> p h d", d=64), axis=AX.X),
                    waits=[("S", tok_sq)])
            P.v(lambda v: v.tensor_scalar_mul(st[:, 0:256], st[:, 0:256], 1.0 / 64))
            P.v(lambda v: v.tensor_scalar(rstd[:, :], st[:, 256:512], 1.0 / 64, EPS,
                                          op0=ALU.mult, op1=ALU.add))
            P.v(lambda v: v.scalar_tensor_tensor(sc5[:, 0:256], st[:, 0:256], 1.0,
                                                 st[:, 0:256], op0=ALU.mult,
                                                 op1=ALU.mult))
            tk = P.v(lambda v: v.tensor_sub(rstd[:, :], rstd[:, :], sc5[:, 0:256]))
            tk2 = P.s(lambda s: s.activation(rstd[:, :], rstd[:, :], AF.Sqrt),
                      waits=[("V", tk)])
            tok_fin = P.v(lambda v: v.reciprocal(rstd[:, :], rstd[:, :]),
                          waits=[("S", tk2)])
            tok_ap = None
            for tt in range(NT):
                tile = src[:, tt * 1024:(tt + 1) * 1024]
                muAP = bass.AP(st, tt * 16, [[512, 128], [1, 16], [0, 64]])
                aAP = bass.AP(rstd, tt * 16, [[256, 128], [1, 16], [0, 64]])
                gAP = bass.AP(gamma, 0, [[64, 128], [0, 16], [1, 64]])
                bAP = bass.AP(beta, 0, [[64, 128], [0, 16], [1, 64]])
                t3d = sc5[:, 0:1024].rearrange("p (h d) -> p h d", d=64)
                t3d2 = sc5[:, 1024:2048].rearrange("p (h d) -> p h d", d=64)
                P.v(lambda v, tile=tile, muAP=muAP, t3d=t3d: v.tensor_tensor(
                    t3d, tile.rearrange("p (h d) -> p h d", d=64), muAP,
                    op=ALU.subtract),
                    waits=[("V", tok_fin)] if tt == 0 else [])
                P.v(lambda v, aAP=aAP, t3d=t3d, t3d2=t3d2: v.tensor_tensor(
                    t3d2, t3d, aAP, op=ALU.mult))
                P.v(lambda v, gAP=gAP, t3d=t3d, t3d2=t3d2: v.tensor_tensor(
                    t3d, t3d2, gAP, op=ALU.mult))
                tok_ap = P.v(lambda v, tile=tile, bAP=bAP, t3d=t3d: v.tensor_tensor(
                    tile.rearrange("p (h d) -> p h d", d=64), t3d, bAP, op=ALU.add))
            return tok_ap

        import os
        p5lim = os.environ.get("P5_LIMIT", "full")
        if p5lim == "proj":
            mm.release()
            return tok_v1ev and None or tok_v1ev

        tok_q1n = head_ln(q1, st_q, rstd_q, sv["qg"], sv["qbe"], tok_q1ev)
        tok_k1n = head_ln(k1, st_k, rstd_k, sv["kg"], sv["kbe"], tok_k1ev)
        if p5lim == "hln":
            mm.release()
            return None

        P.v(lambda v: v.memset(S_f32[:, :], 0.0))
        tok_s0 = P.v(lambda v: v.memset(S_bf[:, :], 0.0))

        P.label("P5.attn")
        tok_s1dma = None
        prev_ofm_ev = None
        prev_o_mm = {}
        prev_sb_all = None
        s1dma_slot = {}
        tok_wo_used = None
        ncchunks = {"attn1": 1, "attn4": 4}.get(p5lim, NT)
        for c in range(ncchunks):
            toks_f = {}
            for (nm, src, dstb) in [("q", q1, qf_c), ("k", k1, kf_c)]:
                last_tr = None
                bank = 4 if nm == "q" else 5
                for hp in range(NF):
                    w_tr = []
                    if hp == 0:
                        w_tr = BK.wr([bank]) + [("V", tok_k1n), ("V", tok_s0)]
                        if (c - 1) in prev_o_mm:
                            w_tr += [("T", prev_o_mm[c - 1])]
                    last_tr = P.t(lambda t, c=c, hp=hp, src=src, bank=bank: t.transpose(
                        PSBbf[:, (bank - 4) * 1024 + hp * 128: (bank - 4) * 1024 + (hp + 1) * 128],
                        src[:, c * 1024 + hp * 128: c * 1024 + (hp + 1) * 128],
                        ident[:, :]),
                        waits=w_tr)
                toks_f[nm] = P.v(lambda v, dstb=dstb, bank=bank: v.tensor_copy(
                    dstb[:, :],
                    PSBbf[:, (bank - 4) * 1024:(bank - 4) * 1024 + 1024]),
                    waits=[("T", last_tr)])
                BK.rd([bank], "V", toks_f[nm])

            if p5lim == "tr":
                prev_o_mm[c] = P.counts["T"]
                continue
            w_st0 = BK.wr([0, 1, 2, 3]) + [("V", toks_f["q"]), ("V", toks_f["k"])]
            for hp in range(NF):
                kA = kf_c[0:64, hp * 128:(hp + 1) * 128]
                qA = qf_c[0:64, hp * 128:(hp + 1) * 128]
                P.t(lambda t, hp=hp, kA=kA, qA=qA: t.matmul(
                    PSA[:, hp * 256: hp * 256 + 128], kA, qA,
                    start=True, stop=True), waits=w_st0 if hp == 0 else [])
            last_stB = None
            for hp in range(NF):
                kB = kf_c[64:128, hp * 128:(hp + 1) * 128]
                qB = qf_c[64:128, hp * 128:(hp + 1) * 128]
                last_stB = P.t(lambda t, hp=hp, kB=kB, qB=qB: t.matmul(
                    PSA[:, hp * 256 + 128: hp * 256 + 256], kB, qB,
                    start=True, stop=True, tile_position=(64, 0)))
            mAP = bass.AP(maskMT, 0, [[128, 128], [0, 8], [1, 128]])
            tok_msk = {}
            for half in range(2):
                w_msk = [("T", last_stB)] if half == 0 else []
                tok_msk[half] = P.v(lambda v, mAP=mAP, half=half: v.tensor_tensor(
                    st_c[:, half * 1024:(half + 1) * 1024].rearrange(
                        "p (a t) -> p a t", t=128),
                    PSA[:, half * 1024:(half + 1) * 1024].rearrange(
                        "p (a t) -> p a t", t=128),
                    mAP, op=ALU.mult), waits=w_msk)
                BK.rd([half * 2, half * 2 + 1], "V", tok_msk[half])
            for hp in range(NF):
                qA = qf_c[0:64, hp * 128:(hp + 1) * 128]
                qB = qf_c[64:128, hp * 128:(hp + 1) * 128]
                stc = st_c[:, hp * 256:(hp + 1) * 256]
                pso = PSB[:, 1024 + hp * 128: 1024 + (hp + 1) * 128]
                w_o = BK.wr([6 + hp // 4]) if hp in (0, 4) else []
                vA = v1[:, c * 1024 + hp * 128: c * 1024 + hp * 128 + 64]
                vB = v1[:, c * 1024 + hp * 128 + 64: c * 1024 + (hp + 1) * 128]
                tok_o1 = P.t(lambda t, pso=pso, vA=vA, stc=stc: t.matmul(
                    pso[0:64, :], vA, stc[:, 0:128], start=True, stop=False),
                    waits=w_o + ([("V", tok_msk[hp // 4])] if hp in (0, 4) else []))
                tok_o2 = P.t(lambda t, pso=pso, vB=vB, stc=stc: t.matmul(
                    pso[64:128, :], vB, stc[:, 128:256], start=True, stop=False,
                    tile_position=(0, 64)))
                sA = S_bf[0:64, hp * 64:(hp + 1) * 64]
                sB = S_bf[64:128, hp * 64:(hp + 1) * 64]
                w_o3 = [("V", prev_sb_all)] if (prev_sb_all and hp == 0) else []
                tok_o3 = P.t(lambda t, pso=pso, sA=sA, qA=qA: t.matmul(
                    pso[0:64, :], sA, qA, start=False, stop=True), waits=w_o3)
                tok_o4 = P.t(lambda t, pso=pso, sB=sB, qB=qB: t.matmul(
                    pso[64:128, :], sB, qB, start=False, stop=True,
                    tile_position=(64, 64)))
                prev_o_mm[c] = tok_o4
            w_su0 = BK.wr([5])
            last_su = None
            for hp in range(NF):
                vA = v1[:, c * 1024 + hp * 128: c * 1024 + hp * 128 + 64]
                vB = v1[:, c * 1024 + hp * 128 + 64: c * 1024 + (hp + 1) * 128]
                kAt = k1[:, c * 1024 + hp * 128: c * 1024 + hp * 128 + 64]
                kBt = k1[:, c * 1024 + hp * 128 + 64: c * 1024 + (hp + 1) * 128]
                pssu = PSB[:, 512 + hp * 64: 512 + (hp + 1) * 64]
                P.t(lambda t, pssu=pssu, kAt=kAt, vA=vA: t.matmul(
                    pssu[0:64, :], kAt, vA, start=True, stop=True),
                    waits=w_su0 if hp == 0 else [])
                last_su = P.t(lambda t, pssu=pssu, kBt=kBt, vB=vB: t.matmul(
                    pssu[64:128, :], kBt, vB, start=True, stop=True,
                    tile_position=(0, 64)))
            tok_sa = P.v(lambda v: v.tensor_add(
                S_f32[:, :], S_f32[:, :], PSB[:, 512:1024]),
                waits=[("T", last_su), ("T", prev_o_mm[c])])
            prev_sb_all = P.v(lambda v: v.tensor_copy(S_bf[:, :], S_f32[:, :]))
            BK.rd([5], "V", tok_sa)
            w_ev = [("T", prev_o_mm[c])]
            if c % 4 == 0 and tok_wo_used is not None:
                w_ev += [("T", tok_wo_used)]
            tok_oev = P.v(lambda v, c=c: v.tensor_copy(
                bass.AP(ofm, (c % 4) * 128, [[NF * BLK, 128], [BLK, NF], [1, 128]]),
                bass.AP(PSB, 1024, [[2048, 128], [128, NF], [1, 128]])),
                waits=w_ev)
            BK.rd([6, 7], "V", tok_oev)

            if c % 4 == 3:
                bb = c // 4
                P.label(f"P5.wo{bb}")
                for oc in range(NF):
                    sl = oc % 2
                    ps = PSB[:, sl * 512:(sl + 1) * 512]
                    w0 = [("V", tok_oev), ("DG", tok_wo)] + BK.wr([4 + sl])
                    last_mm = None
                    for ci in range(NF):
                        last_mm = P.t(lambda t, ci=ci, oc=oc, ps=ps: t.matmul(
                            ps, woring[:, ci * 1024 + oc * 128: ci * 1024 + oc * 128 + 128],
                            ofm[:, ci * BLK:(ci + 1) * BLK],
                            start=(ci == 0), stop=(ci == NF - 1)),
                            waits=w0 if ci == 0 else [])
                    tok_wo_used = last_mm
                    w_se = [("T", last_mm)]
                    if s1dma_slot.get(oc % 2):
                        w_se += [("DY", s1dma_slot[oc % 2])]
                    tok_sev = P.v(lambda v, oc=oc, ps=ps: v.tensor_copy(
                        s1stg[:, (oc % 2) * 512:(oc % 2 + 1) * 512], ps),
                        waits=w_se)
                    BK.rd([4 + sl], "V", tok_sev)
                    tok_s1dma = P.dma_y(lambda y, oc=oc, bb=bb: y.dma_start(
                        s1_hbm[:, oc, bb * BLK:(bb + 1) * BLK],
                        s1stg[:, (oc % 2) * 512:(oc % 2 + 1) * 512]),
                        waits=[("V", tok_sev)])
                    s1dma_slot[oc % 2] = tok_s1dma
        mm.release()
        return tok_s1dma

    tok_p5 = None
    import os as _os
    if _os.environ.get("P5_LIMIT", "full") != "full" and BUILD_UPTO >= 5:
        phase5(tok_p4, tok_p4_store)
        P.label("final")
        P.y(lambda y: y.nop(), waits=[("DY", P.counts.get("DY", 0)),
                                      ("DG", P.counts.get("DG", 0))])
        with contextlib.ExitStack() as st:
            sems = {}
            for name in ["DG", "DY", "T", "V", "S", "G"]:
                sems[name] = st.enter_context(nc.semaphore(name))
            block = st.enter_context(nc.Block())
            P.replay_into(nc, block, sems)
        return nc
    if BUILD_UPTO >= 5:
        tok_p5 = phase5(tok_p4, tok_p4_store)
        if "s1" in dbg_t:
            for c in range(NF):
                P.dma_y(lambda y, c=c: y.dma_start(
                    dbg_t["s1"][:, c, :], s1_hbm[:, c, :]), waits=[("DY", tok_p5)])

    def phase6(tok_s1_last):
        mm.release()
        mm.mark()
        W1t = mm.alloc("W1t", [128, NF * FF], BF16)
        W2t = mm.alloc("W2t", [128, 32 * 1024], BF16)
        mm.mark()
        stg = [mm.alloc(f"tstg{i}", [128, 1024], F32) for i in range(2)]
        tsc = [mm.alloc(f"tsc{i}", [128, 1024], F32) for i in range(2)]

        P.label("P6.tern")
        tern_done = {}
        tok_ld = {}
        pieces = []
        for fc in range(NF):
            for j in range(4):
                pieces.append(("w1", fc, j))
        for cc in range(32):
            pieces.append(("w2", cc, 0))
        prev_tern = {}
        for idx, (mat, a, jj) in enumerate(pieces):
            sl = idx % 2
            if mat == "w1":
                src = dram["w1T"][a * 128:(a + 1) * 128, jj * 1024:(jj + 1) * 1024]
                dst = W1t[:, a * FF + jj * 1024: a * FF + (jj + 1) * 1024]
                rs = w_rs["w1"]
            else:
                src = dram["w2T"][a * 128:(a + 1) * 128, :]
                dst = W2t[:, a * 1024:(a + 1) * 1024]
                rs = w_rs["w2"]
            w_ld = [("DY", tok_s1_last)] if idx < 2 else [("S", prev_tern[idx - 2])]
            tok_l = P.dma_y(lambda y, sl=sl, src=src: y.dma_start(stg[sl][:, :], src),
                            waits=w_ld)
            tok_t1 = P.s(lambda s, sl=sl, rs=rs: s.activation(
                tsc[sl][:, :], stg[sl][:, :], AF.Identity,
                bias=cmagic[:, :], scale=rs[:, :]),
                waits=[("DY", tok_l)] + ([("V", tok_qb32)] if idx == 0 else []) +
                      ([("V", tern_done[idx - 2])] if (idx - 2) in tern_done else []))
            prev_tern[idx] = tok_t1
            P.v(lambda v, sl=sl: v.tensor_scalar(
                tsc[sl][:, :], tsc[sl][:, :], MAGIC, -1.0,
                op0=ALU.subtract, op1=ALU.max),
                waits=[("S", tok_t1)])
            tern_done[idx] = P.v(lambda v, sl=sl, dst=dst: v.tensor_scalar_min(
                dst, tsc[sl][:, :], 1.0))
        tok_tern = tern_done[len(pieces) - 1]
        mm.release()
        h_blk = mm.alloc("h_blk", [128, 32 * 256], BF16)
        xn1b = mm.alloc("xn1b", [128, NF * 256], BF16)
        x1b = mm.alloc("x1b", [128, NF * 256], F32)
        s1b = mm.alloc("s1b", [128, NF * 256], F32)
        gm = mm.alloc("gm", [128, 2 * 256], F32)
        yb = mm.alloc("yb", [128, 2 * 256], F32)

        NBB = 8
        TB = 256
        tok_ydma = {}
        prev_h_mm = None
        prev_g_mm = None
        tok_xnb = tok_x1b = tok_s1b = None
        for bb in range(NBB):
            P.label(f"P6.blk{bb}")
            t0 = bb * TB
            w_ld = [("V", tok_tern)]
            if bb > 0:
                w_ld = [("T", prev_h_mm)]
            tok_xnb = P.dma_y(lambda y, t0=t0: y.dma_start(
                xn1b[:, :].rearrange("p (c t) -> p c t", c=NF),
                xn1_hbm[:, :, t0:t0 + TB]), waits=w_ld)
            w_l2 = [("V", tok_tern)] if bb == 0 else [("V", prev_comb)]
            tok_x1b = P.dma_y(lambda y, t0=t0: y.dma_start(
                x1b[:, :].rearrange("p (c t) -> p c t", c=NF),
                x1_hbm[:, :, t0:t0 + TB]), waits=w_l2)
            tok_s1b = P.dma_y(lambda y, t0=t0: y.dma_start(
                s1b[:, :].rearrange("p (c t) -> p c t", c=NF),
                s1_hbm[:, :, t0:t0 + TB]),
                waits=[("DY", tok_s1_last), ("V", tok_tern)] if bb == 0
                else [("V", prev_comb)])

            prev_hev = {}
            for cc in range(32):
                bank = 4 + cc % 2
                ps = PSB[:, (cc % 2) * 512:(cc % 2) * 512 + TB]
                w0 = [("DY", tok_xnb), ("V", tok_tern)] + BK.wr([bank])
                last_mm = None
                for ci in range(NF):
                    last_mm = P.t(lambda t, ci=ci, cc=cc, ps=ps: t.matmul(
                        ps, W1t[:, ci * FF + cc * 128: ci * FF + cc * 128 + 128],
                        xn1b[:, ci * TB:(ci + 1) * TB],
                        start=(ci == 0), stop=(ci == NF - 1)),
                        waits=w0 if ci == 0 else [])
                prev_h_mm = last_mm
                tok_h = P.s(lambda s, cc=cc, ps=ps: s.activation(
                    h_blk[:, cc * TB:(cc + 1) * TB], ps, AF.Gelu,
                    bias=sv["b1"][:, cc:cc + 1]),
                    waits=[("T", last_mm)])
                BK.rd([bank], "S", tok_h)
                prev_hev[cc] = tok_h
            for mh in range(NF):
                bank = mh % 4
                ps = PSA[:, bank * 512: bank * 512 + TB]
                w0 = [("S", prev_hev[31])] + BK.wr([bank])
                last_mm = None
                for cc in range(32):
                    last_mm = P.t(lambda t, cc=cc, mh=mh, ps=ps: t.matmul(
                        ps, W2t[:, cc * 1024 + mh * 128: cc * 1024 + mh * 128 + 128],
                        h_blk[:, cc * TB:(cc + 1) * TB],
                        start=(cc == 0), stop=(cc == 31)),
                        waits=w0 if cc == 0 else [])
                prev_g_mm = last_mm
                tok_g = P.s(lambda s, mh=mh, ps=ps: s.activation(
                    gm[:, (mh % 2) * TB:(mh % 2 + 1) * TB], ps, AF.Silu,
                    bias=sv["b2"][:, mh:mh + 1]),
                    waits=[("T", last_mm)] +
                          ([("V", tok_yw2)] if mh >= 2 else []))
                tok_yw1 = P.v(lambda v, mh=mh: v.tensor_mul(
                    scr[:, 0:TB], gm[:, (mh % 2) * TB:(mh % 2 + 1) * TB],
                    s1b[:, mh * TB:(mh + 1) * TB]),
                    waits=[("S", tok_g), ("DY", tok_s1b)])
                BK.rd([bank], "V", tok_yw1)
                w_y2 = [("DY", tok_x1b)]
                yk = (bb, mh - 2) if mh >= 2 else (bb - 1, mh + 6)
                if yk in tok_ydma:
                    w_y2 += [("DY", tok_ydma[yk])]
                tok_yw2 = P.v(lambda v, mh=mh: v.tensor_add(
                    yb[:, (mh % 2) * TB:(mh % 2 + 1) * TB], scr[:, 0:TB],
                    x1b[:, mh * TB:(mh + 1) * TB]),
                    waits=w_y2)
                tok_ydma[(bb, mh)] = P.dma_y(lambda y, mh=mh, t0=t0: y.dma_start(
                    yOut[:, mh, t0:t0 + TB],
                    yb[:, (mh % 2) * TB:(mh % 2 + 1) * TB]),
                    waits=[("V", tok_yw2)])
            prev_comb = tok_yw2
        mm.release()
        return tok_ydma[(NBB - 1, NF - 1)]

    if BUILD_UPTO >= 6:
        phase6(tok_p5)

    P.label("final")
    P.y(lambda y: y.nop(), waits=[("DY", P.counts.get("DY", 0)),
                                  ("DG", P.counts.get("DG", 0))])

    with contextlib.ExitStack() as st:
        sems = {}
        for name in ["DG", "DY", "T", "V", "S", "G"]:
            sems[name] = st.enter_context(nc.semaphore(name))
        block = st.enter_context(nc.Block())
        P.replay_into(nc, block, sems)
    return nc


def build_p2(env):
    pass


def build_p3(env):
    pass


def build_p4(env):
    pass


def build_p5(env):
    pass


def build_p6(env):
    pass


_cached = {}


def kernel(**inputs):
    if "nc" not in _cached:
        _cached["nc"] = build_program()
    nc = _cached["nc"]

    x = np.asarray(inputs["L0_attn_x"], np.float32)

    def tr(a):
        return np.ascontiguousarray(np.asarray(a, np.float32).T)

    def cols(vec, k):
        v = np.ascontiguousarray(np.asarray(vec, np.float32))
        return np.ascontiguousarray(v.reshape(k, 128).T)

    def bc(vec):
        return np.ascontiguousarray(np.tile(
            np.asarray(vec, np.float32)[None, :], (128, 1)))

    shared = {
        "wqT": tr(inputs["L0_attn_WQ"]), "wkT": tr(inputs["L0_attn_WK"]),
        "wvT": tr(inputs["L0_attn_WV"]),
        "qwT": tr(inputs["L0_ffn_q_w"]), "kwT": tr(inputs["L0_ffn_k_w"]),
        "vwT": tr(inputs["L0_ffn_v_w"]), "owT": tr(inputs["L0_ffn_o_w"]),
        "nw_c": cols(inputs["L0_ffn_norm_w"], NF),
        "nb_c": cols(inputs["L0_ffn_norm_b"], NF),
        "qb_c": cols(inputs["L0_ffn_q_b"], NF),
        "kb_c": cols(inputs["L0_ffn_k_b"], NF),
        "ob_c": cols(inputs["L0_ffn_o_b"], NF),
        "vb_bc": bc(inputs["L0_ffn_v_b"]),
        "w1qT": tr(inputs["L1_attn_Wq"]), "w1kT": tr(inputs["L1_attn_Wk"]),
        "w1vT": tr(inputs["L1_attn_Wv"]), "w1oT": tr(inputs["L1_attn_Wo"]),
        "qg_bc": bc(inputs["L1_attn_q_gamma"]),
        "qbeta_bc": bc(inputs["L1_attn_q_beta"]),
        "kg_bc": bc(inputs["L1_attn_k_gamma"]),
        "kbeta_bc": bc(inputs["L1_attn_k_beta"]),
        "w1T": tr(inputs["L1_ffn_W1"]), "w2T": tr(inputs["L1_ffn_W2"]),
        "b1_c": cols(inputs["L1_ffn_b1"], 32),
        "b2_c": cols(inputs["L1_ffn_b2"], NF),
    }

    in_maps = []
    for c in range(8):
        m = dict(shared)
        m["xT"] = np.ascontiguousarray(x[c].T)
        in_maps.append(m)

    res = run_bass_kernel_spmd(nc, in_maps, core_ids=list(range(8)))
    _cached["last_res"] = res
    outs = []
    for c in range(8):
        y = np.asarray(res.results[c]["y"], np.float32)
        outs.append(y.transpose(2, 1, 0).reshape(T, D))
    return np.stack(outs, axis=0)
